# version 18
# baseline (speedup 1.0000x reference)
import numpy as np
from contextlib import ExitStack
import concourse.bass as bass
import concourse.mybir as mybir
from concourse.bass_utils import run_bass_kernel_spmd

F32 = mybir.dt.float32
F16 = mybir.dt.float16
FP8 = mybir.dt.float8e4
AF = mybir.ActivationFunctionType
ALU = mybir.AluOpType

T, D, H, DH, RK = 2048, 1024, 16, 64, 256
CH = 512
NCH = T // CH
NB = D // 128
NT = T // 128
PAIRS = H // 2
NXCH, NXS, NX2, NOZ, NH2, NRP, NXC = 1, 2, 2, 1, 1, 2, 1


class Sched:

    ENGS = ("T", "V", "S", "SY", "GP")
    ONE_WAVE = ("cst", "cstg")

    def __init__(self):
        self.steps = {e: [] for e in self.ENGS}
        self.count = {}
        self.writer = {}
        self.readers = {}
        self.watermark = {e: {} for e in self.ENGS}
        self.sem_names = []

    def _bump(self, sem, inc):
        if sem not in self.count:
            self.count[sem] = 0
            self.sem_names.append(sem)
        self.count[sem] += inc
        return self.count[sem]

    def emit(self, eng, fn, reads=(), writes=(), sem=None, inc=None):
        if eng in ("SY", "GP") and inc != 1:
            assert sem is not None, "DMA steps need a sem stream name"
            inc = 16
        else:
            sem = sem or eng
            inc = 1
        waits = {}
        def add(w):
            if w is None:
                return
            ws, wv = w
            if ws == "T" and eng == "T":
                return
            waits[ws] = max(waits.get(ws, 0), wv)
        for t in reads:
            add(self.writer.get(t))
        for t in writes:
            add(self.writer.get(t))
            for r in self.readers.get(t, ()):
                add(r)
        myval = self._bump(sem, inc)
        final_waits = []
        for ws, wv in sorted(waits.items()):
            if ws == sem and eng not in ("SY", "GP"):
                pass
            if self.watermark[eng].get(ws, 0) < wv:
                final_waits.append((ws, wv))
                self.watermark[eng][ws] = wv
        self.steps[eng].append((final_waits, fn, sem, inc))
        for t in writes:
            self.writer[t] = (sem, myval)
            self.readers[t] = []
        for t in reads:
            self.readers.setdefault(t, []).append((sem, myval))

    def replay(self, eng_name, eng_obj, sems):
        for (waits, fn, sem, inc) in self.steps[eng_name]:
            for (ws, wv) in waits:
                if ws in self.ONE_WAVE:
                    wv = self.count[ws]
                eng_obj.wait_ge(sems[ws], wv)
            last = fn(eng_obj)
            assert last is not None, "step fn must return its last instruction"
            last.then_inc(sems[sem], inc)


def build_nc():
    nc = bass.Bass("TRN2", target_bir_lowering=False)
    dt = nc.dram_tensor
    xT = dt("xT", [D, T], F32, kind="ExternalInput").ap()
    xtok_d = dt("xtok_d", [T, D], F32, kind="ExternalInput").ap()
    ident_in = dt("ident_in", [128, 128], F32, kind="ExternalInput").ap()
    wq = dt("wq", [D, D], F32, kind="ExternalInput").ap()
    wd = dt("wd", [D, RK], F32, kind="ExternalInput").ap()
    wu = dt("wu", [RK, 2 * D], F32, kind="ExternalInput").ap()
    wo = dt("wo", [D, D], F32, kind="ExternalInput").ap()
    wf = dt("wf", [D, D], F32, kind="ExternalInput").ap()
    bq_r = dt("bq_r", [1, D], F32, kind="ExternalInput").ap()
    bd_c = dt("bd_c", [128, 2], F32, kind="ExternalInput").ap()
    bu_r = dt("bu_r", [1, 2 * D], F32, kind="ExternalInput").ap()
    bo_c = dt("bo_c", [128, NB], F32, kind="ExternalInput").ap()
    bf_c = dt("bf_c", [128, NB], F32, kind="ExternalInput").ap()
    ones_in = dt("ones_in", [128, 128], F32, kind="ExternalInput").ap()
    onesr_in = dt("onesr_in", [1, CH], F32, kind="ExternalInput").ap()
    cvals_in = dt("cvals_in", [1, 5], F32, kind="ExternalInput").ap()
    scr = dt("scr", [2, 2 * T], F32, kind="Internal").ap()
    zscr = dt("zscr", [H, T], F16, kind="Internal").ap()
    buscr = dt("buscr", [1, 2 * D], F16, kind="Internal").ap()
    yT = dt("yT", [D, T], F32, kind="ExternalOutput").ap()

    es = ExitStack()
    sb = lambda name, shape, d: es.enter_context(nc.sbuf_tensor(name, shape, d))

    wstage = [sb(f"wstage{i}", [128, D], F32) for i in range(2)]
    wq_h = [sb(f"wq_h{k}", [128, D], F16) for k in range(NB)]
    wd_h = [sb(f"wd_h{k}", [128, RK], F16) for k in range(NB)]
    wu_h = [sb(f"wu_h{k}", [128, 2 * D], F16) for k in range(2)]
    wo_h = wq_h
    wf8 = [sb(f"wf8_{k}", [128, D], FP8) for k in range(NB)]
    rwo = sb("rwo", [128, NB], F32)
    rwo_h = sb("rwo_h", [128, NB], F16)
    bq_rf = sb("bq_rf", [1, D], F32)
    bq_rh = sb("bq_rh", [1, D], F16)
    bu_rf = sb("bu_rf", [1, 2 * D], F32)
    bu_rh = sb("bu_rh", [1, 2 * D], F16)
    bd_cf = sb("bd_cf", [128, 2], F32)
    bo_cf = sb("bo_cf", [128, NB], F32)
    bf_cf = sb("bf_cf", [128, NB], F32)
    cvals = sb("cvals", [1, 5], F32)
    ones_f = sb("ones_f", [128, 128], F32)
    ones_h = sb("ones_h", [128, 128], F16)
    onesr_f = sb("onesr_f", [1, CH], F32)
    onesr_h = sb("onesr_h", [1, CH], F16)
    burep = [sb(f"burep{i}", [128, CH], F16) for i in range(2)]

    xtok = [sb(f"xtok{i}", [128, D], F32) for i in range(2)]
    scol = [sb(f"scol{i}", [128, 8], F32) for i in range(2)]
    pst8 = sb("pst8", [8, 128], F32)
    msq4t = sb("msq4t", [128, 4], F32)
    ident = sb("ident", [128, 128], F32)
    m1rep = [sb(f"m1rep{i}", [128, CH], F32) for i in range(NRP)]
    i1rep = [sb(f"i1rep{i}", [128, CH], F32) for i in range(NRP)]
    m2rep = m1rep
    i2rep = i1rep
    xc1 = [sb(f"xc1_{i}", [128, CH], F32) for i in range(NXC)]
    xs1 = [[sb(f"xs1_{i}_{b}", [128, CH], F16) for b in range(NB)] for i in range(NXS)]
    qp = [sb(f"qp{b}", [128, T], F16) for b in range(NB)]
    ebuf = [sb(f"ebuf{i}", [128, CH], F16) for i in range(2)]
    e1buf = ebuf
    kv1 = [[sb(f"kv1_{i}_{r}", [128, CH], F16) for r in range(2)] for i in range(1)]
    kvsb = [sb(f"kvsb{i}", [128, 2 * 1040], F16) for i in range(2)]
    ctxsb = sb("ctxsb", [128, PAIRS * 65], F16)
    bo16 = sb("bo16", [128, NB], F16)
    sbo = sb("sbo", [1, 1], F32)
    sbo_h = sb("sbo_h", [1, 1], F16)
    zlhs = sb("zlhs", [128, NB * 16], F16)
    zdsb = sb("zdsb", [16, CH], F32)
    z16 = sb("z16", [16, CH], F16)
    oev = [sb(f"oev{i}", [128, CH], F32) for i in range(2)]
    rows2 = sb("rows2", [1, 2 * CH], F32)
    m2row = rows2[0:1, 0:CH]
    i2row = rows2[0:1, CH:2 * CH]
    m1c = sb("m1c", [1, CH], F32)
    trow = [sb(f"trow{i}", [1, CH], F32) for i in range(2)]
    trow = trow + trow
    zrep = [sb(f"zrep{i}", [128, CH], F16) for i in range(8)]
    outz = [[sb(f"outz{i}_{b}", [128, CH], F16) for b in range(NB)] for i in range(NOZ)]
    xr = [sb(f"xr{i}", [128, CH], F32) for i in range(2)]
    x2 = [[sb(f"x2_0_{b}", [128, CH], F32) for b in range(NB)]]
    x2v = []
    for nm, hostt in (("x2v0", wstage[0]), ("x2v1", wstage[1]),
                      ("x2v2", xtok[0]), ("x2v3", xtok[1])):
        vt = nc.virtual_sbuf_tensor(nm, [128, 2 * CH], F32)
        nc.back_virtual_tensor_with(vt, hostt)
        x2v.append(vt)
    x2.append([x2v[b // 2][:, (b % 2) * CH:(b % 2 + 1) * CH] for b in range(NB)])
    xsq2 = [sb(f"xsq2_{i}", [128, CH], F16) for i in range(2)]
    xc2 = xc1
    h2q = [[sb(f"h2q{i}_{b}", [128, CH], FP8) for b in range(NB)] for i in range(NH2)]

    psum = lambda name, shape: es.enter_context(nc.psum_tensor(name, shape, F32))
    pmain = [psum("pm0", [128, CH]), psum("pm1", [128, CH])]
    pctx = [psum("pc0", [64, 7 * 65]), psum("pc1", [64, 7 * 65]),
            psum("pc2", [64, 2 * 65])]
    pmisc = psum("pmisc", [128, CH])
    pkv = [psum("pk0", [128, CH]), psum("pk1", [128, CH])]
    pst = pmisc[0:8, 0:128]

    def ctx_ps(h):
        if h < 7:
            return pctx[0][0:64, h * 65:(h + 1) * 65]
        if h < 14:
            return pctx[1][0:64, (h - 7) * 65:(h - 6) * 65]
        return pctx[2][0:64, (h - 14) * 65:(h - 13) * 65]

    def ctxsb_sl(h, ncol=65):
        r0 = (h % 2) * 64
        c0 = (h // 2) * 65
        return ctxsb[r0:r0 + 64, c0:c0 + ncol]

    sc = Sched()
    cc = lambda c: slice(c * CH, (c + 1) * CH)
    cc2 = lambda c: slice(c * 128, (c + 1) * 128)
    import collections as _co
    bg = _co.deque()

    def TH(eng, fn, reads=(), writes=(), sem=None, inc=None):
        return lambda: sc.emit(eng, fn, reads=reads, writes=writes, sem=sem, inc=inc)

    def inject(n=1):
        for _ in range(n):
            if not bg:
                return
            bg.popleft()()

    def drain_bg():
        while bg:
            bg.popleft()()

    def ensure(tok):
        while tok not in sc.writer and bg:
            bg.popleft()()

    def dma(q, key, out, in_, reads=(), writes=()):
        sc.emit(q, lambda e, o=out, i=in_: e.dma_start(out=o, in_=i),
                reads=reads, writes=writes, sem=key)

    dma("SY", "cst", cvals[:, :], cvals_in[:, :], writes=[("cvals",)])
    nc.const_aps.register(F32, -0.5, cvals[0:1, 0:1])
    nc.const_aps.register(F32, 1.0 / 1024.0, cvals[0:1, 4:5])
    dma("SY", "cst", ones_f[:, :], ones_in[:, :], writes=[("ones_f",)])
    dma("SY", "cst", ident[:, :], ident_in[:, :], writes=[("ident",)])
    dma("SY", "cst", onesr_f[:, :], onesr_in[:, :], writes=[("onesr_f",)])
    dma("SY", "cst", bq_rf[:, :], bq_r[:, :], writes=[("bq_rf",)])
    dma("SY", "cst", bd_cf[:, :], bd_c[:, :], writes=[("bd",)])
    dma("SY", "cst", bu_rf[:, :], bu_r[:, :], writes=[("bu_rf",)])
    dma("SY", "cst", bo_cf[:, :], bo_c[:, :], writes=[("bo",)])
    dma("SY", "cst", bf_cf[:, :], bf_c[:, :], writes=[("bf",)])

    sc.emit("V", lambda e: e.tensor_copy(out=ones_h[:, :], in_=ones_f[:, :]),
            reads=[("ones_f",)], writes=[("ones_h",)])
    sc.emit("V", lambda e: e.tensor_copy(out=onesr_h[:, :], in_=onesr_f[:, :]),
            reads=[("onesr_f",)], writes=[("onesr_h",)])
    sc.emit("V", lambda e: e.tensor_copy(out=bq_rh[:, :], in_=bq_rf[:, :]),
            reads=[("bq_rf",)], writes=[("bq_rh",)])
    sc.emit("V", lambda e: e.tensor_copy(out=bu_rh[:, :], in_=bu_rf[:, :]),
            reads=[("bu_rf",)], writes=[("bu_rh",)])
    dma("GP", "busc", buscr[0:1, :], bu_rh[0:1, :], reads=[("bu_rh",)], writes=[("buscr",)])
    dma("GP", "cstg", burep[0][:, :], buscr[0:1, D:D + CH].partition_broadcast(128),
        reads=[("buscr",)], writes=[("burep", 0)])
    dma("GP", "cstg", burep[1][:, :], buscr[0:1, D + CH:2 * D].partition_broadcast(128),
        reads=[("buscr",)], writes=[("burep", 1)])

    wi = [0]

    def load_weight(src_ap, dst_list, accum=None):
        th = []
        nblk = src_ap.shape[0] // 128
        cols = src_ap.shape[1]
        nh = (cols + D - 1) // D
        for k in range(nblk):
            for hh in range(nh):
                c0, c1 = hh * D, min((hh + 1) * D, cols)
                sidx = wi[0] % 2
                wi[0] += 1
                st = wstage[sidx]

                def dth(e, s=st, c0=c0, c1=c1, k=k, sa=src_ap):
                    return e.dma_start(out=s[:, 0:c1 - c0],
                                       in_=sa[k * 128:(k + 1) * 128, c0:c1])
                th.append(TH("SY", dth, reads=(), writes=[("wstage", sidx)],
                             sem=f"wst{sidx}"))
                acc = accum[k] if (accum is not None and hh == 0) else None

                def cast(e, s=st, d=dst_list[k], a=acc, c0=c0, c1=c1):
                    return e.activation(out=d[:, c0:c1], in_=s[:, 0:c1 - c0],
                                        func=AF.Copy, bias=0.0, scale=1.0, accum_out=a)
                th.append(TH("S", cast, reads=[("wstage", sidx)],
                             writes=[("wdst", id(dst_list), k)] if hh == nh - 1 else [("wdsth", id(dst_list), k)]))
        return th

    w_wq = load_weight(wq, wq_h)
    w_wave1 = (load_weight(wd, wd_h) + load_weight(wu, wu_h) + load_weight(wf, wf8))
    sc.emit("V", lambda e: e.tensor_copy(out=bo16[:, :], in_=bo_cf[:, :]),
            reads=[("bo",)], writes=[("bo16",)])
    sc.emit("T", lambda e: e.matmul(out=pmisc[32:33, 0:NB], lhsT=ones_h[:, 0:1],
                                    rhs=bo16[:, :], start=True, stop=True,
                                    skip_group_check=True),
            reads=[("bo16",), ("ones_h",)], writes=[("psum_sx",)])
    sc.emit("V", lambda e: e.reduce_sum(out=sbo[:, :], in_=pmisc[32:33, 0:NB],
                                        axis=mybir.AxisListType.X),
            reads=[("psum_sx",)], writes=[("sbo",)])
    sc.emit("V", lambda e: e.tensor_copy(out=sbo_h[:, :], in_=sbo[:, :]),
            reads=[("sbo",)], writes=[("sbo_h",)])
    for i in range(2):
        sc.emit("V", lambda e, i=i: e.memset(
            kvsb[i][:].rearrange("p (g n) -> p g n", g=32, n=65)[:, :, 64:65], 1.0),
            writes=[("kvones", i)])

    def stats1_head(c, q="GP"):
        th = []
        sc_ = scol[c % 2]
        for ti in range(4):
            def ld(e, ti=ti, c=c):
                return e.dma_start(out=xtok[ti % 2][:, :],
                                   in_=xtok_d[c * CH + ti * 128:c * CH + (ti + 1) * 128, :])
            pfx = "xtk" if q == "GP" else "xtsy"
            th.append(TH(q, ld, writes=[("xtok", ti % 2)], sem=f"{pfx}{ti % 2}"))
            th.append(TH("S", lambda e, ti=ti, sc_=sc_: e.activation(
                out=xtok[ti % 2][:, :], in_=xtok[ti % 2][:, :], func=AF.Copy,
                bias=0.0, scale=1.0, accum_out=sc_[:, ti:ti + 1]),
                reads=[("xtok", ti % 2)], writes=[("scolm", c % 2, ti), ("xtok", ti % 2)]))
            th.append(TH("S", lambda e, ti=ti, sc_=sc_: e.activation(
                out=xtok[ti % 2][:, :], in_=xtok[ti % 2][:, :], func=AF.Square,
                bias=0.0, scale=1.0, accum_out=sc_[:, 4 + ti:5 + ti]),
                reads=[("xtok", ti % 2)], writes=[("scolq", c % 2, ti), ("xtok", ti % 2)]))
        return th

    def stats1_tail(c):
        th = []
        sc_ = scol[c % 2]
        th.append(TH("V", lambda e, sc_=sc_: e.tensor_scalar_mul(out=sc_[:, 0:8], in0=sc_[:, 0:8],
                                                                scalar1=1.0 / D),
                     reads=[("scolm", c % 2, 0), ("scolm", c % 2, 1), ("scolm", c % 2, 2),
                            ("scolm", c % 2, 3), ("scolq", c % 2, 0), ("scolq", c % 2, 1),
                            ("scolq", c % 2, 2), ("scolq", c % 2, 3)],
                     writes=[("scol", c % 2)]))
        th.append(TH("S", lambda e, sc_=sc_: e.activation(out=msq4t[:, :], in_=sc_[:, 0:4],
                                                          func=AF.Square, bias=0.0, scale=1.0),
                     reads=[("scol", c % 2)], writes=[("msq4",)]))
        th.append(TH("V", lambda e, sc_=sc_: e.scalar_tensor_tensor(out=sc_[:, 4:8], in0=sc_[:, 4:8],
                                                                    scalar=1e-5, in1=msq4t[:, :],
                                                                    op0=ALU.add, op1=ALU.subtract),
                     reads=[("scol", c % 2), ("msq4",)], writes=[("scol", c % 2)]))
        th.append(TH("S", lambda e, sc_=sc_: e.activation(out=sc_[:, 4:8], in_=sc_[:, 4:8],
                                                          func=AF.Ln, bias=0.0, scale=1.0),
                     reads=[("scol", c % 2)], writes=[("scol", c % 2)]))
        th.append(TH("S", lambda e, sc_=sc_: e.activation(out=sc_[:, 4:8], in_=sc_[:, 4:8],
                                                          func=AF.Exp, bias=0.0, scale=-0.5),
                     reads=[("scol", c % 2), ("cvals",)], writes=[("scol", c % 2)]))
        th.append(TH("T", lambda e, sc_=sc_: e.transpose(out=pst, in_=sc_[:, :],
                                                         identity=ident[:, :]),
                     reads=[("scol", c % 2), ("ident",)], writes=[("pzdot",)]))
        th.append(TH("V", lambda e: e.tensor_copy(out=pst8[:, :], in_=pst),
                     reads=[("pzdot",)], writes=[("pst8",)]))

        def bnc(e, c=c):
            dst = scr[0:1, :].rearrange("a (g ch u t) -> a g ch u t",
                                        g=2, ch=NCH, u=4)[:, :, c, :, :]
            return e.dma_start(out=dst, in_=pst8[:, :])
        th.append(TH("SY", bnc, reads=[("pst8",)], writes=[("scr1", c)], sem="sc1"))

        def rp1(e, c=c):
            return e.dma_start(out=m1rep[c % NRP][:, :],
                               in_=scr[0:1, cc(c)].partition_broadcast(128))
        th.append(TH("SY", rp1, reads=[("scr1", c)], writes=[("m1rep", c % NRP)], sem=f"rpa{c % NRP}"))

        def rp2(e, c=c):
            return e.dma_start(out=i1rep[c % NRP][:, :],
                               in_=scr[0:1, T + c * CH:T + (c + 1) * CH].partition_broadcast(128))
        th.append(TH("SY", rp2, reads=[("scr1", c)], writes=[("i1rep", c % NRP)], sem=f"rpb{c % NRP}"))
        return th

    def normalize_thunks(c):
        th = []
        for b in range(NB):
            def ld(e, b=b, c=c):
                return e.dma_start(out=xr[b % 2][:, :],
                                   in_=xT[b * 128:(b + 1) * 128, cc(c)])
            th.append(TH("SY", ld, writes=[("xr", b % 2)], sem=f"xr{b % 2}"))
            th.append(TH("V", lambda e, b=b, c=c: e.tensor_sub(
                out=xc1[b % NXC][:, :], in0=xr[b % 2][:, :], in1=m1rep[c % NRP][:, :]),
                reads=[("xr", b % 2), ("m1rep", c % NRP)], writes=[("xc1", b % NXC)]))
            th.append(TH("V", lambda e, b=b, c=c: e.tensor_mul(
                out=xs1[c % NXS][b][:, :], in0=xc1[b % NXC][:, :], in1=i1rep[c % NRP][:, :]),
                reads=[("xc1", b % NXC), ("i1rep", c % NRP)], writes=[("xs1", c % NXS, b)]))
        return th

    def emit_compute1(c, post_q=None):
        for _ in range(4):
            inject(2)
        for db in range(NB):
            inject(2)
            ensure(("wdst", id(wq_h), NB - 1))
            pb = pmain[db % 2]

            def qmm(e, db=db, c=c, pb=pb):
                for kb in range(NB):
                    e.matmul(out=pb[:, :], lhsT=wq_h[kb][:, db * 128:(db + 1) * 128],
                             rhs=xs1[c % NXS][kb][:, :], start=(kb == 0), stop=False)
                return e.matmul(out=pb[:, :], lhsT=bq_rh[0:1, db * 128:(db + 1) * 128],
                                rhs=onesr_h[:, :], start=False, stop=True)
            sc.emit("T", qmm,
                    reads=[("xs1", c % NXS, kb) for kb in range(NB)] +
                          [("wdst", id(wq_h), kb) for kb in range(NB)] +
                          [("bq_rh",), ("onesr_h",)],
                    writes=[("pA", db % 2)])
            sc.emit("S", lambda e, pb=pb, db=db: e.activation(
                out=ebuf[db % 2][:, :], in_=pb[:, :], func=AF.Exp, bias=0.0, scale=1.0),
                reads=[("pA", db % 2)], writes=[("ebuf", db % 2)])
            sc.emit("V", lambda e, db=db: e.tensor_scalar_min(
                out=e1buf[db % 2][:, :], in0=ebuf[db % 2][:, :], scalar1=1.0),
                reads=[("ebuf", db % 2)], writes=[("ebuf", db % 2)])
            sc.emit("V", lambda e, pb=pb, db=db, c=c: e.scalar_tensor_tensor(
                out=qp[db][:, cc(c)], in0=pb[:, :], scalar=0.0, in1=e1buf[db % 2][:, :],
                op0=ALU.max, op1=ALU.add),
                reads=[("pA", db % 2), ("ebuf", db % 2)], writes=[("qp", db, c)])
        if post_q:
            bg.extend(post_q)
        ensure(("wdst", id(wd_h), NB - 1))
        for rb in range(2):
            pb = pmain[rb % 2]

            def kv1mm(e, rb=rb, c=c, pb=pb):
                last = None
                for kb in range(NB):
                    last = e.matmul(out=pb[:, :], lhsT=wd_h[kb][:, rb * 128:(rb + 1) * 128],
                                    rhs=xs1[c % NXS][kb][:, :], start=(kb == 0), stop=(kb == NB - 1))
                return last
            sc.emit("T", kv1mm,
                    reads=[("xs1", c % NXS, kb) for kb in range(NB)] +
                          [("wdst", id(wd_h), kb) for kb in range(NB)],
                    writes=[("pA", rb % 2)])
            sc.emit("V", lambda e, pb=pb, rb=rb, c=c: e.tensor_scalar(
                out=kv1[0][rb][:, :], in0=pb[:, :], scalar1=bd_cf[:, rb:rb + 1],
                scalar2=None, op0=ALU.add),
                reads=[("pA", rb % 2), ("bd",)], writes=[("kv1", 0, rb)])
        ensure(("wdst", id(wu_h), 1))
        for ti in range(4):
            i = c * 4 + ti
            kb_i = i % 2
            kvv = kvsb[kb_i][:].rearrange("p (g n) -> p g n", g=32, n=65)
            for fc in range(4):
                inject(2)
                pk = pkv[fc % 2]
                if fc < 2:
                    def kvmm(e, ti=ti, c=c, fc=fc, pk=pk):
                        for rb in range(2):
                            e.matmul(out=pk[:, :], lhsT=kv1[0][rb][:, ti * 128:(ti + 1) * 128],
                                     rhs=wu_h[rb][:, fc * CH:(fc + 1) * CH], start=(rb == 0), stop=False)
                        return e.matmul(out=pk[:, :], lhsT=onesr_h[0:1, 0:128],
                                        rhs=bu_rh[0:1, fc * CH:(fc + 1) * CH], start=False, stop=True)
                else:
                    def kvmm(e, ti=ti, c=c, fc=fc, pk=pk):
                        last = None
                        for rb in range(2):
                            last = e.matmul(out=pk[:, :], lhsT=kv1[0][rb][:, ti * 128:(ti + 1) * 128],
                                            rhs=wu_h[rb][:, fc * CH:(fc + 1) * CH],
                                            start=(rb == 0), stop=(rb == 1))
                        return last
                sc.emit("T", kvmm,
                        reads=[("kv1", 0, 0), ("kv1", 0, 1),
                               ("wdst", id(wu_h), 0), ("wdst", id(wu_h), 1),
                               ("onesr_h",), ("bu_rh",)],
                        writes=[("pC", fc % 2)])
                if fc < 2:
                    dst = kvv[:, fc * 8:(fc + 1) * 8, 0:64]
                    sc.emit("S", lambda e, pk=pk, fc=fc: e.activation(
                        out=ebuf[fc % 2][:, :], in_=pk[:, :], func=AF.Exp, bias=0.0, scale=1.0),
                        reads=[("pC", fc % 2)], writes=[("ebuf", fc % 2)])
                    sc.emit("V", lambda e, fc=fc: e.tensor_scalar_min(
                        out=e1buf[fc % 2][:, :], in0=ebuf[fc % 2][:, :], scalar1=1.0),
                        reads=[("ebuf", fc % 2)], writes=[("ebuf", fc % 2)])
                    sc.emit("V", lambda e, pk=pk, dst=dst, fc=fc: e.scalar_tensor_tensor(
                        out=dst, in0=pk[:, :], scalar=0.0, in1=e1buf[fc % 2][:, :],
                        op0=ALU.max, op1=ALU.add),
                        reads=[("pC", fc % 2), ("ebuf", fc % 2), ("kvones", kb_i)],
                        writes=[("kvk", kb_i, fc)])
                else:
                    dst = kvv[:, 16 + (fc - 2) * 8:16 + (fc - 1) * 8, 0:64]
                    sc.emit("V", lambda e, pk=pk, dst=dst, fc=fc: e.tensor_add(
                        out=dst, in0=pk[:, :], in1=burep[fc - 2][:, :]),
                        reads=[("pC", fc % 2), ("burep", fc - 2), ("kvones", kb_i)],
                        writes=[("kvv", kb_i, fc - 2)])

            def ctxmm(e, i=i, kb_i=kb_i):
                last = None
                for h in range(H):
                    first_in_bank = h in (0, 7, 14)
                    last = e.matmul(
                        out=ctx_ps(h),
                        lhsT=kvsb[kb_i][:, h * 65:h * 65 + 64],
                        rhs=kvsb[kb_i][:, 1040 + h * 65:1040 + (h + 1) * 65],
                        start=(i == 0 and first_in_bank), stop=(i == NT - 1),
                        skip_group_check=True)
                return last
            sc.emit("T", ctxmm,
                    reads=[("kvk", kb_i, 0), ("kvk", kb_i, 1),
                           ("kvv", kb_i, 0), ("kvv", kb_i, 1)],
                    writes=[("kvsb_free", kb_i)] + ([("ctx_acc",)] if i == NT - 1 else []))

    w_wave2 = load_weight(wo, wo_h, accum=[rwo[:, k:k + 1] for k in range(NB)])
    w_wave2.append(TH("V", lambda e: e.tensor_copy(out=rwo_h[:, :], in_=rwo[:, :]),
                      reads=[("wdst", id(wo_h), k) for k in range(NB)], writes=[("rwo_h",)]))

    s0 = stats1_head(0, q="SY") + stats1_tail(0) + stats1_head(1, q="SY")
    bg.extend(w_wq)
    while s0 or bg:
        if s0:
            s0.pop(0)()
        inject(1)
    for t in normalize_thunks(0):
        t()
    bg.extend(stats1_tail(1))
    bg.extend(normalize_thunks(1))
    bg.extend(w_wave1)
    for c in range(NCH):
        if c + 2 < NCH:
            bg.extend(stats1_head(c + 2))
        emit_compute1(c, post_q=(w_wave2 if c == NCH - 1 else None))
        if c + 2 < NCH:
            bg.extend(stats1_tail(c + 2))
            bg.extend(normalize_thunks(c + 2))
    drain_bg()


    for h in range(H):
        sc.emit("S", lambda e, h=h: e.activation(out=ctxsb_sl(h),
                                                 in_=ctx_ps(h)[0:64, 0:65],
                                                 func=AF.Copy, bias=0.0, scale=1.0),
                reads=[("ctx_acc",)], writes=[("ctxsb", h)])
    sc.emit("V", lambda e: e.memset(zlhs[:, :], 0.0), writes=[("zlhs_z",)])
    for kb in range(NB):
        for j in range(2):
            h = 2 * kb + j
            sc.emit("V", lambda e, kb=kb, j=j, h=h: e.tensor_copy(
                out=zlhs[j * 64:j * 64 + 64, kb * 16 + h:kb * 16 + h + 1],
                in_=ctxsb_sl(h, 65)[:, 64:65]),
                reads=[("ctxsb", h), ("zlhs_z",)], writes=[("zlhs", h)])

    def a1z_thunks(c):
        th = []

        def zmm(e, c=c):
            last = None
            for kb in range(NB):
                last = e.matmul(out=pmisc[0:16, :], lhsT=zlhs[:, kb * 16:(kb + 1) * 16],
                                rhs=qp[kb][:, cc(c)], start=(kb == 0), stop=(kb == NB - 1),
                                skip_group_check=True)
            return last
        th.append(TH("T", zmm, reads=[("zlhs", h) for h in range(H)] +
                     [("qp", kb, c) for kb in range(NB)], writes=[("pzdot",)]))
        th.append(TH("V", lambda e, c=c: e.tensor_scalar_add(
            out=zdsb[:, :], in0=pmisc[0:16, :], scalar1=1e-6),
            reads=[("pzdot",)], writes=[("zdsb",)]))
        th.append(TH("S", lambda e, c=c: e.activation(out=zdsb[:, :], in_=zdsb[:, :],
                                                      func=AF.Ln, bias=0.0, scale=1.0),
                     reads=[("zdsb",)], writes=[("zdsb",)]))
        th.append(TH("V", lambda e, c=c: e.tensor_scalar_mul(out=zdsb[:, :], in0=zdsb[:, :],
                                                             scalar1=-1.0),
                     reads=[("zdsb",)], writes=[("zdsb",)]))
        th.append(TH("S", lambda e, c=c: e.activation(out=zdsb[:, :], in_=zdsb[:, :],
                                                      func=AF.Exp, bias=0.0, scale=1.0),
                     reads=[("zdsb",)], writes=[("zdsb",)]))
        th.append(TH("V", lambda e: e.tensor_scalar_mul(out=z16[:, :], in0=zdsb[:, :],
                                                        scalar1=1024.0),
                     reads=[("zdsb",)], writes=[("z16",)]))
        th.append(TH("SY", lambda e, c=c: e.dma_start(out=zscr[:, cc(c)], in_=z16[:, :]),
                     reads=[("z16",)], writes=[("zscr", c)], sem="zsc"))
        return th

    def zb_thunks(c):
        th = []
        for p in range(PAIRS):
            th.extend(zrep_bcast(c, p))
        return th

    def zrep_bcast(c, p):
        def za(e, p=p, c=c):
            return e.dma_start(out=zrep[p][0:64, :],
                               in_=zscr[2 * p:2 * p + 1, cc(c)].partition_broadcast(64))

        def zb(e, p=p, c=c):
            return e.dma_start(out=zrep[p][64:128, :],
                               in_=zscr[2 * p + 1:2 * p + 2, cc(c)].partition_broadcast(64))
        return [TH("SY", za, reads=[("zscr", c)],
                   writes=[("zrep", p)], sem=f"zr{p}a"),
                TH("SY", zb, reads=[("zscr", c)],
                   writes=[("zrep2", p)], sem=f"zr{p}b")]

    def a1o_thunks(c):
        th = []
        for p in range(PAIRS):
            pb = pmain[p % 2]

            def outmm(e, p=p, c=c, pb=pb):
                e.matmul(out=pb[0:64, :], lhsT=ctxsb_sl(2 * p, 64),
                         rhs=qp[p][0:64, cc(c)], start=True, stop=True)
                return e.matmul(out=pb[64:128, :], lhsT=ctxsb_sl(2 * p + 1, 64),
                                rhs=qp[p][64:128, cc(c)], start=True, stop=True)
            th.append(TH("T", outmm, reads=[("ctxsb", 2 * p), ("ctxsb", 2 * p + 1), ("qp", p, c)],
                         writes=[("pA", p % 2)]))
            th.append(TH("S", lambda e, p=p, pb=pb: e.activation(
                out=oev[p % 2][:, :], in_=pb[:, :], func=AF.Identity,
                bias=0.0, scale=1.0 / 1024.0),
                reads=[("pA", p % 2), ("cvals",)], writes=[("oev", p % 2)]))
            th.append(TH("GP", lambda e, p=p, c=c: e.tensor_mul(
                out=outz[c % NOZ][p][:, :], in0=oev[p % 2][:, :], in1=zrep[p][:, :]),
                reads=[("oev", p % 2), ("zrep", p), ("zrep2", p)],
                writes=[("outz", c % NOZ, p)], sem="GP", inc=1))
        return th

    def a2_thunks(c):
        ensure(("wdst", id(wo_h), NB - 1))
        th = []
        for db in range(NB):
            pb = pmain[db % 2]

            def womm(e, db=db, c=c, pb=pb):
                last = None
                for kb in range(NB):
                    last = e.matmul(out=pb[:, :], lhsT=wo_h[kb][:, db * 128:(db + 1) * 128],
                                    rhs=outz[c % NOZ][kb][:, :], start=(kb == 0), stop=(kb == NB - 1))
                return last
            th.append(TH("T", womm,
                         reads=[("outz", c % NOZ, kb) for kb in range(NB)] +
                               [("wdst", id(wo_h), kb) for kb in range(NB)],
                         writes=[("pA", db % 2)]))

            def xld(e, db=db, c=c):
                return e.dma_start(out=xr[db % 2][:, :],
                                   in_=xT[db * 128:(db + 1) * 128, cc(c)])
            th.append(TH("SY", xld, writes=[("xr", db % 2)], sem=f"xr{db % 2}"))
            th.append(TH("V", lambda e, pb=pb, db=db, c=c: e.scalar_tensor_tensor(
                out=x2[c % NX2][db][:, :], in0=pb[:, :], scalar=bo_cf[:, db:db + 1],
                in1=xr[db % 2][:, :], op0=ALU.add, op1=ALU.add),
                reads=[("pA", db % 2), ("xr", db % 2), ("bo",)],
                writes=[("x2", c % NX2, db)]))
        return th

    def c1_thunks(c):
        th = []

        def sdmm(e, c=c):
            for kb in range(NB):
                e.matmul(out=pmisc[32:33, :], lhsT=rwo_h[:, kb:kb + 1],
                         rhs=outz[c % NOZ][kb][:, :], start=(kb == 0), stop=False,
                         skip_group_check=True)
            return e.matmul(out=pmisc[32:33, :], lhsT=sbo_h[:, :], rhs=onesr_h[:, :],
                            start=False, stop=True, skip_group_check=True)
        th.append(TH("T", sdmm, reads=[("outz", c % NOZ, kb) for kb in range(NB)] +
                     [("rwo_h",), ("sbo_h",), ("onesr_h",)], writes=[("psum_sx",)]))
        for db in range(NB):
            th.append(TH("S", lambda e, db=db, c=c: e.activation(
                out=xsq2[db % 2][:, :], in_=x2[c % NX2][db][:, :], func=AF.Square,
                bias=0.0, scale=1.0),
                reads=[("x2", c % NX2, db)], writes=[("xsq2", db % 2)]))
            th.append(TH("T", lambda e, db=db: e.matmul(
                out=pmisc[64:65, :], lhsT=ones_h[:, 0:1], rhs=xsq2[db % 2][:, :],
                start=(db == 0), stop=(db == NB - 1), skip_group_check=True),
                reads=[("xsq2", db % 2), ("ones_h",)], writes=[("psum_sq",)]))
        th.append(TH("SY", lambda e, c=c: e.dma_start(out=m1c[:, :], in_=scr[0:1, cc(c)]),
                     reads=[("scr1", c)], writes=[("m1c",)], sem="m1c"))
        th.append(TH("V", lambda e: e.tensor_scalar_mul(out=trow[0][:, :], in0=pmisc[32:33, :],
                                                        scalar1=1.0 / D),
                     reads=[("psum_sx",)], writes=[("trow0",)]))
        th.append(TH("V", lambda e, c=c: e.tensor_add(out=m2row[:, :], in0=m1c[:, :],
                                                      in1=trow[0][:, :]),
                     reads=[("trow0",), ("m1c",)], writes=[("m2row",)]))
        th.append(TH("V", lambda e: e.tensor_scalar_mul(out=trow[0][:, :], in0=pmisc[64:65, :],
                                                        scalar1=1.0 / D),
                     reads=[("psum_sq",)], writes=[("trow0",)]))
        th.append(TH("S", lambda e, c=c: e.activation(out=trow[1][:, :], in_=m2row[:, :],
                                                      func=AF.Square, bias=0.0, scale=1.0),
                     reads=[("m2row",)], writes=[("trow1",)]))
        th.append(TH("V", lambda e: e.scalar_tensor_tensor(out=trow[0][:, :], in0=trow[0][:, :],
                                                           scalar=1e-5, in1=trow[1][:, :],
                                                           op0=ALU.add, op1=ALU.subtract),
                     reads=[("trow0",), ("trow1",)], writes=[("trow0",)]))
        th.append(TH("S", lambda e: e.activation(out=trow[1][:, :], in_=trow[0][:, :],
                                                 func=AF.Ln, bias=0.0, scale=1.0),
                     reads=[("trow0",)], writes=[("trow1",)]))
        th.append(TH("S", lambda e, c=c: e.activation(out=i2row[:, :], in_=trow[1][:, :],
                                                      func=AF.Exp, bias=0.0, scale=-0.5),
                     reads=[("trow1",), ("cvals",)], writes=[("i2row",)]))

        def bnc(e, c=c):
            return e.dma_start(out=scr[1:2, :].rearrange("a (g t) -> a g t", g=2)[:, :, cc(c)],
                               in_=rows2[0:1, :].rearrange("a (g t) -> a g t", g=2))
        th.append(TH("SY", bnc, reads=[("m2row",), ("i2row",)], writes=[("scr2", c)], sem="sc2"))

        def rp1(e, c=c):
            return e.dma_start(out=m2rep[c % NRP][:, :],
                               in_=scr[1:2, cc(c)].partition_broadcast(128))
        th.append(TH("SY", rp1, reads=[("scr2", c)], writes=[("m1rep", c % NRP)], sem=f"rpa{c % NRP}"))

        def rp2(e, c=c):
            return e.dma_start(out=i2rep[c % NRP][:, :],
                               in_=scr[1:2, T + c * CH:T + (c + 1) * CH].partition_broadcast(128))
        th.append(TH("SY", rp2, reads=[("scr2", c)], writes=[("i1rep", c % NRP)], sem=f"rpb{c % NRP}"))
        return th

    def c2a_thunks(c):
        ensure(("wdst", id(wf8), NB - 1))
        th = []
        for db in range(NB):
            th.append(TH("V", lambda e, db=db, c=c: e.tensor_sub(
                out=xc2[db % NXC][:, :], in0=x2[c % NX2][db][:, :], in1=m2rep[c % NRP][:, :]),
                reads=[("x2", c % NX2, db), ("m1rep", c % NRP)], writes=[("xc1", db % NXC)]))
            th.append(TH("V", lambda e, db=db, c=c: e.tensor_mul(
                out=h2q[c % NH2][db][:, :], in0=xc2[db % NXC][:, :], in1=i2rep[c % NRP][:, :]),
                reads=[("xc1", db % NXC), ("i1rep", c % NRP)], writes=[("h2q", c % NH2, db)]))
            th.append(TH("T", lambda e, kb=db, c=c: e.matmul(
                out=pkv[0][:, :], lhsT=wf8[kb][:, 0:128],
                rhs=h2q[c % NH2][kb][:, :], start=(kb == 0), stop=(kb == NB - 1)),
                reads=[("h2q", c % NH2, db), ("wdst", id(wf8), db)],
                writes=[("pC", 0)]))
        return th

    def c2b_thunks(c):
        th = []
        for fb in range(NB):
            pb = pkv[fb % 2]
            if fb > 0:
                def ffnmm(e, fb=fb, c=c, pb=pb):
                    last = None
                    for kb in range(NB):
                        last = e.matmul(out=pb[:, :], lhsT=wf8[kb][:, fb * 128:(fb + 1) * 128],
                                        rhs=h2q[c % NH2][kb][:, :], start=(kb == 0), stop=(kb == NB - 1))
                    return last
                th.append(TH("T", ffnmm,
                             reads=[("h2q", c % NH2, kb) for kb in range(NB)] +
                                   [("wdst", id(wf8), kb) for kb in range(NB)],
                             writes=[("pC", fb % 2)]))
            th.append(TH("V", lambda e, pb=pb, fb=fb, c=c: e.scalar_tensor_tensor(
                out=x2[c % NX2][fb][:, :], in0=pb[:, :], scalar=bf_cf[:, fb:fb + 1],
                in1=x2[c % NX2][fb][:, :], op0=ALU.add, op1=ALU.add),
                reads=[("pC", fb % 2)], writes=[("x2", c % NX2, fb)]))

            def yst(e, fb=fb, c=c):
                return e.dma_start(out=yT[fb * 128:(fb + 1) * 128, cc(c)],
                                   in_=x2[c % NX2][fb][:, :])
            th.append(TH("GP", yst, reads=[("x2", c % NX2, fb)],
                         writes=[("ydma", c, fb)], sem=f"yd{fb}"))
        return th

    def run_all(th):
        for t in th:
            t()

    def interleave(l1, l2):
        i = j = 0
        while i < len(l1) or j < len(l2):
            if i < len(l1):
                l1[i]()
                i += 1
            if j < len(l2):
                l2[j]()
                j += 1

    run_all(a1z_thunks(0))
    run_all(zb_thunks(0))
    interleave(a1z_thunks(1) if NCH > 1 else [], a1o_thunks(0))
    run_all(a2_thunks(0))
    for c in range(NCH):
        ahead = (a1z_thunks(c + 2) if c + 2 < NCH else []) + \
                (zb_thunks(c + 1) if c + 1 < NCH else [])
        interleave(ahead, c1_thunks(c))
        a1o = a1o_thunks(c + 1) if c + 1 < NCH else []
        interleave(c2a_thunks(c), a1o)
        if c + 1 < NCH:
            run_all(a2_thunks(c + 1))
        run_all(c2b_thunks(c))

    sems = {}
    with ExitStack() as ses:
        for name in sc.sem_names:
            sems[name] = ses.enter_context(nc.semaphore(f"sem{name}"))
        block = ses.enter_context(nc.Block())

        @block.tensor
        def _(eng):
            sc.replay("T", eng, sems)

        @block.vector
        def _(eng):
            sc.replay("V", eng, sems)

        @block.scalar
        def _(eng):
            sc.replay("S", eng, sems)

        @block.sync
        def _(eng):
            sc.replay("SY", eng, sems)

        @block.gpsimd
        def _(eng):
            sc.replay("GP", eng, sems)
            for fb in range(NB):
                eng.wait_ge(sems[f"yd{fb}"], sc.count[f"yd{fb}"])
    es.close()
    return nc


_NC_CACHE = None


def kernel(L0_attn_x, L0_attn_Wq, L0_attn_bq, L0_attn_Wd, L0_attn_bd,
           L0_attn_Wu, L0_attn_bu, L0_attn_Wo, L0_attn_bo,
           L0_ffn_weight, L0_ffn_bias):
    global _NC_CACHE
    x = np.asarray(L0_attn_x, dtype=np.float32)
    B = x.shape[0]
    f32 = lambda a: np.ascontiguousarray(np.asarray(a, dtype=np.float32))
    kidx = np.concatenate([np.arange(h * 128, h * 128 + 64) for h in range(H)])
    vidx = np.concatenate([np.arange(h * 128 + 64, h * 128 + 128) for h in range(H)])
    perm = np.concatenate([kidx, vidx])
    common = {
        "wq": f32(L0_attn_Wq),
        "wd": f32(L0_attn_Wd),
        "wu": f32(np.asarray(L0_attn_Wu, dtype=np.float32)[:, perm]),
        "wo": f32(L0_attn_Wo),
        "wf": f32(np.asarray(L0_ffn_weight, dtype=np.float32).T),
        "bq_r": f32(np.asarray(L0_attn_bq, dtype=np.float32).reshape(1, D)),
        "bd_c": f32(np.asarray(L0_attn_bd, dtype=np.float32).reshape(2, 128).T),
        "bu_r": f32(np.asarray(L0_attn_bu, dtype=np.float32)[perm].reshape(1, 2 * D)),
        "bo_c": f32(np.asarray(L0_attn_bo, dtype=np.float32).reshape(NB, 128).T),
        "bf_c": f32(np.asarray(L0_ffn_bias, dtype=np.float32).reshape(NB, 128).T),
        "ones_in": np.ones((128, 128), dtype=np.float32),
        "onesr_in": np.ones((1, CH), dtype=np.float32),
        "cvals_in": np.array([[-0.5, 1.0 / D, 1e-5, -1.0, 1.0 / 1024.0]], dtype=np.float32),
        "ident_in": np.eye(128, dtype=np.float32),
    }
    in_maps = []
    for b in range(B):
        m = dict(common)
        m["xT"] = np.ascontiguousarray(x[b].T)
        m["xtok_d"] = np.ascontiguousarray(x[b])
        in_maps.append(m)
    if _NC_CACHE is None:
        _NC_CACHE = build_nc()
    res = run_bass_kernel_spmd(_NC_CACHE, in_maps, core_ids=list(range(B)))
    out = np.stack([np.ascontiguousarray(r["yT"].T) for r in res.results], axis=0)
    return out


# version 19
# speedup vs baseline: 1.0084x; 1.0084x over previous
import numpy as np
from contextlib import ExitStack
import concourse.bass as bass
import concourse.mybir as mybir
from concourse.bass_utils import run_bass_kernel_spmd

F32 = mybir.dt.float32
F16 = mybir.dt.float16
FP8 = mybir.dt.float8e4
AF = mybir.ActivationFunctionType
ALU = mybir.AluOpType

T, D, H, DH, RK = 2048, 1024, 16, 64, 256
CH = 512
NCH = T // CH
NB = D // 128
NT = T // 128
PAIRS = H // 2
NXCH, NXS, NX2, NOZ, NH2, NRP, NXC = 1, 2, 2, 1, 1, 2, 1


class Sched:

    ENGS = ("T", "V", "S", "SY", "GP")
    ONE_WAVE = ("cst", "cstg")

    def __init__(self):
        self.steps = {e: [] for e in self.ENGS}
        self.count = {}
        self.writer = {}
        self.readers = {}
        self.watermark = {e: {} for e in self.ENGS}
        self.sem_names = []

    def _bump(self, sem, inc):
        if sem not in self.count:
            self.count[sem] = 0
            self.sem_names.append(sem)
        self.count[sem] += inc
        return self.count[sem]

    def emit(self, eng, fn, reads=(), writes=(), sem=None, inc=None):
        if eng in ("SY", "GP") and inc != 1:
            assert sem is not None, "DMA steps need a sem stream name"
            inc = 16
        else:
            sem = sem or eng
            inc = 1
        waits = {}
        def add(w):
            if w is None:
                return
            ws, wv = w
            if ws == "T" and eng == "T":
                return
            waits[ws] = max(waits.get(ws, 0), wv)
        for t in reads:
            add(self.writer.get(t))
        for t in writes:
            add(self.writer.get(t))
            for r in self.readers.get(t, ()):
                add(r)
        myval = self._bump(sem, inc)
        final_waits = []
        for ws, wv in sorted(waits.items()):
            if ws == sem and eng not in ("SY", "GP"):
                pass
            if self.watermark[eng].get(ws, 0) < wv:
                final_waits.append((ws, wv))
                self.watermark[eng][ws] = wv
        self.steps[eng].append((final_waits, fn, sem, inc))
        for t in writes:
            self.writer[t] = (sem, myval)
            self.readers[t] = []
        for t in reads:
            self.readers.setdefault(t, []).append((sem, myval))

    def replay(self, eng_name, eng_obj, sems):
        for (waits, fn, sem, inc) in self.steps[eng_name]:
            for (ws, wv) in waits:
                if ws in self.ONE_WAVE:
                    wv = self.count[ws]
                eng_obj.wait_ge(sems[ws], wv)
            last = fn(eng_obj)
            assert last is not None, "step fn must return its last instruction"
            last.then_inc(sems[sem], inc)


def build_nc():
    nc = bass.Bass("TRN2", target_bir_lowering=False)
    dt = nc.dram_tensor
    xT = dt("xT", [D, T], F32, kind="ExternalInput").ap()
    xtok_d = dt("xtok_d", [T, D], F32, kind="ExternalInput").ap()
    ident_in = dt("ident_in", [128, 128], F32, kind="ExternalInput").ap()
    wq = dt("wq", [D, D], F32, kind="ExternalInput").ap()
    wd = dt("wd", [D, RK], F32, kind="ExternalInput").ap()
    wu = dt("wu", [RK, 2 * D], F32, kind="ExternalInput").ap()
    wo = dt("wo", [D, D], F32, kind="ExternalInput").ap()
    wf = dt("wf", [D, D], F32, kind="ExternalInput").ap()
    bq_r = dt("bq_r", [1, D], F32, kind="ExternalInput").ap()
    bd_c = dt("bd_c", [128, 2], F32, kind="ExternalInput").ap()
    bu_r = dt("bu_r", [1, 2 * D], F32, kind="ExternalInput").ap()
    bo_c = dt("bo_c", [128, NB], F32, kind="ExternalInput").ap()
    bf_c = dt("bf_c", [128, NB], F32, kind="ExternalInput").ap()
    ones_in = dt("ones_in", [128, 128], F32, kind="ExternalInput").ap()
    onesr_in = dt("onesr_in", [1, CH], F32, kind="ExternalInput").ap()
    cvals_in = dt("cvals_in", [1, 5], F32, kind="ExternalInput").ap()
    scr = dt("scr", [2, 2 * T], F32, kind="Internal").ap()
    zscr = dt("zscr", [H, T], F16, kind="Internal").ap()
    buscr = dt("buscr", [1, 2 * D], F16, kind="Internal").ap()
    yT = dt("yT", [D, T], F32, kind="ExternalOutput").ap()

    es = ExitStack()
    sb = lambda name, shape, d: es.enter_context(nc.sbuf_tensor(name, shape, d))

    wstage = [sb(f"wstage{i}", [128, D], F32) for i in range(2)]
    wq_h = [sb(f"wq_h{k}", [128, D], F16) for k in range(NB)]
    wd_h = [sb(f"wd_h{k}", [128, RK], F16) for k in range(NB)]
    wu_h = [sb(f"wu_h{k}", [128, 2 * D], F16) for k in range(2)]
    wo_h = wq_h
    wf8 = [sb(f"wf8_{k}", [128, D], FP8) for k in range(NB)]
    rwo = sb("rwo", [128, NB], F32)
    rwo_h = sb("rwo_h", [128, NB], F16)
    bq_rf = sb("bq_rf", [1, D], F32)
    bq_rh = sb("bq_rh", [1, D], F16)
    bu_rf = sb("bu_rf", [1, 2 * D], F32)
    bu_rh = sb("bu_rh", [1, 2 * D], F16)
    bd_cf = sb("bd_cf", [128, 2], F32)
    bo_cf = sb("bo_cf", [128, NB], F32)
    bf_cf = sb("bf_cf", [128, NB], F32)
    cvals = sb("cvals", [1, 5], F32)
    ones_f = sb("ones_f", [128, 128], F32)
    ones_h = sb("ones_h", [128, 128], F16)
    onesr_f = sb("onesr_f", [1, CH], F32)
    onesr_h = sb("onesr_h", [1, CH], F16)
    burep = [sb(f"burep{i}", [128, CH], F16) for i in range(2)]

    xtok = [sb(f"xtok{i}", [128, D], F32) for i in range(2)]
    scol = [sb(f"scol{i}", [128, 8], F32) for i in range(2)]
    pst8 = sb("pst8", [8, 128], F32)
    msq4t = sb("msq4t", [128, 4], F32)
    ident = sb("ident", [128, 128], F32)
    m1rep = [sb(f"m1rep{i}", [128, CH], F32) for i in range(NRP)]
    i1rep = [sb(f"i1rep{i}", [128, CH], F32) for i in range(NRP)]
    m2rep = m1rep
    i2rep = i1rep
    xc1 = [sb(f"xc1_{i}", [128, CH], F32) for i in range(NXC)]
    xs1 = [[sb(f"xs1_{i}_{b}", [128, CH], F16) for b in range(NB)] for i in range(NXS)]
    qp = [sb(f"qp{b}", [128, T], F16) for b in range(NB)]
    ebuf = [sb(f"ebuf{i}", [128, CH], F16) for i in range(2)]
    e1buf = ebuf
    kv1 = [[sb(f"kv1_{i}_{r}", [128, CH], F16) for r in range(2)] for i in range(1)]
    kvsb = [sb(f"kvsb{i}", [128, 2 * 1040], F16) for i in range(2)]
    ctxsb = sb("ctxsb", [128, PAIRS * 65], F16)
    bo16 = sb("bo16", [128, NB], F16)
    sbo = sb("sbo", [1, 1], F32)
    sbo_h = sb("sbo_h", [1, 1], F16)
    zlhs = sb("zlhs", [128, NB * 16], F16)
    zdsb = sb("zdsb", [16, CH], F32)
    z16 = sb("z16", [16, CH], F16)
    oev = [sb(f"oev{i}", [128, CH], F32) for i in range(2)]
    rows2 = sb("rows2", [1, 2 * CH], F32)
    m2row = rows2[0:1, 0:CH]
    i2row = rows2[0:1, CH:2 * CH]
    m1c = sb("m1c", [1, CH], F32)
    trow = [sb(f"trow{i}", [1, CH], F32) for i in range(2)]
    trow = trow + trow
    zrep = [sb(f"zrep{i}", [128, CH], F16) for i in range(8)]
    outz = [[sb(f"outz{i}_{b}", [128, CH], F16) for b in range(NB)] for i in range(NOZ)]
    xr = [sb(f"xr{i}", [128, CH], F32) for i in range(2)]
    x2 = [[sb(f"x2_0_{b}", [128, CH], F32) for b in range(NB)]]
    x2v = []
    for nm, hostt in (("x2v0", wstage[0]), ("x2v1", wstage[1]),
                      ("x2v2", xtok[0]), ("x2v3", xtok[1])):
        vt = nc.virtual_sbuf_tensor(nm, [128, 2 * CH], F32)
        nc.back_virtual_tensor_with(vt, hostt)
        x2v.append(vt)
    x2.append([x2v[b // 2][:, (b % 2) * CH:(b % 2 + 1) * CH] for b in range(NB)])
    xsq2 = [sb(f"xsq2_{i}", [128, CH], F16) for i in range(2)]
    xc2 = xc1
    h2q = [[sb(f"h2q{i}_{b}", [128, CH], FP8) for b in range(NB)] for i in range(NH2)]

    psum = lambda name, shape: es.enter_context(nc.psum_tensor(name, shape, F32))
    pmain = [psum("pm0", [128, CH]), psum("pm1", [128, CH])]
    pctx = [psum("pc0", [64, 7 * 65]), psum("pc1", [64, 7 * 65]),
            psum("pc2", [64, 2 * 65])]
    pmisc = psum("pmisc", [128, CH])
    pkv = [psum("pk0", [128, CH]), psum("pk1", [128, CH])]
    pst = pmisc[0:8, 0:128]

    def ctx_ps(h):
        if h < 7:
            return pctx[0][0:64, h * 65:(h + 1) * 65]
        if h < 14:
            return pctx[1][0:64, (h - 7) * 65:(h - 6) * 65]
        return pctx[2][0:64, (h - 14) * 65:(h - 13) * 65]

    def ctxsb_sl(h, ncol=65):
        r0 = (h % 2) * 64
        c0 = (h // 2) * 65
        return ctxsb[r0:r0 + 64, c0:c0 + ncol]

    sc = Sched()
    cc = lambda c: slice(c * CH, (c + 1) * CH)
    cc2 = lambda c: slice(c * 128, (c + 1) * 128)
    import collections as _co
    bg = _co.deque()

    def TH(eng, fn, reads=(), writes=(), sem=None, inc=None):
        return lambda: sc.emit(eng, fn, reads=reads, writes=writes, sem=sem, inc=inc)

    def inject(n=1):
        for _ in range(n):
            if not bg:
                return
            bg.popleft()()

    def drain_bg():
        while bg:
            bg.popleft()()

    def ensure(tok):
        while tok not in sc.writer and bg:
            bg.popleft()()

    def dma(q, key, out, in_, reads=(), writes=()):
        sc.emit(q, lambda e, o=out, i=in_: e.dma_start(out=o, in_=i),
                reads=reads, writes=writes, sem=key)

    dma("SY", "cst", cvals[:, :], cvals_in[:, :], writes=[("cvals",)])
    nc.const_aps.register(F32, -0.5, cvals[0:1, 0:1])
    nc.const_aps.register(F32, 1.0 / 1024.0, cvals[0:1, 4:5])
    dma("SY", "cst", ones_f[:, :], ones_in[:, :], writes=[("ones_f",)])
    dma("SY", "cst", ident[:, :], ident_in[:, :], writes=[("ident",)])
    dma("SY", "cst", onesr_f[:, :], onesr_in[:, :], writes=[("onesr_f",)])
    dma("SY", "cst", bq_rf[:, :], bq_r[:, :], writes=[("bq_rf",)])
    dma("SY", "cst", bd_cf[:, :], bd_c[:, :], writes=[("bd",)])
    dma("SY", "cst", bu_rf[:, :], bu_r[:, :], writes=[("bu_rf",)])
    dma("SY", "cst", bo_cf[:, :], bo_c[:, :], writes=[("bo",)])
    dma("SY", "cst", bf_cf[:, :], bf_c[:, :], writes=[("bf",)])

    sc.emit("V", lambda e: e.tensor_copy(out=ones_h[:, :], in_=ones_f[:, :]),
            reads=[("ones_f",)], writes=[("ones_h",)])
    sc.emit("V", lambda e: e.tensor_copy(out=onesr_h[:, :], in_=onesr_f[:, :]),
            reads=[("onesr_f",)], writes=[("onesr_h",)])
    sc.emit("V", lambda e: e.tensor_copy(out=bq_rh[:, :], in_=bq_rf[:, :]),
            reads=[("bq_rf",)], writes=[("bq_rh",)])
    sc.emit("V", lambda e: e.tensor_copy(out=bu_rh[:, :], in_=bu_rf[:, :]),
            reads=[("bu_rf",)], writes=[("bu_rh",)])
    dma("GP", "busc", buscr[0:1, :], bu_rh[0:1, :], reads=[("bu_rh",)], writes=[("buscr",)])
    dma("GP", "cstg", burep[0][:, :], buscr[0:1, D:D + CH].partition_broadcast(128),
        reads=[("buscr",)], writes=[("burep", 0)])
    dma("GP", "cstg", burep[1][:, :], buscr[0:1, D + CH:2 * D].partition_broadcast(128),
        reads=[("buscr",)], writes=[("burep", 1)])

    wi = [0]

    def load_weight(src_ap, dst_list, accum=None):
        th = []
        nblk = src_ap.shape[0] // 128
        cols = src_ap.shape[1]
        nh = (cols + D - 1) // D
        for k in range(nblk):
            for hh in range(nh):
                c0, c1 = hh * D, min((hh + 1) * D, cols)
                sidx = wi[0] % 2
                wi[0] += 1
                st = wstage[sidx]

                def dth(e, s=st, c0=c0, c1=c1, k=k, sa=src_ap):
                    return e.dma_start(out=s[:, 0:c1 - c0],
                                       in_=sa[k * 128:(k + 1) * 128, c0:c1])
                th.append(TH("SY", dth, reads=(), writes=[("wstage", sidx)],
                             sem=f"wst{sidx}"))
                acc = accum[k] if (accum is not None and hh == 0) else None

                def cast(e, s=st, d=dst_list[k], a=acc, c0=c0, c1=c1):
                    return e.activation(out=d[:, c0:c1], in_=s[:, 0:c1 - c0],
                                        func=AF.Copy, bias=0.0, scale=1.0, accum_out=a)
                th.append(TH("S", cast, reads=[("wstage", sidx)],
                             writes=[("wdst", id(dst_list), k)] if hh == nh - 1 else [("wdsth", id(dst_list), k)]))
        return th

    w_wq = load_weight(wq, wq_h)
    w_wave1 = (load_weight(wd, wd_h) + load_weight(wu, wu_h) + load_weight(wf, wf8))
    sc.emit("V", lambda e: e.tensor_copy(out=bo16[:, :], in_=bo_cf[:, :]),
            reads=[("bo",)], writes=[("bo16",)])
    sc.emit("T", lambda e: e.matmul(out=pmisc[32:33, 0:NB], lhsT=ones_h[:, 0:1],
                                    rhs=bo16[:, :], start=True, stop=True,
                                    skip_group_check=True),
            reads=[("bo16",), ("ones_h",)], writes=[("psum_sx",)])
    sc.emit("V", lambda e: e.reduce_sum(out=sbo[:, :], in_=pmisc[32:33, 0:NB],
                                        axis=mybir.AxisListType.X),
            reads=[("psum_sx",)], writes=[("sbo",)])
    sc.emit("V", lambda e: e.tensor_copy(out=sbo_h[:, :], in_=sbo[:, :]),
            reads=[("sbo",)], writes=[("sbo_h",)])
    for i in range(2):
        sc.emit("V", lambda e, i=i: e.memset(
            kvsb[i][:].rearrange("p (g n) -> p g n", g=32, n=65)[:, :, 64:65], 1.0),
            writes=[("kvones", i)])

    def stats1_head(c, q="GP"):
        th = []
        sc_ = scol[c % 2]
        for ti in range(4):
            def ld(e, ti=ti, c=c):
                return e.dma_start(out=xtok[ti % 2][:, :],
                                   in_=xtok_d[c * CH + ti * 128:c * CH + (ti + 1) * 128, :])
            pfx = "xtk" if q == "GP" else "xtsy"
            th.append(TH(q, ld, writes=[("xtok", ti % 2)], sem=f"{pfx}{ti % 2}"))
            th.append(TH("S", lambda e, ti=ti, sc_=sc_: e.activation(
                out=xtok[ti % 2][:, :], in_=xtok[ti % 2][:, :], func=AF.Copy,
                bias=0.0, scale=1.0, accum_out=sc_[:, ti:ti + 1]),
                reads=[("xtok", ti % 2)], writes=[("scolm", c % 2, ti), ("xtok", ti % 2)]))
            th.append(TH("S", lambda e, ti=ti, sc_=sc_: e.activation(
                out=xtok[ti % 2][:, :], in_=xtok[ti % 2][:, :], func=AF.Square,
                bias=0.0, scale=1.0, accum_out=sc_[:, 4 + ti:5 + ti]),
                reads=[("xtok", ti % 2)], writes=[("scolq", c % 2, ti), ("xtok", ti % 2)]))
        return th

    def stats1_tail(c):
        th = []
        sc_ = scol[c % 2]
        th.append(TH("V", lambda e, sc_=sc_: e.tensor_scalar_mul(out=sc_[:, 0:8], in0=sc_[:, 0:8],
                                                                scalar1=1.0 / D),
                     reads=[("scolm", c % 2, 0), ("scolm", c % 2, 1), ("scolm", c % 2, 2),
                            ("scolm", c % 2, 3), ("scolq", c % 2, 0), ("scolq", c % 2, 1),
                            ("scolq", c % 2, 2), ("scolq", c % 2, 3)],
                     writes=[("scol", c % 2)]))
        th.append(TH("S", lambda e, sc_=sc_: e.activation(out=msq4t[:, :], in_=sc_[:, 0:4],
                                                          func=AF.Square, bias=0.0, scale=1.0),
                     reads=[("scol", c % 2)], writes=[("msq4",)]))
        th.append(TH("V", lambda e, sc_=sc_: e.scalar_tensor_tensor(out=sc_[:, 4:8], in0=sc_[:, 4:8],
                                                                    scalar=1e-5, in1=msq4t[:, :],
                                                                    op0=ALU.add, op1=ALU.subtract),
                     reads=[("scol", c % 2), ("msq4",)], writes=[("scol", c % 2)]))
        th.append(TH("S", lambda e, sc_=sc_: e.activation(out=sc_[:, 4:8], in_=sc_[:, 4:8],
                                                          func=AF.Ln, bias=0.0, scale=1.0),
                     reads=[("scol", c % 2)], writes=[("scol", c % 2)]))
        th.append(TH("S", lambda e, sc_=sc_: e.activation(out=sc_[:, 4:8], in_=sc_[:, 4:8],
                                                          func=AF.Exp, bias=0.0, scale=-0.5),
                     reads=[("scol", c % 2), ("cvals",)], writes=[("scol", c % 2)]))
        th.append(TH("T", lambda e, sc_=sc_: e.transpose(out=pst, in_=sc_[:, :],
                                                         identity=ident[:, :]),
                     reads=[("scol", c % 2), ("ident",)], writes=[("pzdot",)]))
        th.append(TH("V", lambda e: e.tensor_copy(out=pst8[:, :], in_=pst),
                     reads=[("pzdot",)], writes=[("pst8",)]))

        def bnc(e, c=c):
            dst = scr[0:1, :].rearrange("a (g ch u t) -> a g ch u t",
                                        g=2, ch=NCH, u=4)[:, :, c, :, :]
            return e.dma_start(out=dst, in_=pst8[:, :])
        th.append(TH("SY", bnc, reads=[("pst8",)], writes=[("scr1", c)], sem="sc1"))

        def rp1(e, c=c):
            return e.dma_start(out=m1rep[c % NRP][:, :],
                               in_=scr[0:1, cc(c)].partition_broadcast(128))
        th.append(TH("SY", rp1, reads=[("scr1", c)], writes=[("m1rep", c % NRP)], sem=f"rpa{c % NRP}"))

        def rp2(e, c=c):
            return e.dma_start(out=i1rep[c % NRP][:, :],
                               in_=scr[0:1, T + c * CH:T + (c + 1) * CH].partition_broadcast(128))
        th.append(TH("SY", rp2, reads=[("scr1", c)], writes=[("i1rep", c % NRP)], sem=f"rpb{c % NRP}"))
        return th

    def normalize_thunks(c):
        th = []
        for b in range(NB):
            def ld(e, b=b, c=c):
                return e.dma_start(out=xr[b % 2][:, :],
                                   in_=xT[b * 128:(b + 1) * 128, cc(c)])
            th.append(TH("SY", ld, writes=[("xr", b % 2)], sem=f"xr{b % 2}"))
            th.append(TH("V", lambda e, b=b, c=c: e.tensor_sub(
                out=xc1[b % NXC][:, :], in0=xr[b % 2][:, :], in1=m1rep[c % NRP][:, :]),
                reads=[("xr", b % 2), ("m1rep", c % NRP)], writes=[("xc1", b % NXC)]))
            th.append(TH("V", lambda e, b=b, c=c: e.tensor_mul(
                out=xs1[c % NXS][b][:, :], in0=xc1[b % NXC][:, :], in1=i1rep[c % NRP][:, :]),
                reads=[("xc1", b % NXC), ("i1rep", c % NRP)], writes=[("xs1", c % NXS, b)]))
        return th

    def emit_compute1(c, post_q=None):
        for _ in range(4):
            inject(2)
        for db in range(NB):
            inject(2)
            ensure(("wdst", id(wq_h), NB - 1))
            pb = pmain[db % 2]

            def qmm(e, db=db, c=c, pb=pb):
                for kb in range(NB):
                    e.matmul(out=pb[:, :], lhsT=wq_h[kb][:, db * 128:(db + 1) * 128],
                             rhs=xs1[c % NXS][kb][:, :], start=(kb == 0), stop=False)
                return e.matmul(out=pb[:, :], lhsT=bq_rh[0:1, db * 128:(db + 1) * 128],
                                rhs=onesr_h[:, :], start=False, stop=True)
            sc.emit("T", qmm,
                    reads=[("xs1", c % NXS, kb) for kb in range(NB)] +
                          [("wdst", id(wq_h), kb) for kb in range(NB)] +
                          [("bq_rh",), ("onesr_h",)],
                    writes=[("pA", db % 2)])
            sc.emit("S", lambda e, pb=pb, db=db: e.activation(
                out=ebuf[db % 2][:, :], in_=pb[:, :], func=AF.Exp, bias=0.0, scale=1.0),
                reads=[("pA", db % 2)], writes=[("ebuf", db % 2)])
            sc.emit("V", lambda e, db=db: e.tensor_scalar_min(
                out=e1buf[db % 2][:, :], in0=ebuf[db % 2][:, :], scalar1=1.0),
                reads=[("ebuf", db % 2)], writes=[("ebuf", db % 2)])
            sc.emit("V", lambda e, pb=pb, db=db, c=c: e.scalar_tensor_tensor(
                out=qp[db][:, cc(c)], in0=pb[:, :], scalar=0.0, in1=e1buf[db % 2][:, :],
                op0=ALU.max, op1=ALU.add),
                reads=[("pA", db % 2), ("ebuf", db % 2)], writes=[("qp", db, c)])
        if post_q:
            bg.extend(post_q)
        ensure(("wdst", id(wd_h), NB - 1))
        for rb in range(2):
            pb = pmain[rb % 2]

            def kv1mm(e, rb=rb, c=c, pb=pb):
                last = None
                for kb in range(NB):
                    last = e.matmul(out=pb[:, :], lhsT=wd_h[kb][:, rb * 128:(rb + 1) * 128],
                                    rhs=xs1[c % NXS][kb][:, :], start=(kb == 0), stop=(kb == NB - 1))
                return last
            sc.emit("T", kv1mm,
                    reads=[("xs1", c % NXS, kb) for kb in range(NB)] +
                          [("wdst", id(wd_h), kb) for kb in range(NB)],
                    writes=[("pA", rb % 2)])
            sc.emit("V", lambda e, pb=pb, rb=rb, c=c: e.tensor_scalar(
                out=kv1[0][rb][:, :], in0=pb[:, :], scalar1=bd_cf[:, rb:rb + 1],
                scalar2=None, op0=ALU.add),
                reads=[("pA", rb % 2), ("bd",)], writes=[("kv1", 0, rb)])
        ensure(("wdst", id(wu_h), 1))
        for ti in range(4):
            i = c * 4 + ti
            kb_i = i % 2
            kvv = kvsb[kb_i][:].rearrange("p (g n) -> p g n", g=32, n=65)
            for fc in range(4):
                inject(2)
                pk = pkv[fc % 2]
                if fc < 2:
                    def kvmm(e, ti=ti, c=c, fc=fc, pk=pk):
                        for rb in range(2):
                            e.matmul(out=pk[:, :], lhsT=kv1[0][rb][:, ti * 128:(ti + 1) * 128],
                                     rhs=wu_h[rb][:, fc * CH:(fc + 1) * CH], start=(rb == 0), stop=False)
                        return e.matmul(out=pk[:, :], lhsT=onesr_h[0:1, 0:128],
                                        rhs=bu_rh[0:1, fc * CH:(fc + 1) * CH], start=False, stop=True)
                else:
                    def kvmm(e, ti=ti, c=c, fc=fc, pk=pk):
                        last = None
                        for rb in range(2):
                            last = e.matmul(out=pk[:, :], lhsT=kv1[0][rb][:, ti * 128:(ti + 1) * 128],
                                            rhs=wu_h[rb][:, fc * CH:(fc + 1) * CH],
                                            start=(rb == 0), stop=(rb == 1))
                        return last
                sc.emit("T", kvmm,
                        reads=[("kv1", 0, 0), ("kv1", 0, 1),
                               ("wdst", id(wu_h), 0), ("wdst", id(wu_h), 1),
                               ("onesr_h",), ("bu_rh",)],
                        writes=[("pC", fc % 2)])
                if fc < 2:
                    dst = kvv[:, fc * 8:(fc + 1) * 8, 0:64]
                    sc.emit("S", lambda e, pk=pk, fc=fc: e.activation(
                        out=ebuf[fc % 2][:, :], in_=pk[:, :], func=AF.Exp, bias=0.0, scale=1.0),
                        reads=[("pC", fc % 2)], writes=[("ebuf", fc % 2)])
                    sc.emit("V", lambda e, fc=fc: e.tensor_scalar_min(
                        out=e1buf[fc % 2][:, :], in0=ebuf[fc % 2][:, :], scalar1=1.0),
                        reads=[("ebuf", fc % 2)], writes=[("ebuf", fc % 2)])
                    sc.emit("V", lambda e, pk=pk, dst=dst, fc=fc: e.scalar_tensor_tensor(
                        out=dst, in0=pk[:, :], scalar=0.0, in1=e1buf[fc % 2][:, :],
                        op0=ALU.max, op1=ALU.add),
                        reads=[("pC", fc % 2), ("ebuf", fc % 2), ("kvones", kb_i)],
                        writes=[("kvk", kb_i, fc)])
                else:
                    dst = kvv[:, 16 + (fc - 2) * 8:16 + (fc - 1) * 8, 0:64]
                    sc.emit("V", lambda e, pk=pk, dst=dst, fc=fc: e.tensor_add(
                        out=dst, in0=pk[:, :], in1=burep[fc - 2][:, :]),
                        reads=[("pC", fc % 2), ("burep", fc - 2), ("kvones", kb_i)],
                        writes=[("kvv", kb_i, fc - 2)])

            def ctxmm(e, i=i, kb_i=kb_i):
                last = None
                for h in range(H):
                    first_in_bank = h in (0, 7, 14)
                    last = e.matmul(
                        out=ctx_ps(h),
                        lhsT=kvsb[kb_i][:, h * 65:h * 65 + 64],
                        rhs=kvsb[kb_i][:, 1040 + h * 65:1040 + (h + 1) * 65],
                        start=(i == 0 and first_in_bank), stop=(i == NT - 1),
                        skip_group_check=True)
                return last
            sc.emit("T", ctxmm,
                    reads=[("kvk", kb_i, 0), ("kvk", kb_i, 1),
                           ("kvv", kb_i, 0), ("kvv", kb_i, 1)],
                    writes=[("kvsb_free", kb_i)] + ([("ctx_acc",)] if i == NT - 1 else []))

    w_wave2 = load_weight(wo, wo_h, accum=[rwo[:, k:k + 1] for k in range(NB)])
    w_wave2.append(TH("V", lambda e: e.tensor_copy(out=rwo_h[:, :], in_=rwo[:, :]),
                      reads=[("wdst", id(wo_h), k) for k in range(NB)], writes=[("rwo_h",)]))

    s0 = stats1_head(0, q="SY") + stats1_tail(0) + stats1_head(1, q="SY")
    bg.extend(w_wq)
    while s0 or bg:
        if s0:
            s0.pop(0)()
        inject(1)
    for t in normalize_thunks(0):
        t()
    bg.extend(stats1_tail(1))
    bg.extend(normalize_thunks(1))
    bg.extend(w_wave1)
    for c in range(NCH):
        if c + 2 < NCH:
            bg.extend(stats1_head(c + 2))
        emit_compute1(c, post_q=(w_wave2 if c == NCH - 1 else None))
        if c + 2 < NCH:
            bg.extend(stats1_tail(c + 2))
            bg.extend(normalize_thunks(c + 2))
    drain_bg()


    for h in range(H):
        sc.emit("S", lambda e, h=h: e.activation(out=ctxsb_sl(h),
                                                 in_=ctx_ps(h)[0:64, 0:65],
                                                 func=AF.Copy, bias=0.0, scale=1.0),
                reads=[("ctx_acc",)], writes=[("ctxsb", h)])
    sc.emit("V", lambda e: e.memset(zlhs[:, :], 0.0), writes=[("zlhs_z",)])
    for kb in range(NB):
        for j in range(2):
            h = 2 * kb + j
            sc.emit("V", lambda e, kb=kb, j=j, h=h: e.tensor_copy(
                out=zlhs[j * 64:j * 64 + 64, kb * 16 + h:kb * 16 + h + 1],
                in_=ctxsb_sl(h, 65)[:, 64:65]),
                reads=[("ctxsb", h), ("zlhs_z",)], writes=[("zlhs", h)])

    def a1z_thunks(c):
        th = []

        def zmm(e, c=c):
            last = None
            for kb in range(NB):
                last = e.matmul(out=pmisc[0:16, :], lhsT=zlhs[:, kb * 16:(kb + 1) * 16],
                                rhs=qp[kb][:, cc(c)], start=(kb == 0), stop=(kb == NB - 1),
                                skip_group_check=True)
            return last
        th.append(TH("T", zmm, reads=[("zlhs", h) for h in range(H)] +
                     [("qp", kb, c) for kb in range(NB)], writes=[("pzdot",)]))
        th.append(TH("V", lambda e, c=c: e.tensor_scalar_add(
            out=zdsb[:, :], in0=pmisc[0:16, :], scalar1=1e-6),
            reads=[("pzdot",)], writes=[("zdsb",)]))
        th.append(TH("S", lambda e, c=c: e.activation(out=zdsb[:, :], in_=zdsb[:, :],
                                                      func=AF.Ln, bias=0.0, scale=1.0),
                     reads=[("zdsb",)], writes=[("zdsb",)]))
        th.append(TH("V", lambda e, c=c: e.tensor_scalar_mul(out=zdsb[:, :], in0=zdsb[:, :],
                                                             scalar1=-1.0),
                     reads=[("zdsb",)], writes=[("zdsb",)]))
        th.append(TH("S", lambda e, c=c: e.activation(out=zdsb[:, :], in_=zdsb[:, :],
                                                      func=AF.Exp, bias=0.0, scale=1.0),
                     reads=[("zdsb",)], writes=[("zdsb",)]))
        th.append(TH("V", lambda e: e.tensor_scalar_mul(out=z16[:, :], in0=zdsb[:, :],
                                                        scalar1=1024.0),
                     reads=[("zdsb",)], writes=[("z16",)]))
        th.append(TH("SY", lambda e, c=c: e.dma_start(out=zscr[:, cc(c)], in_=z16[:, :]),
                     reads=[("z16",)], writes=[("zscr", c)], sem="zsc"))
        return th

    def zb_thunks(c):
        th = []
        for p in range(PAIRS):
            th.extend(zrep_bcast(c, p))
        return th

    def zrep_bcast(c, p):
        def za(e, p=p, c=c):
            return e.dma_start(out=zrep[p][0:64, :],
                               in_=zscr[2 * p:2 * p + 1, cc(c)].partition_broadcast(64))

        def zb(e, p=p, c=c):
            return e.dma_start(out=zrep[p][64:128, :],
                               in_=zscr[2 * p + 1:2 * p + 2, cc(c)].partition_broadcast(64))
        return [TH("SY", za, reads=[("zscr", c)],
                   writes=[("zrep", p)], sem=f"zr{p}a"),
                TH("SY", zb, reads=[("zscr", c)],
                   writes=[("zrep2", p)], sem=f"zr{p}b")]

    def a1o_thunks(c):
        th = []
        for p in range(PAIRS):
            pb = pmain[p % 2]

            def outmm(e, p=p, c=c, pb=pb):
                e.matmul(out=pb[0:64, :], lhsT=ctxsb_sl(2 * p, 64),
                         rhs=qp[p][0:64, cc(c)], start=True, stop=True)
                return e.matmul(out=pb[64:128, :], lhsT=ctxsb_sl(2 * p + 1, 64),
                                rhs=qp[p][64:128, cc(c)], start=True, stop=True)
            th.append(TH("T", outmm, reads=[("ctxsb", 2 * p), ("ctxsb", 2 * p + 1), ("qp", p, c)],
                         writes=[("pA", p % 2)]))
            th.append(TH("S", lambda e, p=p, pb=pb: e.activation(
                out=oev[p % 2][:, :], in_=pb[:, :], func=AF.Identity,
                bias=0.0, scale=1.0 / 1024.0),
                reads=[("pA", p % 2), ("cvals",)], writes=[("oev", p % 2)]))
            th.append(TH("GP", lambda e, p=p, c=c: e.tensor_mul(
                out=outz[c % NOZ][p][:, :], in0=oev[p % 2][:, :], in1=zrep[p][:, :]),
                reads=[("oev", p % 2), ("zrep", p), ("zrep2", p)],
                writes=[("outz", c % NOZ, p)], sem="GP", inc=1))
        return th

    def a2_thunks(c):
        ensure(("wdst", id(wo_h), NB - 1))
        th = []
        for db in range(NB):
            pb = pmain[db % 2]

            def womm(e, db=db, c=c, pb=pb):
                last = None
                for kb in range(NB):
                    last = e.matmul(out=pb[:, :], lhsT=wo_h[kb][:, db * 128:(db + 1) * 128],
                                    rhs=outz[c % NOZ][kb][:, :], start=(kb == 0), stop=(kb == NB - 1))
                return last
            th.append(TH("T", womm,
                         reads=[("outz", c % NOZ, kb) for kb in range(NB)] +
                               [("wdst", id(wo_h), kb) for kb in range(NB)],
                         writes=[("pA", db % 2)]))

            def xld(e, db=db, c=c):
                return e.dma_start(out=xr[db % 2][:, :],
                                   in_=xT[db * 128:(db + 1) * 128, cc(c)])
            th.append(TH("SY", xld, writes=[("xr", db % 2)], sem=f"xr{db % 2}"))
            th.append(TH("V", lambda e, pb=pb, db=db, c=c: e.scalar_tensor_tensor(
                out=x2[c % NX2][db][:, :], in0=pb[:, :], scalar=bo_cf[:, db:db + 1],
                in1=xr[db % 2][:, :], op0=ALU.add, op1=ALU.add),
                reads=[("pA", db % 2), ("xr", db % 2), ("bo",)],
                writes=[("x2", c % NX2, db)]))
        return th

    def c1_thunks(c):
        th = []

        def sdmm(e, c=c):
            for kb in range(NB):
                e.matmul(out=pmisc[32:33, :], lhsT=rwo_h[:, kb:kb + 1],
                         rhs=outz[c % NOZ][kb][:, :], start=(kb == 0), stop=False,
                         skip_group_check=True)
            return e.matmul(out=pmisc[32:33, :], lhsT=sbo_h[:, :], rhs=onesr_h[:, :],
                            start=False, stop=True, skip_group_check=True)
        th.append(TH("T", sdmm, reads=[("outz", c % NOZ, kb) for kb in range(NB)] +
                     [("rwo_h",), ("sbo_h",), ("onesr_h",)], writes=[("psum_sx",)]))
        for db in range(NB):
            th.append(TH("S", lambda e, db=db, c=c: e.activation(
                out=xsq2[db % 2][:, :], in_=x2[c % NX2][db][:, :], func=AF.Square,
                bias=0.0, scale=1.0),
                reads=[("x2", c % NX2, db)], writes=[("xsq2", db % 2)]))
            th.append(TH("T", lambda e, db=db: e.matmul(
                out=pmisc[64:65, :], lhsT=ones_h[:, 0:1], rhs=xsq2[db % 2][:, :],
                start=(db == 0), stop=(db == NB - 1), skip_group_check=True),
                reads=[("xsq2", db % 2), ("ones_h",)], writes=[("psum_sq",)]))
        th.append(TH("SY", lambda e, c=c: e.dma_start(out=m1c[:, :], in_=scr[0:1, cc(c)]),
                     reads=[("scr1", c)], writes=[("m1c",)], sem="m1c"))
        th.append(TH("V", lambda e: e.tensor_scalar_mul(out=trow[0][:, :], in0=pmisc[32:33, :],
                                                        scalar1=1.0 / D),
                     reads=[("psum_sx",)], writes=[("trow0",)]))
        th.append(TH("V", lambda e, c=c: e.tensor_add(out=m2row[:, :], in0=m1c[:, :],
                                                      in1=trow[0][:, :]),
                     reads=[("trow0",), ("m1c",)], writes=[("m2row",)]))
        th.append(TH("V", lambda e: e.tensor_scalar_mul(out=trow[0][:, :], in0=pmisc[64:65, :],
                                                        scalar1=1.0 / D),
                     reads=[("psum_sq",)], writes=[("trow0",)]))
        th.append(TH("S", lambda e, c=c: e.activation(out=trow[1][:, :], in_=m2row[:, :],
                                                      func=AF.Square, bias=0.0, scale=1.0),
                     reads=[("m2row",)], writes=[("trow1",)]))
        th.append(TH("V", lambda e: e.scalar_tensor_tensor(out=trow[0][:, :], in0=trow[0][:, :],
                                                           scalar=1e-5, in1=trow[1][:, :],
                                                           op0=ALU.add, op1=ALU.subtract),
                     reads=[("trow0",), ("trow1",)], writes=[("trow0",)]))
        th.append(TH("S", lambda e: e.activation(out=trow[1][:, :], in_=trow[0][:, :],
                                                 func=AF.Ln, bias=0.0, scale=1.0),
                     reads=[("trow0",)], writes=[("trow1",)]))
        th.append(TH("S", lambda e, c=c: e.activation(out=i2row[:, :], in_=trow[1][:, :],
                                                      func=AF.Exp, bias=0.0, scale=-0.5),
                     reads=[("trow1",), ("cvals",)], writes=[("i2row",)]))

        def bnc(e, c=c):
            return e.dma_start(out=scr[1:2, :].rearrange("a (g t) -> a g t", g=2)[:, :, cc(c)],
                               in_=rows2[0:1, :].rearrange("a (g t) -> a g t", g=2))
        th.append(TH("SY", bnc, reads=[("m2row",), ("i2row",)], writes=[("scr2", c)], sem="sc2"))

        def rp1(e, c=c):
            return e.dma_start(out=m2rep[c % NRP][:, :],
                               in_=scr[1:2, cc(c)].partition_broadcast(128))
        th.append(TH("SY", rp1, reads=[("scr2", c)], writes=[("m1rep", c % NRP)], sem=f"rpa{c % NRP}"))

        def rp2(e, c=c):
            return e.dma_start(out=i2rep[c % NRP][:, :],
                               in_=scr[1:2, T + c * CH:T + (c + 1) * CH].partition_broadcast(128))
        th.append(TH("SY", rp2, reads=[("scr2", c)], writes=[("i1rep", c % NRP)], sem=f"rpb{c % NRP}"))
        return th

    def c2a_thunks(c):
        ensure(("wdst", id(wf8), NB - 1))
        th = []
        for db in range(NB):
            th.append(TH("V", lambda e, db=db, c=c: e.tensor_sub(
                out=xc2[db % NXC][:, :], in0=x2[c % NX2][db][:, :], in1=m2rep[c % NRP][:, :]),
                reads=[("x2", c % NX2, db), ("m1rep", c % NRP)], writes=[("xc1", db % NXC)]))
            th.append(TH("V", lambda e, db=db, c=c: e.tensor_mul(
                out=h2q[c % NH2][db][:, :], in0=xc2[db % NXC][:, :], in1=i2rep[c % NRP][:, :]),
                reads=[("xc1", db % NXC), ("i1rep", c % NRP)], writes=[("h2q", c % NH2, db)]))
            th.append(TH("T", lambda e, kb=db, c=c: e.matmul(
                out=pkv[0][:, :], lhsT=wf8[kb][:, 0:128],
                rhs=h2q[c % NH2][kb][:, :], start=(kb == 0), stop=(kb == NB - 1)),
                reads=[("h2q", c % NH2, db), ("wdst", id(wf8), db)],
                writes=[("pC", 0)]))
        return th

    def c2b_thunks(c):
        th = []
        for fb in range(NB):
            pb = pkv[fb % 2]
            if fb > 0:
                def ffnmm(e, fb=fb, c=c, pb=pb):
                    last = None
                    for kb in range(NB):
                        last = e.matmul(out=pb[:, :], lhsT=wf8[kb][:, fb * 128:(fb + 1) * 128],
                                        rhs=h2q[c % NH2][kb][:, :], start=(kb == 0), stop=(kb == NB - 1))
                    return last
                th.append(TH("T", ffnmm,
                             reads=[("h2q", c % NH2, kb) for kb in range(NB)] +
                                   [("wdst", id(wf8), kb) for kb in range(NB)],
                             writes=[("pC", fb % 2)]))
            th.append(TH("V", lambda e, pb=pb, fb=fb, c=c: e.scalar_tensor_tensor(
                out=x2[c % NX2][fb][:, :], in0=pb[:, :], scalar=bf_cf[:, fb:fb + 1],
                in1=x2[c % NX2][fb][:, :], op0=ALU.add, op1=ALU.add),
                reads=[("pC", fb % 2)], writes=[("x2", c % NX2, fb)]))

            def yst(e, fb=fb, c=c):
                return e.dma_start(out=yT[fb * 128:(fb + 1) * 128, cc(c)],
                                   in_=x2[c % NX2][fb][:, :])
            th.append(TH("GP", yst, reads=[("x2", c % NX2, fb)],
                         writes=[("ydma", c, fb)], sem=f"yd{fb}_{c % 2}"))
        return th

    def run_all(th):
        for t in th:
            t()

    def interleave(l1, l2):
        i = j = 0
        while i < len(l1) or j < len(l2):
            if i < len(l1):
                l1[i]()
                i += 1
            if j < len(l2):
                l2[j]()
                j += 1

    run_all(a1z_thunks(0))
    run_all(zb_thunks(0))
    interleave(a1z_thunks(1) if NCH > 1 else [], a1o_thunks(0))
    run_all(a2_thunks(0))
    for c in range(NCH):
        ahead = (a1z_thunks(c + 2) if c + 2 < NCH else []) + \
                (zb_thunks(c + 1) if c + 1 < NCH else [])
        interleave(ahead, c1_thunks(c))
        a1o = a1o_thunks(c + 1) if c + 1 < NCH else []
        interleave(c2a_thunks(c), a1o)
        run_all(c2b_thunks(c))
        if c + 1 < NCH:
            run_all(a2_thunks(c + 1))

    sems = {}
    with ExitStack() as ses:
        for name in sc.sem_names:
            sems[name] = ses.enter_context(nc.semaphore(f"sem{name}"))
        block = ses.enter_context(nc.Block())

        @block.tensor
        def _(eng):
            sc.replay("T", eng, sems)

        @block.vector
        def _(eng):
            sc.replay("V", eng, sems)

        @block.scalar
        def _(eng):
            sc.replay("S", eng, sems)

        @block.sync
        def _(eng):
            sc.replay("SY", eng, sems)

        @block.gpsimd
        def _(eng):
            sc.replay("GP", eng, sems)
            for fb in range(NB):
                for par in range(2):
                    eng.wait_ge(sems[f"yd{fb}_{par}"], sc.count[f"yd{fb}_{par}"])
    es.close()
    return nc


_NC_CACHE = None


def kernel(L0_attn_x, L0_attn_Wq, L0_attn_bq, L0_attn_Wd, L0_attn_bd,
           L0_attn_Wu, L0_attn_bu, L0_attn_Wo, L0_attn_bo,
           L0_ffn_weight, L0_ffn_bias):
    global _NC_CACHE
    x = np.asarray(L0_attn_x, dtype=np.float32)
    B = x.shape[0]
    f32 = lambda a: np.ascontiguousarray(np.asarray(a, dtype=np.float32))
    kidx = np.concatenate([np.arange(h * 128, h * 128 + 64) for h in range(H)])
    vidx = np.concatenate([np.arange(h * 128 + 64, h * 128 + 128) for h in range(H)])
    perm = np.concatenate([kidx, vidx])
    common = {
        "wq": f32(L0_attn_Wq),
        "wd": f32(L0_attn_Wd),
        "wu": f32(np.asarray(L0_attn_Wu, dtype=np.float32)[:, perm]),
        "wo": f32(L0_attn_Wo),
        "wf": f32(np.asarray(L0_ffn_weight, dtype=np.float32).T),
        "bq_r": f32(np.asarray(L0_attn_bq, dtype=np.float32).reshape(1, D)),
        "bd_c": f32(np.asarray(L0_attn_bd, dtype=np.float32).reshape(2, 128).T),
        "bu_r": f32(np.asarray(L0_attn_bu, dtype=np.float32)[perm].reshape(1, 2 * D)),
        "bo_c": f32(np.asarray(L0_attn_bo, dtype=np.float32).reshape(NB, 128).T),
        "bf_c": f32(np.asarray(L0_ffn_bias, dtype=np.float32).reshape(NB, 128).T),
        "ones_in": np.ones((128, 128), dtype=np.float32),
        "onesr_in": np.ones((1, CH), dtype=np.float32),
        "cvals_in": np.array([[-0.5, 1.0 / D, 1e-5, -1.0, 1.0 / 1024.0]], dtype=np.float32),
        "ident_in": np.eye(128, dtype=np.float32),
    }
    in_maps = []
    for b in range(B):
        m = dict(common)
        m["xT"] = np.ascontiguousarray(x[b].T)
        m["xtok_d"] = np.ascontiguousarray(x[b])
        in_maps.append(m)
    if _NC_CACHE is None:
        _NC_CACHE = build_nc()
    res = run_bass_kernel_spmd(_NC_CACHE, in_maps, core_ids=list(range(B)))
    out = np.stack([np.ascontiguousarray(r["yT"].T) for r in res.results], axis=0)
    return out


# version 20
# speedup vs baseline: 1.0206x; 1.0121x over previous
import numpy as np
from contextlib import ExitStack
import concourse.bass as bass
import concourse.mybir as mybir
from concourse.bass_utils import run_bass_kernel_spmd

F32 = mybir.dt.float32
F16 = mybir.dt.float16
FP8 = mybir.dt.float8e4
AF = mybir.ActivationFunctionType
ALU = mybir.AluOpType

T, D, H, DH, RK = 2048, 1024, 16, 64, 256
CH = 512
NCH = T // CH
NB = D // 128
NT = T // 128
PAIRS = H // 2
NXCH, NXS, NX2, NOZ, NH2, NRP, NXC = 1, 2, 2, 1, 1, 2, 1


class Sched:

    ENGS = ("T", "V", "S", "SY", "GP")
    ONE_WAVE = ("cst", "cstg")

    def __init__(self):
        self.steps = {e: [] for e in self.ENGS}
        self.count = {}
        self.writer = {}
        self.readers = {}
        self.watermark = {e: {} for e in self.ENGS}
        self.sem_names = []

    def _bump(self, sem, inc):
        if sem not in self.count:
            self.count[sem] = 0
            self.sem_names.append(sem)
        self.count[sem] += inc
        return self.count[sem]

    def emit(self, eng, fn, reads=(), writes=(), sem=None, inc=None):
        if eng in ("SY", "GP") and inc != 1:
            assert sem is not None, "DMA steps need a sem stream name"
            inc = 16
        else:
            sem = sem or eng
            inc = 1
        waits = {}
        def add(w):
            if w is None:
                return
            ws, wv = w
            if ws == "T" and eng == "T":
                return
            waits[ws] = max(waits.get(ws, 0), wv)
        for t in reads:
            add(self.writer.get(t))
        for t in writes:
            add(self.writer.get(t))
            for r in self.readers.get(t, ()):
                add(r)
        myval = self._bump(sem, inc)
        final_waits = []
        for ws, wv in sorted(waits.items()):
            if ws == sem and eng not in ("SY", "GP"):
                pass
            if self.watermark[eng].get(ws, 0) < wv:
                final_waits.append((ws, wv))
                self.watermark[eng][ws] = wv
        self.steps[eng].append((final_waits, fn, sem, inc))
        for t in writes:
            self.writer[t] = (sem, myval)
            self.readers[t] = []
        for t in reads:
            self.readers.setdefault(t, []).append((sem, myval))

    def replay(self, eng_name, eng_obj, sems):
        for (waits, fn, sem, inc) in self.steps[eng_name]:
            for (ws, wv) in waits:
                if ws in self.ONE_WAVE:
                    wv = self.count[ws]
                eng_obj.wait_ge(sems[ws], wv)
            last = fn(eng_obj)
            assert last is not None, "step fn must return its last instruction"
            last.then_inc(sems[sem], inc)


def build_nc():
    nc = bass.Bass("TRN2", target_bir_lowering=False)
    dt = nc.dram_tensor
    xT = dt("xT", [D, T], F32, kind="ExternalInput").ap()
    xtok_d = dt("xtok_d", [T, D], F32, kind="ExternalInput").ap()
    ident_in = dt("ident_in", [128, 128], F32, kind="ExternalInput").ap()
    wq = dt("wq", [D, D], F32, kind="ExternalInput").ap()
    wd = dt("wd", [D, RK], F32, kind="ExternalInput").ap()
    wu = dt("wu", [RK, 2 * D], F32, kind="ExternalInput").ap()
    wo = dt("wo", [D, D], F32, kind="ExternalInput").ap()
    wf = dt("wf", [D, D], F32, kind="ExternalInput").ap()
    bq_r = dt("bq_r", [1, D], F32, kind="ExternalInput").ap()
    bd_c = dt("bd_c", [128, 2], F32, kind="ExternalInput").ap()
    bu_r = dt("bu_r", [1, 2 * D], F32, kind="ExternalInput").ap()
    bo_c = dt("bo_c", [128, NB], F32, kind="ExternalInput").ap()
    bf_c = dt("bf_c", [128, NB], F32, kind="ExternalInput").ap()
    ones_in = dt("ones_in", [128, 128], F32, kind="ExternalInput").ap()
    onesr_in = dt("onesr_in", [1, CH], F32, kind="ExternalInput").ap()
    cvals_in = dt("cvals_in", [1, 5], F32, kind="ExternalInput").ap()
    scr = dt("scr", [2, 2 * T], F32, kind="Internal").ap()
    zscr = dt("zscr", [H, T], F16, kind="Internal").ap()
    buscr = dt("buscr", [1, 2 * D], F16, kind="Internal").ap()
    yT = dt("yT", [D, T], F32, kind="ExternalOutput").ap()

    es = ExitStack()
    sb = lambda name, shape, d: es.enter_context(nc.sbuf_tensor(name, shape, d))

    wstage = [sb(f"wstage{i}", [128, D], F32) for i in range(2)]
    wq_h = [sb(f"wq_h{k}", [128, D], F16) for k in range(NB)]
    wd_h = [sb(f"wd_h{k}", [128, RK], F16) for k in range(NB)]
    wu_h = [sb(f"wu_h{k}", [128, 2 * D], F16) for k in range(2)]
    wo_h = wq_h
    wf8 = [sb(f"wf8_{k}", [128, D], FP8) for k in range(NB)]
    rwo = sb("rwo", [128, NB], F32)
    rwo_h = sb("rwo_h", [128, NB], F16)
    bq_rf = sb("bq_rf", [1, D], F32)
    bq_rh = sb("bq_rh", [1, D], F16)
    bu_rf = sb("bu_rf", [1, 2 * D], F32)
    bu_rh = sb("bu_rh", [1, 2 * D], F16)
    bd_cf = sb("bd_cf", [128, 2], F32)
    bo_cf = sb("bo_cf", [128, NB], F32)
    bf_cf = sb("bf_cf", [128, NB], F32)
    cvals = sb("cvals", [1, 5], F32)
    ones_f = sb("ones_f", [128, 128], F32)
    ones_h = sb("ones_h", [128, 128], F16)
    onesr_f = sb("onesr_f", [1, CH], F32)
    onesr_h = sb("onesr_h", [1, CH], F16)
    burep = [sb(f"burep{i}", [128, CH], F16) for i in range(2)]

    xtok = [sb(f"xtok{i}", [128, D], F32) for i in range(2)]
    scol = [sb(f"scol{i}", [128, 8], F32) for i in range(2)]
    pst8 = sb("pst8", [8, 128], F32)
    msq4t = sb("msq4t", [128, 4], F32)
    ident = sb("ident", [128, 128], F32)
    m1rep = [sb(f"m1rep{i}", [128, CH], F32) for i in range(NRP)]
    i1rep = [sb(f"i1rep{i}", [128, CH], F32) for i in range(NRP)]
    m2rep = m1rep
    i2rep = i1rep
    xc1 = [sb(f"xc1_{i}", [128, CH], F32) for i in range(NXC)]
    xs1 = [[sb(f"xs1_{i}_{b}", [128, CH], F16) for b in range(NB)] for i in range(NXS)]
    qp = [sb(f"qp{b}", [128, T], F16) for b in range(NB)]
    ebuf = [sb(f"ebuf{i}", [128, CH], F16) for i in range(2)]
    e1buf = ebuf
    kv1 = [[sb(f"kv1_{i}_{r}", [128, CH], F16) for r in range(2)] for i in range(1)]
    kvsb = [sb(f"kvsb{i}", [128, 2 * 1040], F16) for i in range(2)]
    ctxsb = sb("ctxsb", [128, PAIRS * 65], F16)
    bo16 = sb("bo16", [128, NB], F16)
    sbo = sb("sbo", [1, 1], F32)
    sbo_h = sb("sbo_h", [1, 1], F16)
    zlhs = sb("zlhs", [128, NB * 16], F16)
    zdsb = sb("zdsb", [16, CH], F32)
    z16 = sb("z16", [16, CH], F16)
    oev = [sb(f"oev{i}", [128, CH], F32) for i in range(2)]
    rows2 = sb("rows2", [1, 2 * CH], F32)
    m2row = rows2[0:1, 0:CH]
    i2row = rows2[0:1, CH:2 * CH]
    m1c = sb("m1c", [1, CH], F32)
    trow = [sb(f"trow{i}", [1, CH], F32) for i in range(2)]
    trow = trow + trow
    zrep = [sb(f"zrep{i}", [128, CH], F16) for i in range(8)]
    zrv = []
    for nm, hostt in (("zrv0", wu_h[0]), ("zrv1", wu_h[1])):
        vt = nc.virtual_sbuf_tensor(nm, [128, 4 * CH], F16)
        nc.back_virtual_tensor_with(vt, hostt)
        zrv.append(vt)
    zreps = [zrep, [zrv[p // 4][:, (p % 4) * CH:(p % 4 + 1) * CH] for p in range(8)]]
    outz = [[sb(f"outz{i}_{b}", [128, CH], F16) for b in range(NB)] for i in range(NOZ)]
    xr = [sb(f"xr{i}", [128, CH], F32) for i in range(2)]
    x2 = [[sb(f"x2_0_{b}", [128, CH], F32) for b in range(NB)]]
    x2v = []
    for nm, hostt in (("x2v0", wstage[0]), ("x2v1", wstage[1]),
                      ("x2v2", xtok[0]), ("x2v3", xtok[1])):
        vt = nc.virtual_sbuf_tensor(nm, [128, 2 * CH], F32)
        nc.back_virtual_tensor_with(vt, hostt)
        x2v.append(vt)
    x2.append([x2v[b // 2][:, (b % 2) * CH:(b % 2 + 1) * CH] for b in range(NB)])
    xsq2 = [sb(f"xsq2_{i}", [128, CH], F16) for i in range(2)]
    xc2 = xc1
    h2q = [[sb(f"h2q{i}_{b}", [128, CH], FP8) for b in range(NB)] for i in range(NH2)]

    psum = lambda name, shape: es.enter_context(nc.psum_tensor(name, shape, F32))
    pmain = [psum("pm0", [128, CH]), psum("pm1", [128, CH])]
    pctx = [psum("pc0", [64, 7 * 65]), psum("pc1", [64, 7 * 65]),
            psum("pc2", [64, 2 * 65])]
    pmisc = psum("pmisc", [128, CH])
    pkv = [psum("pk0", [128, CH]), psum("pk1", [128, CH])]
    pst = pmisc[0:8, 0:128]

    def ctx_ps(h):
        if h < 7:
            return pctx[0][0:64, h * 65:(h + 1) * 65]
        if h < 14:
            return pctx[1][0:64, (h - 7) * 65:(h - 6) * 65]
        return pctx[2][0:64, (h - 14) * 65:(h - 13) * 65]

    def ctxsb_sl(h, ncol=65):
        r0 = (h % 2) * 64
        c0 = (h // 2) * 65
        return ctxsb[r0:r0 + 64, c0:c0 + ncol]

    sc = Sched()
    cc = lambda c: slice(c * CH, (c + 1) * CH)
    cc2 = lambda c: slice(c * 128, (c + 1) * 128)
    import collections as _co
    bg = _co.deque()

    def TH(eng, fn, reads=(), writes=(), sem=None, inc=None):
        return lambda: sc.emit(eng, fn, reads=reads, writes=writes, sem=sem, inc=inc)

    def inject(n=1):
        for _ in range(n):
            if not bg:
                return
            bg.popleft()()

    def drain_bg():
        while bg:
            bg.popleft()()

    def ensure(tok):
        while tok not in sc.writer and bg:
            bg.popleft()()

    def dma(q, key, out, in_, reads=(), writes=()):
        sc.emit(q, lambda e, o=out, i=in_: e.dma_start(out=o, in_=i),
                reads=reads, writes=writes, sem=key)

    dma("SY", "cst", cvals[:, :], cvals_in[:, :], writes=[("cvals",)])
    nc.const_aps.register(F32, -0.5, cvals[0:1, 0:1])
    nc.const_aps.register(F32, 1.0 / 1024.0, cvals[0:1, 4:5])
    dma("SY", "cst", ones_f[:, :], ones_in[:, :], writes=[("ones_f",)])
    dma("SY", "cst", ident[:, :], ident_in[:, :], writes=[("ident",)])
    dma("SY", "cst", onesr_f[:, :], onesr_in[:, :], writes=[("onesr_f",)])
    dma("SY", "cst", bq_rf[:, :], bq_r[:, :], writes=[("bq_rf",)])
    dma("SY", "cst", bd_cf[:, :], bd_c[:, :], writes=[("bd",)])
    dma("SY", "cst", bu_rf[:, :], bu_r[:, :], writes=[("bu_rf",)])
    dma("SY", "cst", bo_cf[:, :], bo_c[:, :], writes=[("bo",)])
    dma("SY", "cst", bf_cf[:, :], bf_c[:, :], writes=[("bf",)])

    sc.emit("V", lambda e: e.tensor_copy(out=ones_h[:, :], in_=ones_f[:, :]),
            reads=[("ones_f",)], writes=[("ones_h",)])
    sc.emit("V", lambda e: e.tensor_copy(out=onesr_h[:, :], in_=onesr_f[:, :]),
            reads=[("onesr_f",)], writes=[("onesr_h",)])
    sc.emit("V", lambda e: e.tensor_copy(out=bq_rh[:, :], in_=bq_rf[:, :]),
            reads=[("bq_rf",)], writes=[("bq_rh",)])
    sc.emit("V", lambda e: e.tensor_copy(out=bu_rh[:, :], in_=bu_rf[:, :]),
            reads=[("bu_rf",)], writes=[("bu_rh",)])
    dma("GP", "busc", buscr[0:1, :], bu_rh[0:1, :], reads=[("bu_rh",)], writes=[("buscr",)])
    dma("GP", "cstg", burep[0][:, :], buscr[0:1, D:D + CH].partition_broadcast(128),
        reads=[("buscr",)], writes=[("burep", 0)])
    dma("GP", "cstg", burep[1][:, :], buscr[0:1, D + CH:2 * D].partition_broadcast(128),
        reads=[("buscr",)], writes=[("burep", 1)])

    wi = [0]

    def load_weight(src_ap, dst_list, accum=None):
        th = []
        nblk = src_ap.shape[0] // 128
        cols = src_ap.shape[1]
        nh = (cols + D - 1) // D
        for k in range(nblk):
            for hh in range(nh):
                c0, c1 = hh * D, min((hh + 1) * D, cols)
                sidx = wi[0] % 2
                wi[0] += 1
                st = wstage[sidx]

                def dth(e, s=st, c0=c0, c1=c1, k=k, sa=src_ap):
                    return e.dma_start(out=s[:, 0:c1 - c0],
                                       in_=sa[k * 128:(k + 1) * 128, c0:c1])
                th.append(TH("SY", dth, reads=(), writes=[("wstage", sidx)],
                             sem=f"wst{sidx}"))
                acc = accum[k] if (accum is not None and hh == 0) else None

                def cast(e, s=st, d=dst_list[k], a=acc, c0=c0, c1=c1):
                    return e.activation(out=d[:, c0:c1], in_=s[:, 0:c1 - c0],
                                        func=AF.Copy, bias=0.0, scale=1.0, accum_out=a)
                th.append(TH("S", cast, reads=[("wstage", sidx)],
                             writes=[("wdst", id(dst_list), k)] if hh == nh - 1 else [("wdsth", id(dst_list), k)]))
        return th

    w_wq = load_weight(wq, wq_h)
    w_wave1 = (load_weight(wd, wd_h) + load_weight(wu, wu_h) + load_weight(wf, wf8))
    sc.emit("V", lambda e: e.tensor_copy(out=bo16[:, :], in_=bo_cf[:, :]),
            reads=[("bo",)], writes=[("bo16",)])
    sc.emit("T", lambda e: e.matmul(out=pmisc[32:33, 0:NB], lhsT=ones_h[:, 0:1],
                                    rhs=bo16[:, :], start=True, stop=True,
                                    skip_group_check=True),
            reads=[("bo16",), ("ones_h",)], writes=[("psum_sx",)])
    sc.emit("V", lambda e: e.reduce_sum(out=sbo[:, :], in_=pmisc[32:33, 0:NB],
                                        axis=mybir.AxisListType.X),
            reads=[("psum_sx",)], writes=[("sbo",)])
    sc.emit("V", lambda e: e.tensor_copy(out=sbo_h[:, :], in_=sbo[:, :]),
            reads=[("sbo",)], writes=[("sbo_h",)])
    for i in range(2):
        sc.emit("V", lambda e, i=i: e.memset(
            kvsb[i][:].rearrange("p (g n) -> p g n", g=32, n=65)[:, :, 64:65], 1.0),
            writes=[("kvones", i)])

    def stats1_head(c, q="GP"):
        th = []
        sc_ = scol[c % 2]
        for ti in range(4):
            def ld(e, ti=ti, c=c):
                return e.dma_start(out=xtok[ti % 2][:, :],
                                   in_=xtok_d[c * CH + ti * 128:c * CH + (ti + 1) * 128, :])
            pfx = "xtk" if q == "GP" else "xtsy"
            th.append(TH(q, ld, writes=[("xtok", ti % 2)], sem=f"{pfx}{ti % 2}"))
            th.append(TH("S", lambda e, ti=ti, sc_=sc_: e.activation(
                out=xtok[ti % 2][:, :], in_=xtok[ti % 2][:, :], func=AF.Copy,
                bias=0.0, scale=1.0, accum_out=sc_[:, ti:ti + 1]),
                reads=[("xtok", ti % 2)], writes=[("scolm", c % 2, ti), ("xtok", ti % 2)]))
            th.append(TH("S", lambda e, ti=ti, sc_=sc_: e.activation(
                out=xtok[ti % 2][:, :], in_=xtok[ti % 2][:, :], func=AF.Square,
                bias=0.0, scale=1.0, accum_out=sc_[:, 4 + ti:5 + ti]),
                reads=[("xtok", ti % 2)], writes=[("scolq", c % 2, ti), ("xtok", ti % 2)]))
        return th

    def stats1_tail(c):
        th = []
        sc_ = scol[c % 2]
        th.append(TH("V", lambda e, sc_=sc_: e.tensor_scalar_mul(out=sc_[:, 0:8], in0=sc_[:, 0:8],
                                                                scalar1=1.0 / D),
                     reads=[("scolm", c % 2, 0), ("scolm", c % 2, 1), ("scolm", c % 2, 2),
                            ("scolm", c % 2, 3), ("scolq", c % 2, 0), ("scolq", c % 2, 1),
                            ("scolq", c % 2, 2), ("scolq", c % 2, 3)],
                     writes=[("scol", c % 2)]))
        th.append(TH("S", lambda e, sc_=sc_: e.activation(out=msq4t[:, :], in_=sc_[:, 0:4],
                                                          func=AF.Square, bias=0.0, scale=1.0),
                     reads=[("scol", c % 2)], writes=[("msq4",)]))
        th.append(TH("V", lambda e, sc_=sc_: e.scalar_tensor_tensor(out=sc_[:, 4:8], in0=sc_[:, 4:8],
                                                                    scalar=1e-5, in1=msq4t[:, :],
                                                                    op0=ALU.add, op1=ALU.subtract),
                     reads=[("scol", c % 2), ("msq4",)], writes=[("scol", c % 2)]))
        th.append(TH("S", lambda e, sc_=sc_: e.activation(out=sc_[:, 4:8], in_=sc_[:, 4:8],
                                                          func=AF.Ln, bias=0.0, scale=1.0),
                     reads=[("scol", c % 2)], writes=[("scol", c % 2)]))
        th.append(TH("S", lambda e, sc_=sc_: e.activation(out=sc_[:, 4:8], in_=sc_[:, 4:8],
                                                          func=AF.Exp, bias=0.0, scale=-0.5),
                     reads=[("scol", c % 2), ("cvals",)], writes=[("scol", c % 2)]))
        th.append(TH("T", lambda e, sc_=sc_: e.transpose(out=pst, in_=sc_[:, :],
                                                         identity=ident[:, :]),
                     reads=[("scol", c % 2), ("ident",)], writes=[("pzdot",)]))
        th.append(TH("V", lambda e: e.tensor_copy(out=pst8[:, :], in_=pst),
                     reads=[("pzdot",)], writes=[("pst8",)]))

        def bnc(e, c=c):
            dst = scr[0:1, :].rearrange("a (g ch u t) -> a g ch u t",
                                        g=2, ch=NCH, u=4)[:, :, c, :, :]
            return e.dma_start(out=dst, in_=pst8[:, :])
        th.append(TH("SY", bnc, reads=[("pst8",)], writes=[("scr1", c)], sem="sc1"))

        def rp1(e, c=c):
            return e.dma_start(out=m1rep[c % NRP][:, :],
                               in_=scr[0:1, cc(c)].partition_broadcast(128))
        th.append(TH("SY", rp1, reads=[("scr1", c)], writes=[("m1rep", c % NRP)], sem=f"rpa{c % NRP}"))

        def rp2(e, c=c):
            return e.dma_start(out=i1rep[c % NRP][:, :],
                               in_=scr[0:1, T + c * CH:T + (c + 1) * CH].partition_broadcast(128))
        th.append(TH("SY", rp2, reads=[("scr1", c)], writes=[("i1rep", c % NRP)], sem=f"rpb{c % NRP}"))
        return th

    def normalize_thunks(c):
        th = []
        for b in range(NB):
            def ld(e, b=b, c=c):
                return e.dma_start(out=xr[b % 2][:, :],
                                   in_=xT[b * 128:(b + 1) * 128, cc(c)])
            th.append(TH("SY", ld, writes=[("xr", b % 2)], sem=f"xr{b % 2}"))
            th.append(TH("V", lambda e, b=b, c=c: e.tensor_sub(
                out=xc1[b % NXC][:, :], in0=xr[b % 2][:, :], in1=m1rep[c % NRP][:, :]),
                reads=[("xr", b % 2), ("m1rep", c % NRP)], writes=[("xc1", b % NXC)]))
            th.append(TH("V", lambda e, b=b, c=c: e.tensor_mul(
                out=xs1[c % NXS][b][:, :], in0=xc1[b % NXC][:, :], in1=i1rep[c % NRP][:, :]),
                reads=[("xc1", b % NXC), ("i1rep", c % NRP)], writes=[("xs1", c % NXS, b)]))
        return th

    def emit_compute1(c, post_q=None):
        for _ in range(4):
            inject(2)
        for db in range(NB):
            inject(2)
            ensure(("wdst", id(wq_h), NB - 1))
            pb = pmain[db % 2]

            def qmm(e, db=db, c=c, pb=pb):
                for kb in range(NB):
                    e.matmul(out=pb[:, :], lhsT=wq_h[kb][:, db * 128:(db + 1) * 128],
                             rhs=xs1[c % NXS][kb][:, :], start=(kb == 0), stop=False)
                return e.matmul(out=pb[:, :], lhsT=bq_rh[0:1, db * 128:(db + 1) * 128],
                                rhs=onesr_h[:, :], start=False, stop=True)
            sc.emit("T", qmm,
                    reads=[("xs1", c % NXS, kb) for kb in range(NB)] +
                          [("wdst", id(wq_h), kb) for kb in range(NB)] +
                          [("bq_rh",), ("onesr_h",)],
                    writes=[("pA", db % 2)])
            sc.emit("S", lambda e, pb=pb, db=db: e.activation(
                out=ebuf[db % 2][:, :], in_=pb[:, :], func=AF.Exp, bias=0.0, scale=1.0),
                reads=[("pA", db % 2)], writes=[("ebuf", db % 2)])
            sc.emit("V", lambda e, db=db: e.tensor_scalar_min(
                out=e1buf[db % 2][:, :], in0=ebuf[db % 2][:, :], scalar1=1.0),
                reads=[("ebuf", db % 2)], writes=[("ebuf", db % 2)])
            sc.emit("V", lambda e, pb=pb, db=db, c=c: e.scalar_tensor_tensor(
                out=qp[db][:, cc(c)], in0=pb[:, :], scalar=0.0, in1=e1buf[db % 2][:, :],
                op0=ALU.max, op1=ALU.add),
                reads=[("pA", db % 2), ("ebuf", db % 2)], writes=[("qp", db, c)])
        if post_q:
            bg.extend(post_q)
        ensure(("wdst", id(wd_h), NB - 1))
        for rb in range(2):
            pb = pmain[rb % 2]

            def kv1mm(e, rb=rb, c=c, pb=pb):
                last = None
                for kb in range(NB):
                    last = e.matmul(out=pb[:, :], lhsT=wd_h[kb][:, rb * 128:(rb + 1) * 128],
                                    rhs=xs1[c % NXS][kb][:, :], start=(kb == 0), stop=(kb == NB - 1))
                return last
            sc.emit("T", kv1mm,
                    reads=[("xs1", c % NXS, kb) for kb in range(NB)] +
                          [("wdst", id(wd_h), kb) for kb in range(NB)],
                    writes=[("pA", rb % 2)])
            sc.emit("V", lambda e, pb=pb, rb=rb, c=c: e.tensor_scalar(
                out=kv1[0][rb][:, :], in0=pb[:, :], scalar1=bd_cf[:, rb:rb + 1],
                scalar2=None, op0=ALU.add),
                reads=[("pA", rb % 2), ("bd",)], writes=[("kv1", 0, rb)])
        ensure(("wdst", id(wu_h), 1))
        for ti in range(4):
            i = c * 4 + ti
            kb_i = i % 2
            kvv = kvsb[kb_i][:].rearrange("p (g n) -> p g n", g=32, n=65)
            for fc in range(4):
                inject(2)
                pk = pkv[fc % 2]
                if fc < 2:
                    def kvmm(e, ti=ti, c=c, fc=fc, pk=pk):
                        for rb in range(2):
                            e.matmul(out=pk[:, :], lhsT=kv1[0][rb][:, ti * 128:(ti + 1) * 128],
                                     rhs=wu_h[rb][:, fc * CH:(fc + 1) * CH], start=(rb == 0), stop=False)
                        return e.matmul(out=pk[:, :], lhsT=onesr_h[0:1, 0:128],
                                        rhs=bu_rh[0:1, fc * CH:(fc + 1) * CH], start=False, stop=True)
                else:
                    def kvmm(e, ti=ti, c=c, fc=fc, pk=pk):
                        last = None
                        for rb in range(2):
                            last = e.matmul(out=pk[:, :], lhsT=kv1[0][rb][:, ti * 128:(ti + 1) * 128],
                                            rhs=wu_h[rb][:, fc * CH:(fc + 1) * CH],
                                            start=(rb == 0), stop=(rb == 1))
                        return last
                sc.emit("T", kvmm,
                        reads=[("kv1", 0, 0), ("kv1", 0, 1),
                               ("wdst", id(wu_h), 0), ("wdst", id(wu_h), 1),
                               ("onesr_h",), ("bu_rh",)],
                        writes=[("pC", fc % 2)])
                if fc < 2:
                    dst = kvv[:, fc * 8:(fc + 1) * 8, 0:64]
                    sc.emit("S", lambda e, pk=pk, fc=fc: e.activation(
                        out=ebuf[fc % 2][:, :], in_=pk[:, :], func=AF.Exp, bias=0.0, scale=1.0),
                        reads=[("pC", fc % 2)], writes=[("ebuf", fc % 2)])
                    sc.emit("V", lambda e, fc=fc: e.tensor_scalar_min(
                        out=e1buf[fc % 2][:, :], in0=ebuf[fc % 2][:, :], scalar1=1.0),
                        reads=[("ebuf", fc % 2)], writes=[("ebuf", fc % 2)])
                    sc.emit("V", lambda e, pk=pk, dst=dst, fc=fc: e.scalar_tensor_tensor(
                        out=dst, in0=pk[:, :], scalar=0.0, in1=e1buf[fc % 2][:, :],
                        op0=ALU.max, op1=ALU.add),
                        reads=[("pC", fc % 2), ("ebuf", fc % 2), ("kvones", kb_i)],
                        writes=[("kvk", kb_i, fc)])
                else:
                    dst = kvv[:, 16 + (fc - 2) * 8:16 + (fc - 1) * 8, 0:64]
                    sc.emit("V", lambda e, pk=pk, dst=dst, fc=fc: e.tensor_add(
                        out=dst, in0=pk[:, :], in1=burep[fc - 2][:, :]),
                        reads=[("pC", fc % 2), ("burep", fc - 2), ("kvones", kb_i)],
                        writes=[("kvv", kb_i, fc - 2)])

            def ctxmm(e, i=i, kb_i=kb_i):
                last = None
                for h in range(H):
                    first_in_bank = h in (0, 7, 14)
                    last = e.matmul(
                        out=ctx_ps(h),
                        lhsT=kvsb[kb_i][:, h * 65:h * 65 + 64],
                        rhs=kvsb[kb_i][:, 1040 + h * 65:1040 + (h + 1) * 65],
                        start=(i == 0 and first_in_bank), stop=(i == NT - 1),
                        skip_group_check=True)
                return last
            sc.emit("T", ctxmm,
                    reads=[("kvk", kb_i, 0), ("kvk", kb_i, 1),
                           ("kvv", kb_i, 0), ("kvv", kb_i, 1)],
                    writes=[("kvsb_free", kb_i)] + ([("ctx_acc",)] if i == NT - 1 else []))

    w_wave2 = load_weight(wo, wo_h, accum=[rwo[:, k:k + 1] for k in range(NB)])
    w_wave2.append(TH("V", lambda e: e.tensor_copy(out=rwo_h[:, :], in_=rwo[:, :]),
                      reads=[("wdst", id(wo_h), k) for k in range(NB)], writes=[("rwo_h",)]))

    s0 = stats1_head(0, q="SY") + stats1_tail(0) + stats1_head(1, q="SY")
    bg.extend(w_wq)
    while s0 or bg:
        if s0:
            s0.pop(0)()
        inject(1)
    for t in normalize_thunks(0):
        t()
    bg.extend(stats1_tail(1))
    bg.extend(normalize_thunks(1))
    bg.extend(w_wave1)
    for c in range(NCH):
        if c + 2 < NCH:
            bg.extend(stats1_head(c + 2))
        emit_compute1(c, post_q=(w_wave2 if c == NCH - 1 else None))
        if c + 2 < NCH:
            bg.extend(stats1_tail(c + 2))
            bg.extend(normalize_thunks(c + 2))
    drain_bg()


    for h in range(H):
        sc.emit("S", lambda e, h=h: e.activation(out=ctxsb_sl(h),
                                                 in_=ctx_ps(h)[0:64, 0:65],
                                                 func=AF.Copy, bias=0.0, scale=1.0),
                reads=[("ctx_acc",)], writes=[("ctxsb", h)])
    sc.emit("V", lambda e: e.memset(zlhs[:, :], 0.0), writes=[("zlhs_z",)])
    for kb in range(NB):
        for j in range(2):
            h = 2 * kb + j
            sc.emit("V", lambda e, kb=kb, j=j, h=h: e.tensor_copy(
                out=zlhs[j * 64:j * 64 + 64, kb * 16 + h:kb * 16 + h + 1],
                in_=ctxsb_sl(h, 65)[:, 64:65]),
                reads=[("ctxsb", h), ("zlhs_z",)], writes=[("zlhs", h)])

    def a1z_thunks(c):
        th = []

        def zmm(e, c=c):
            last = None
            for kb in range(NB):
                last = e.matmul(out=pmisc[0:16, :], lhsT=zlhs[:, kb * 16:(kb + 1) * 16],
                                rhs=qp[kb][:, cc(c)], start=(kb == 0), stop=(kb == NB - 1),
                                skip_group_check=True)
            return last
        th.append(TH("T", zmm, reads=[("zlhs", h) for h in range(H)] +
                     [("qp", kb, c) for kb in range(NB)], writes=[("pzdot",)]))
        th.append(TH("V", lambda e, c=c: e.tensor_scalar_add(
            out=zdsb[:, :], in0=pmisc[0:16, :], scalar1=1e-6),
            reads=[("pzdot",)], writes=[("zdsb",)]))
        th.append(TH("S", lambda e, c=c: e.activation(out=zdsb[:, :], in_=zdsb[:, :],
                                                      func=AF.Ln, bias=0.0, scale=1.0),
                     reads=[("zdsb",)], writes=[("zdsb",)]))
        th.append(TH("V", lambda e, c=c: e.tensor_scalar_mul(out=zdsb[:, :], in0=zdsb[:, :],
                                                             scalar1=-1.0),
                     reads=[("zdsb",)], writes=[("zdsb",)]))
        th.append(TH("S", lambda e, c=c: e.activation(out=zdsb[:, :], in_=zdsb[:, :],
                                                      func=AF.Exp, bias=0.0, scale=1.0),
                     reads=[("zdsb",)], writes=[("zdsb",)]))
        th.append(TH("V", lambda e: e.tensor_scalar_mul(out=z16[:, :], in0=zdsb[:, :],
                                                        scalar1=1024.0),
                     reads=[("zdsb",)], writes=[("z16",)]))
        th.append(TH("SY", lambda e, c=c: e.dma_start(out=zscr[:, cc(c)], in_=z16[:, :]),
                     reads=[("z16",)], writes=[("zscr", c)], sem="zsc"))
        return th

    def zb_thunks(c):
        th = []
        for p in range(PAIRS):
            th.extend(zrep_bcast(c, p))
        return th

    def zrep_bcast(c, p):
        def za(e, p=p, c=c):
            return e.dma_start(out=zreps[c % 2][p][0:64, :],
                               in_=zscr[2 * p:2 * p + 1, cc(c)].partition_broadcast(64))

        def zb(e, p=p, c=c):
            return e.dma_start(out=zreps[c % 2][p][64:128, :],
                               in_=zscr[2 * p + 1:2 * p + 2, cc(c)].partition_broadcast(64))
        return [TH("SY", za, reads=[("zscr", c)],
                   writes=[("zrep", c % 2, p)], sem=f"zr{p}a{c % 2}"),
                TH("SY", zb, reads=[("zscr", c)],
                   writes=[("zrep2", c % 2, p)], sem=f"zr{p}b{c % 2}")]

    def a1o_thunks(c):
        th = []
        for p in range(PAIRS):
            pb = pmain[p % 2]

            def outmm(e, p=p, c=c, pb=pb):
                e.matmul(out=pb[0:64, :], lhsT=ctxsb_sl(2 * p, 64),
                         rhs=qp[p][0:64, cc(c)], start=True, stop=True)
                return e.matmul(out=pb[64:128, :], lhsT=ctxsb_sl(2 * p + 1, 64),
                                rhs=qp[p][64:128, cc(c)], start=True, stop=True)
            th.append(TH("T", outmm, reads=[("ctxsb", 2 * p), ("ctxsb", 2 * p + 1), ("qp", p, c)],
                         writes=[("pA", p % 2)]))
            th.append(TH("S", lambda e, p=p, pb=pb: e.activation(
                out=oev[p % 2][:, :], in_=pb[:, :], func=AF.Identity,
                bias=0.0, scale=1.0 / 1024.0),
                reads=[("pA", p % 2), ("cvals",)], writes=[("oev", p % 2)]))
            th.append(TH("GP", lambda e, p=p, c=c: e.tensor_mul(
                out=outz[c % NOZ][p][:, :], in0=oev[p % 2][:, :], in1=zreps[c % 2][p][:, :]),
                reads=[("oev", p % 2), ("zrep", c % 2, p), ("zrep2", c % 2, p)],
                writes=[("outz", c % NOZ, p)], sem="GP", inc=1))
        return th

    def a2_thunks(c):
        ensure(("wdst", id(wo_h), NB - 1))
        th = []
        for db in range(NB):
            pb = pmain[db % 2]

            def womm(e, db=db, c=c, pb=pb):
                last = None
                for kb in range(NB):
                    last = e.matmul(out=pb[:, :], lhsT=wo_h[kb][:, db * 128:(db + 1) * 128],
                                    rhs=outz[c % NOZ][kb][:, :], start=(kb == 0), stop=(kb == NB - 1))
                return last
            th.append(TH("T", womm,
                         reads=[("outz", c % NOZ, kb) for kb in range(NB)] +
                               [("wdst", id(wo_h), kb) for kb in range(NB)],
                         writes=[("pA", db % 2)]))

            def xld(e, db=db, c=c):
                return e.dma_start(out=xr[db % 2][:, :],
                                   in_=xT[db * 128:(db + 1) * 128, cc(c)])
            th.append(TH("SY", xld, writes=[("xr", db % 2)], sem=f"xr{db % 2}"))
            th.append(TH("V", lambda e, pb=pb, db=db, c=c: e.scalar_tensor_tensor(
                out=x2[c % NX2][db][:, :], in0=pb[:, :], scalar=bo_cf[:, db:db + 1],
                in1=xr[db % 2][:, :], op0=ALU.add, op1=ALU.add),
                reads=[("pA", db % 2), ("xr", db % 2), ("bo",)],
                writes=[("x2", c % NX2, db)]))
        return th

    def c1_thunks(c):
        th = []

        def sdmm(e, c=c):
            for kb in range(NB):
                e.matmul(out=pmisc[32:33, :], lhsT=rwo_h[:, kb:kb + 1],
                         rhs=outz[c % NOZ][kb][:, :], start=(kb == 0), stop=False,
                         skip_group_check=True)
            return e.matmul(out=pmisc[32:33, :], lhsT=sbo_h[:, :], rhs=onesr_h[:, :],
                            start=False, stop=True, skip_group_check=True)
        th.append(TH("T", sdmm, reads=[("outz", c % NOZ, kb) for kb in range(NB)] +
                     [("rwo_h",), ("sbo_h",), ("onesr_h",)], writes=[("psum_sx",)]))
        for db in range(NB):
            th.append(TH("S", lambda e, db=db, c=c: e.activation(
                out=xsq2[db % 2][:, :], in_=x2[c % NX2][db][:, :], func=AF.Square,
                bias=0.0, scale=1.0),
                reads=[("x2", c % NX2, db)], writes=[("xsq2", db % 2)]))
            th.append(TH("T", lambda e, db=db: e.matmul(
                out=pmisc[64:65, :], lhsT=ones_h[:, 0:1], rhs=xsq2[db % 2][:, :],
                start=(db == 0), stop=(db == NB - 1), skip_group_check=True),
                reads=[("xsq2", db % 2), ("ones_h",)], writes=[("psum_sq",)]))
        th.append(TH("SY", lambda e, c=c: e.dma_start(out=m1c[:, :], in_=scr[0:1, cc(c)]),
                     reads=[("scr1", c)], writes=[("m1c",)], sem="m1c"))
        th.append(TH("V", lambda e: e.tensor_scalar_mul(out=trow[0][:, :], in0=pmisc[32:33, :],
                                                        scalar1=1.0 / D),
                     reads=[("psum_sx",)], writes=[("trow0",)]))
        th.append(TH("V", lambda e, c=c: e.tensor_add(out=m2row[:, :], in0=m1c[:, :],
                                                      in1=trow[0][:, :]),
                     reads=[("trow0",), ("m1c",)], writes=[("m2row",)]))
        th.append(TH("V", lambda e: e.tensor_scalar_mul(out=trow[0][:, :], in0=pmisc[64:65, :],
                                                        scalar1=1.0 / D),
                     reads=[("psum_sq",)], writes=[("trow0",)]))
        th.append(TH("S", lambda e, c=c: e.activation(out=trow[1][:, :], in_=m2row[:, :],
                                                      func=AF.Square, bias=0.0, scale=1.0),
                     reads=[("m2row",)], writes=[("trow1",)]))
        th.append(TH("V", lambda e: e.scalar_tensor_tensor(out=trow[0][:, :], in0=trow[0][:, :],
                                                           scalar=1e-5, in1=trow[1][:, :],
                                                           op0=ALU.add, op1=ALU.subtract),
                     reads=[("trow0",), ("trow1",)], writes=[("trow0",)]))
        th.append(TH("S", lambda e: e.activation(out=trow[1][:, :], in_=trow[0][:, :],
                                                 func=AF.Ln, bias=0.0, scale=1.0),
                     reads=[("trow0",)], writes=[("trow1",)]))
        th.append(TH("S", lambda e, c=c: e.activation(out=i2row[:, :], in_=trow[1][:, :],
                                                      func=AF.Exp, bias=0.0, scale=-0.5),
                     reads=[("trow1",), ("cvals",)], writes=[("i2row",)]))

        def bnc(e, c=c):
            return e.dma_start(out=scr[1:2, :].rearrange("a (g t) -> a g t", g=2)[:, :, cc(c)],
                               in_=rows2[0:1, :].rearrange("a (g t) -> a g t", g=2))
        th.append(TH("SY", bnc, reads=[("m2row",), ("i2row",)], writes=[("scr2", c)], sem="sc2"))

        def rp1(e, c=c):
            return e.dma_start(out=m2rep[c % NRP][:, :],
                               in_=scr[1:2, cc(c)].partition_broadcast(128))
        th.append(TH("SY", rp1, reads=[("scr2", c)], writes=[("m1rep", c % NRP)], sem=f"rpa{c % NRP}"))

        def rp2(e, c=c):
            return e.dma_start(out=i2rep[c % NRP][:, :],
                               in_=scr[1:2, T + c * CH:T + (c + 1) * CH].partition_broadcast(128))
        th.append(TH("SY", rp2, reads=[("scr2", c)], writes=[("i1rep", c % NRP)], sem=f"rpb{c % NRP}"))
        return th

    def c2a_thunks(c):
        ensure(("wdst", id(wf8), NB - 1))
        th = []
        for db in range(NB):
            th.append(TH("V", lambda e, db=db, c=c: e.tensor_sub(
                out=xc2[db % NXC][:, :], in0=x2[c % NX2][db][:, :], in1=m2rep[c % NRP][:, :]),
                reads=[("x2", c % NX2, db), ("m1rep", c % NRP)], writes=[("xc1", db % NXC)]))
            th.append(TH("V", lambda e, db=db, c=c: e.tensor_mul(
                out=h2q[c % NH2][db][:, :], in0=xc2[db % NXC][:, :], in1=i2rep[c % NRP][:, :]),
                reads=[("xc1", db % NXC), ("i1rep", c % NRP)], writes=[("h2q", c % NH2, db)]))
            th.append(TH("T", lambda e, kb=db, c=c: e.matmul(
                out=pkv[0][:, :], lhsT=wf8[kb][:, 0:128],
                rhs=h2q[c % NH2][kb][:, :], start=(kb == 0), stop=(kb == NB - 1)),
                reads=[("h2q", c % NH2, db), ("wdst", id(wf8), db)],
                writes=[("pC", 0)]))
        return th

    def c2b_thunks(c):
        th = []
        for fb in range(NB):
            pb = pkv[fb % 2]
            if fb > 0:
                def ffnmm(e, fb=fb, c=c, pb=pb):
                    last = None
                    for kb in range(NB):
                        last = e.matmul(out=pb[:, :], lhsT=wf8[kb][:, fb * 128:(fb + 1) * 128],
                                        rhs=h2q[c % NH2][kb][:, :], start=(kb == 0), stop=(kb == NB - 1))
                    return last
                th.append(TH("T", ffnmm,
                             reads=[("h2q", c % NH2, kb) for kb in range(NB)] +
                                   [("wdst", id(wf8), kb) for kb in range(NB)],
                             writes=[("pC", fb % 2)]))
            th.append(TH("V", lambda e, pb=pb, fb=fb, c=c: e.scalar_tensor_tensor(
                out=x2[c % NX2][fb][:, :], in0=pb[:, :], scalar=bf_cf[:, fb:fb + 1],
                in1=x2[c % NX2][fb][:, :], op0=ALU.add, op1=ALU.add),
                reads=[("pC", fb % 2)], writes=[("x2", c % NX2, fb)]))

            def yst(e, fb=fb, c=c):
                return e.dma_start(out=yT[fb * 128:(fb + 1) * 128, cc(c)],
                                   in_=x2[c % NX2][fb][:, :])
            th.append(TH("GP", yst, reads=[("x2", c % NX2, fb)],
                         writes=[("ydma", c, fb)], sem=f"yd{fb}_{c % 2}"))
        return th

    def run_all(th):
        for t in th:
            t()

    def interleave(l1, l2):
        i = j = 0
        while i < len(l1) or j < len(l2):
            if i < len(l1):
                l1[i]()
                i += 1
            if j < len(l2):
                l2[j]()
                j += 1

    run_all(a1z_thunks(0))
    run_all(zb_thunks(0))
    interleave(a1z_thunks(1) if NCH > 1 else [], a1o_thunks(0))
    run_all(a2_thunks(0))
    for c in range(NCH):
        ahead = (a1z_thunks(c + 2) if c + 2 < NCH else []) + \
                (zb_thunks(c + 1) if c + 1 < NCH else [])
        interleave(ahead, c1_thunks(c))
        a1o = a1o_thunks(c + 1) if c + 1 < NCH else []
        interleave(c2a_thunks(c), a1o)
        run_all(c2b_thunks(c))
        if c + 1 < NCH:
            run_all(a2_thunks(c + 1))

    sems = {}
    with ExitStack() as ses:
        for name in sc.sem_names:
            sems[name] = ses.enter_context(nc.semaphore(f"sem{name}"))
        block = ses.enter_context(nc.Block())

        @block.tensor
        def _(eng):
            sc.replay("T", eng, sems)

        @block.vector
        def _(eng):
            sc.replay("V", eng, sems)

        @block.scalar
        def _(eng):
            sc.replay("S", eng, sems)

        @block.sync
        def _(eng):
            sc.replay("SY", eng, sems)

        @block.gpsimd
        def _(eng):
            sc.replay("GP", eng, sems)
            for fb in range(NB):
                for par in range(2):
                    eng.wait_ge(sems[f"yd{fb}_{par}"], sc.count[f"yd{fb}_{par}"])
    es.close()
    return nc


_NC_CACHE = None


def kernel(L0_attn_x, L0_attn_Wq, L0_attn_bq, L0_attn_Wd, L0_attn_bd,
           L0_attn_Wu, L0_attn_bu, L0_attn_Wo, L0_attn_bo,
           L0_ffn_weight, L0_ffn_bias):
    global _NC_CACHE
    x = np.asarray(L0_attn_x, dtype=np.float32)
    B = x.shape[0]
    f32 = lambda a: np.ascontiguousarray(np.asarray(a, dtype=np.float32))
    kidx = np.concatenate([np.arange(h * 128, h * 128 + 64) for h in range(H)])
    vidx = np.concatenate([np.arange(h * 128 + 64, h * 128 + 128) for h in range(H)])
    perm = np.concatenate([kidx, vidx])
    common = {
        "wq": f32(L0_attn_Wq),
        "wd": f32(L0_attn_Wd),
        "wu": f32(np.asarray(L0_attn_Wu, dtype=np.float32)[:, perm]),
        "wo": f32(L0_attn_Wo),
        "wf": f32(np.asarray(L0_ffn_weight, dtype=np.float32).T),
        "bq_r": f32(np.asarray(L0_attn_bq, dtype=np.float32).reshape(1, D)),
        "bd_c": f32(np.asarray(L0_attn_bd, dtype=np.float32).reshape(2, 128).T),
        "bu_r": f32(np.asarray(L0_attn_bu, dtype=np.float32)[perm].reshape(1, 2 * D)),
        "bo_c": f32(np.asarray(L0_attn_bo, dtype=np.float32).reshape(NB, 128).T),
        "bf_c": f32(np.asarray(L0_ffn_bias, dtype=np.float32).reshape(NB, 128).T),
        "ones_in": np.ones((128, 128), dtype=np.float32),
        "onesr_in": np.ones((1, CH), dtype=np.float32),
        "cvals_in": np.array([[-0.5, 1.0 / D, 1e-5, -1.0, 1.0 / 1024.0]], dtype=np.float32),
        "ident_in": np.eye(128, dtype=np.float32),
    }
    in_maps = []
    for b in range(B):
        m = dict(common)
        m["xT"] = np.ascontiguousarray(x[b].T)
        m["xtok_d"] = np.ascontiguousarray(x[b])
        in_maps.append(m)
    if _NC_CACHE is None:
        _NC_CACHE = build_nc()
    res = run_bass_kernel_spmd(_NC_CACHE, in_maps, core_ids=list(range(B)))
    out = np.stack([np.ascontiguousarray(r["yT"].T) for r in res.results], axis=0)
    return out


# version 21
# speedup vs baseline: 1.0338x; 1.0129x over previous
import numpy as np
from contextlib import ExitStack
import concourse.bass as bass
import concourse.mybir as mybir
from concourse.bass_utils import run_bass_kernel_spmd

F32 = mybir.dt.float32
F16 = mybir.dt.float16
FP8 = mybir.dt.float8e4
AF = mybir.ActivationFunctionType
ALU = mybir.AluOpType

T, D, H, DH, RK = 2048, 1024, 16, 64, 256
CH = 512
NCH = T // CH
NB = D // 128
NT = T // 128
PAIRS = H // 2
NXCH, NXS, NX2, NOZ, NH2, NRP, NXC = 1, 2, 2, 1, 1, 2, 1


class Sched:

    ENGS = ("T", "V", "S", "SY", "GP")
    ONE_WAVE = ("cst", "cstg")

    def __init__(self):
        self.steps = {e: [] for e in self.ENGS}
        self.count = {}
        self.writer = {}
        self.readers = {}
        self.watermark = {e: {} for e in self.ENGS}
        self.sem_names = []

    def _bump(self, sem, inc):
        if sem not in self.count:
            self.count[sem] = 0
            self.sem_names.append(sem)
        self.count[sem] += inc
        return self.count[sem]

    def emit(self, eng, fn, reads=(), writes=(), sem=None, inc=None):
        if eng in ("SY", "GP") and inc != 1:
            assert sem is not None, "DMA steps need a sem stream name"
            inc = 16
        else:
            sem = sem or eng
            inc = 1
        waits = {}
        def add(w):
            if w is None:
                return
            ws, wv = w
            if ws == "T" and eng == "T":
                return
            waits[ws] = max(waits.get(ws, 0), wv)
        for t in reads:
            add(self.writer.get(t))
        for t in writes:
            add(self.writer.get(t))
            for r in self.readers.get(t, ()):
                add(r)
        myval = self._bump(sem, inc)
        final_waits = []
        for ws, wv in sorted(waits.items()):
            if ws == sem and eng not in ("SY", "GP"):
                pass
            if self.watermark[eng].get(ws, 0) < wv:
                final_waits.append((ws, wv))
                self.watermark[eng][ws] = wv
        self.steps[eng].append((final_waits, fn, sem, inc))
        for t in writes:
            self.writer[t] = (sem, myval)
            self.readers[t] = []
        for t in reads:
            self.readers.setdefault(t, []).append((sem, myval))

    def replay(self, eng_name, eng_obj, sems):
        for (waits, fn, sem, inc) in self.steps[eng_name]:
            for (ws, wv) in waits:
                if ws in self.ONE_WAVE:
                    wv = self.count[ws]
                eng_obj.wait_ge(sems[ws], wv)
            last = fn(eng_obj)
            assert last is not None, "step fn must return its last instruction"
            last.then_inc(sems[sem], inc)


def build_nc():
    nc = bass.Bass("TRN2", target_bir_lowering=False)
    dt = nc.dram_tensor
    xT = dt("xT", [D, T], F32, kind="ExternalInput").ap()
    xtok_d = dt("xtok_d", [T, D], F32, kind="ExternalInput").ap()
    ident_in = dt("ident_in", [128, 128], F32, kind="ExternalInput").ap()
    wq = dt("wq", [D, D], F32, kind="ExternalInput").ap()
    wd = dt("wd", [D, RK], F32, kind="ExternalInput").ap()
    wu = dt("wu", [RK, 2 * D], F32, kind="ExternalInput").ap()
    wo = dt("wo", [D, D], F32, kind="ExternalInput").ap()
    wf = dt("wf", [D, D], F32, kind="ExternalInput").ap()
    bq_r = dt("bq_r", [1, D], F32, kind="ExternalInput").ap()
    bd_c = dt("bd_c", [128, 2], F32, kind="ExternalInput").ap()
    bu_r = dt("bu_r", [1, 2 * D], F32, kind="ExternalInput").ap()
    bo_c = dt("bo_c", [128, NB], F32, kind="ExternalInput").ap()
    bf_c = dt("bf_c", [128, NB], F32, kind="ExternalInput").ap()
    ones_in = dt("ones_in", [128, 128], F32, kind="ExternalInput").ap()
    onesr_in = dt("onesr_in", [1, CH], F32, kind="ExternalInput").ap()
    cvals_in = dt("cvals_in", [1, 5], F32, kind="ExternalInput").ap()
    scr = dt("scr", [2, 2 * T], F32, kind="Internal").ap()
    zscr = dt("zscr", [H, T], F16, kind="Internal").ap()
    buscr = dt("buscr", [1, 2 * D], F16, kind="Internal").ap()
    yT = dt("yT", [D, T], F32, kind="ExternalOutput").ap()

    es = ExitStack()
    sb = lambda name, shape, d: es.enter_context(nc.sbuf_tensor(name, shape, d))

    wstage = [sb(f"wstage{i}", [128, D], F32) for i in range(2)]
    wq_h = [sb(f"wq_h{k}", [128, D], F16) for k in range(NB)]
    wd_h = [sb(f"wd_h{k}", [128, RK], F16) for k in range(NB)]
    wu_h = [sb(f"wu_h{k}", [128, 2 * D], F16) for k in range(2)]
    wo_h = wq_h
    wf8 = [sb(f"wf8_{k}", [128, D], FP8) for k in range(NB)]
    rwo = sb("rwo", [128, NB], F32)
    rwo_h = sb("rwo_h", [128, NB], F16)
    bq_rf = sb("bq_rf", [1, D], F32)
    bq_rh = sb("bq_rh", [1, D], F16)
    bu_rf = sb("bu_rf", [1, 2 * D], F32)
    bu_rh = sb("bu_rh", [1, 2 * D], F16)
    bd_cf = sb("bd_cf", [128, 2], F32)
    bo_cf = sb("bo_cf", [128, NB], F32)
    bf_cf = sb("bf_cf", [128, NB], F32)
    cvals = sb("cvals", [1, 5], F32)
    ones_f = sb("ones_f", [128, 128], F32)
    ones_h = sb("ones_h", [128, 128], F16)
    onesr_f = sb("onesr_f", [1, CH], F32)
    onesr_h = sb("onesr_h", [1, CH], F16)
    burep = [sb(f"burep{i}", [128, CH], F16) for i in range(2)]

    xtok = [sb(f"xtok{i}", [128, D], F32) for i in range(2)]
    scol = [sb(f"scol{i}", [128, 8], F32) for i in range(2)]
    pst8 = sb("pst8", [8, 128], F32)
    msq4t = sb("msq4t", [128, 4], F32)
    ident = sb("ident", [128, 128], F32)
    m1rep = [sb(f"m1rep{i}", [128, CH], F32) for i in range(NRP)]
    i1rep = [sb(f"i1rep{i}", [128, CH], F32) for i in range(NRP)]
    m2rep = m1rep
    i2rep = i1rep
    xc1 = [sb(f"xc1_{i}", [128, CH], F32) for i in range(NXC)]
    xs1 = [[sb(f"xs1_{i}_{b}", [128, CH], F16) for b in range(NB)] for i in range(NXS)]
    qp = [sb(f"qp{b}", [128, T], F16) for b in range(NB)]
    ebuf = [sb(f"ebuf{i}", [128, CH], F16) for i in range(2)]
    e1buf = ebuf
    kv1 = [[sb(f"kv1_{i}_{r}", [128, CH], F16) for r in range(2)] for i in range(1)]
    kvsb = [sb(f"kvsb{i}", [128, 2 * 1040], F16) for i in range(2)]
    ctxsb = sb("ctxsb", [128, PAIRS * 65], F16)
    bo16 = sb("bo16", [128, NB], F16)
    sbo = sb("sbo", [1, 1], F32)
    sbo_h = sb("sbo_h", [1, 1], F16)
    zlhs = sb("zlhs", [128, NB * 16], F16)
    zdsb = sb("zdsb", [16, CH], F32)
    z16 = sb("z16", [16, CH], F16)
    oev = [sb(f"oev{i}", [128, CH], F16) for i in range(4)]
    rows2 = sb("rows2", [1, 2 * CH], F32)
    m2row = rows2[0:1, 0:CH]
    i2row = rows2[0:1, CH:2 * CH]
    m1c = sb("m1c", [1, CH], F32)
    trow = [sb(f"trow{i}", [1, CH], F32) for i in range(2)]
    trow = trow + trow
    zrep = [sb(f"zrep{i}", [128, CH], F16) for i in range(8)]
    zrv = []
    for nm, hostt in (("zrv0", wu_h[0]), ("zrv1", wu_h[1])):
        vt = nc.virtual_sbuf_tensor(nm, [128, 4 * CH], F16)
        nc.back_virtual_tensor_with(vt, hostt)
        zrv.append(vt)
    zreps = [zrep, [zrv[p // 4][:, (p % 4) * CH:(p % 4 + 1) * CH] for p in range(8)]]
    outz = [[sb(f"outz{i}_{b}", [128, CH], F16) for b in range(NB)] for i in range(NOZ)]
    xr = [sb(f"xr{i}", [128, CH], F32) for i in range(2)]
    x2 = [[sb(f"x2_0_{b}", [128, CH], F32) for b in range(NB)]]
    x2v = []
    for nm, hostt in (("x2v0", wstage[0]), ("x2v1", wstage[1]),
                      ("x2v2", xtok[0]), ("x2v3", xtok[1])):
        vt = nc.virtual_sbuf_tensor(nm, [128, 2 * CH], F32)
        nc.back_virtual_tensor_with(vt, hostt)
        x2v.append(vt)
    x2.append([x2v[b // 2][:, (b % 2) * CH:(b % 2 + 1) * CH] for b in range(NB)])
    xsq2 = [sb(f"xsq2_{i}", [128, CH], F16) for i in range(2)]
    xc2 = xc1
    h2q = [[sb(f"h2q{i}_{b}", [128, CH], FP8) for b in range(NB)] for i in range(NH2)]

    psum = lambda name, shape: es.enter_context(nc.psum_tensor(name, shape, F32))
    pmain = [psum("pm0", [128, CH]), psum("pm1", [128, CH])]
    pctx = [psum("pc0", [64, 7 * 65]), psum("pc1", [64, 7 * 65]),
            psum("pc2", [64, 2 * 65])]
    pmisc = psum("pmisc", [128, CH])
    pkv = [psum("pk0", [128, CH]), psum("pk1", [128, CH])]
    pst = pmisc[0:8, 0:128]

    def ctx_ps(h):
        if h < 7:
            return pctx[0][0:64, h * 65:(h + 1) * 65]
        if h < 14:
            return pctx[1][0:64, (h - 7) * 65:(h - 6) * 65]
        return pctx[2][0:64, (h - 14) * 65:(h - 13) * 65]

    def ctxsb_sl(h, ncol=65):
        r0 = (h % 2) * 64
        c0 = (h // 2) * 65
        return ctxsb[r0:r0 + 64, c0:c0 + ncol]

    sc = Sched()
    cc = lambda c: slice(c * CH, (c + 1) * CH)
    cc2 = lambda c: slice(c * 128, (c + 1) * 128)
    import collections as _co
    bg = _co.deque()

    def TH(eng, fn, reads=(), writes=(), sem=None, inc=None):
        return lambda: sc.emit(eng, fn, reads=reads, writes=writes, sem=sem, inc=inc)

    def inject(n=1):
        for _ in range(n):
            if not bg:
                return
            bg.popleft()()

    def drain_bg():
        while bg:
            bg.popleft()()

    def ensure(tok):
        while tok not in sc.writer and bg:
            bg.popleft()()

    def dma(q, key, out, in_, reads=(), writes=()):
        sc.emit(q, lambda e, o=out, i=in_: e.dma_start(out=o, in_=i),
                reads=reads, writes=writes, sem=key)

    dma("SY", "cst", cvals[:, :], cvals_in[:, :], writes=[("cvals",)])
    nc.const_aps.register(F32, -0.5, cvals[0:1, 0:1])
    nc.const_aps.register(F32, 1.0 / 1024.0, cvals[0:1, 4:5])
    dma("SY", "cst", ones_f[:, :], ones_in[:, :], writes=[("ones_f",)])
    dma("SY", "cst", ident[:, :], ident_in[:, :], writes=[("ident",)])
    dma("SY", "cst", onesr_f[:, :], onesr_in[:, :], writes=[("onesr_f",)])
    dma("SY", "cst", bq_rf[:, :], bq_r[:, :], writes=[("bq_rf",)])
    dma("SY", "cst", bd_cf[:, :], bd_c[:, :], writes=[("bd",)])
    dma("SY", "cst", bu_rf[:, :], bu_r[:, :], writes=[("bu_rf",)])
    dma("SY", "cst", bo_cf[:, :], bo_c[:, :], writes=[("bo",)])
    dma("SY", "cst", bf_cf[:, :], bf_c[:, :], writes=[("bf",)])

    sc.emit("V", lambda e: e.tensor_copy(out=ones_h[:, :], in_=ones_f[:, :]),
            reads=[("ones_f",)], writes=[("ones_h",)])
    sc.emit("V", lambda e: e.tensor_copy(out=onesr_h[:, :], in_=onesr_f[:, :]),
            reads=[("onesr_f",)], writes=[("onesr_h",)])
    sc.emit("V", lambda e: e.tensor_copy(out=bq_rh[:, :], in_=bq_rf[:, :]),
            reads=[("bq_rf",)], writes=[("bq_rh",)])
    sc.emit("V", lambda e: e.tensor_copy(out=bu_rh[:, :], in_=bu_rf[:, :]),
            reads=[("bu_rf",)], writes=[("bu_rh",)])
    dma("GP", "busc", buscr[0:1, :], bu_rh[0:1, :], reads=[("bu_rh",)], writes=[("buscr",)])
    dma("GP", "cstg", burep[0][:, :], buscr[0:1, D:D + CH].partition_broadcast(128),
        reads=[("buscr",)], writes=[("burep", 0)])
    dma("GP", "cstg", burep[1][:, :], buscr[0:1, D + CH:2 * D].partition_broadcast(128),
        reads=[("buscr",)], writes=[("burep", 1)])

    wi = [0]

    def load_weight(src_ap, dst_list, accum=None):
        th = []
        nblk = src_ap.shape[0] // 128
        cols = src_ap.shape[1]
        nh = (cols + D - 1) // D
        for k in range(nblk):
            for hh in range(nh):
                c0, c1 = hh * D, min((hh + 1) * D, cols)
                sidx = wi[0] % 2
                wi[0] += 1
                st = wstage[sidx]

                def dth(e, s=st, c0=c0, c1=c1, k=k, sa=src_ap):
                    return e.dma_start(out=s[:, 0:c1 - c0],
                                       in_=sa[k * 128:(k + 1) * 128, c0:c1])
                th.append(TH("SY", dth, reads=(), writes=[("wstage", sidx)],
                             sem=f"wst{sidx}"))
                acc = accum[k] if (accum is not None and hh == 0) else None

                def cast(e, s=st, d=dst_list[k], a=acc, c0=c0, c1=c1):
                    return e.activation(out=d[:, c0:c1], in_=s[:, 0:c1 - c0],
                                        func=AF.Copy, bias=0.0, scale=1.0, accum_out=a)
                th.append(TH("S", cast, reads=[("wstage", sidx)],
                             writes=[("wdst", id(dst_list), k)] if hh == nh - 1 else [("wdsth", id(dst_list), k)]))
        return th

    w_wq = load_weight(wq, wq_h)
    w_wave1 = (load_weight(wd, wd_h) + load_weight(wu, wu_h) + load_weight(wf, wf8))
    sc.emit("V", lambda e: e.tensor_copy(out=bo16[:, :], in_=bo_cf[:, :]),
            reads=[("bo",)], writes=[("bo16",)])
    sc.emit("T", lambda e: e.matmul(out=pmisc[32:33, 0:NB], lhsT=ones_h[:, 0:1],
                                    rhs=bo16[:, :], start=True, stop=True,
                                    skip_group_check=True),
            reads=[("bo16",), ("ones_h",)], writes=[("psum_sx",)])
    sc.emit("V", lambda e: e.reduce_sum(out=sbo[:, :], in_=pmisc[32:33, 0:NB],
                                        axis=mybir.AxisListType.X),
            reads=[("psum_sx",)], writes=[("sbo",)])
    sc.emit("V", lambda e: e.tensor_copy(out=sbo_h[:, :], in_=sbo[:, :]),
            reads=[("sbo",)], writes=[("sbo_h",)])
    for i in range(2):
        sc.emit("V", lambda e, i=i: e.memset(
            kvsb[i][:].rearrange("p (g n) -> p g n", g=32, n=65)[:, :, 64:65], 1.0),
            writes=[("kvones", i)])

    def stats1_head(c, q="GP"):
        th = []
        sc_ = scol[c % 2]
        for ti in range(4):
            def ld(e, ti=ti, c=c):
                return e.dma_start(out=xtok[ti % 2][:, :],
                                   in_=xtok_d[c * CH + ti * 128:c * CH + (ti + 1) * 128, :])
            pfx = "xtk" if q == "GP" else "xtsy"
            th.append(TH(q, ld, writes=[("xtok", ti % 2)], sem=f"{pfx}{ti % 2}"))
            th.append(TH("S", lambda e, ti=ti, sc_=sc_: e.activation(
                out=xtok[ti % 2][:, :], in_=xtok[ti % 2][:, :], func=AF.Copy,
                bias=0.0, scale=1.0, accum_out=sc_[:, ti:ti + 1]),
                reads=[("xtok", ti % 2)], writes=[("scolm", c % 2, ti), ("xtok", ti % 2)]))
            th.append(TH("S", lambda e, ti=ti, sc_=sc_: e.activation(
                out=xtok[ti % 2][:, :], in_=xtok[ti % 2][:, :], func=AF.Square,
                bias=0.0, scale=1.0, accum_out=sc_[:, 4 + ti:5 + ti]),
                reads=[("xtok", ti % 2)], writes=[("scolq", c % 2, ti), ("xtok", ti % 2)]))
        return th

    def stats1_tail(c):
        th = []
        sc_ = scol[c % 2]
        th.append(TH("V", lambda e, sc_=sc_: e.tensor_scalar_mul(out=sc_[:, 0:8], in0=sc_[:, 0:8],
                                                                scalar1=1.0 / D),
                     reads=[("scolm", c % 2, 0), ("scolm", c % 2, 1), ("scolm", c % 2, 2),
                            ("scolm", c % 2, 3), ("scolq", c % 2, 0), ("scolq", c % 2, 1),
                            ("scolq", c % 2, 2), ("scolq", c % 2, 3)],
                     writes=[("scol", c % 2)]))
        th.append(TH("S", lambda e, sc_=sc_: e.activation(out=msq4t[:, :], in_=sc_[:, 0:4],
                                                          func=AF.Square, bias=0.0, scale=1.0),
                     reads=[("scol", c % 2)], writes=[("msq4",)]))
        th.append(TH("V", lambda e, sc_=sc_: e.scalar_tensor_tensor(out=sc_[:, 4:8], in0=sc_[:, 4:8],
                                                                    scalar=1e-5, in1=msq4t[:, :],
                                                                    op0=ALU.add, op1=ALU.subtract),
                     reads=[("scol", c % 2), ("msq4",)], writes=[("scol", c % 2)]))
        th.append(TH("S", lambda e, sc_=sc_: e.activation(out=sc_[:, 4:8], in_=sc_[:, 4:8],
                                                          func=AF.Ln, bias=0.0, scale=1.0),
                     reads=[("scol", c % 2)], writes=[("scol", c % 2)]))
        th.append(TH("S", lambda e, sc_=sc_: e.activation(out=sc_[:, 4:8], in_=sc_[:, 4:8],
                                                          func=AF.Exp, bias=0.0, scale=-0.5),
                     reads=[("scol", c % 2), ("cvals",)], writes=[("scol", c % 2)]))
        th.append(TH("T", lambda e, sc_=sc_: e.transpose(out=pst, in_=sc_[:, :],
                                                         identity=ident[:, :]),
                     reads=[("scol", c % 2), ("ident",)], writes=[("pzdot",)]))
        th.append(TH("V", lambda e: e.tensor_copy(out=pst8[:, :], in_=pst),
                     reads=[("pzdot",)], writes=[("pst8",)]))

        def bnc(e, c=c):
            dst = scr[0:1, :].rearrange("a (g ch u t) -> a g ch u t",
                                        g=2, ch=NCH, u=4)[:, :, c, :, :]
            return e.dma_start(out=dst, in_=pst8[:, :])
        th.append(TH("SY", bnc, reads=[("pst8",)], writes=[("scr1", c)], sem="sc1"))

        def rp1(e, c=c):
            return e.dma_start(out=m1rep[c % NRP][:, :],
                               in_=scr[0:1, cc(c)].partition_broadcast(128))
        th.append(TH("SY", rp1, reads=[("scr1", c)], writes=[("m1rep", c % NRP)], sem=f"rpa{c % NRP}"))

        def rp2(e, c=c):
            return e.dma_start(out=i1rep[c % NRP][:, :],
                               in_=scr[0:1, T + c * CH:T + (c + 1) * CH].partition_broadcast(128))
        th.append(TH("SY", rp2, reads=[("scr1", c)], writes=[("i1rep", c % NRP)], sem=f"rpb{c % NRP}"))
        return th

    def normalize_thunks(c):
        th = []
        for b in range(NB):
            def ld(e, b=b, c=c):
                return e.dma_start(out=xr[b % 2][:, :],
                                   in_=xT[b * 128:(b + 1) * 128, cc(c)])
            th.append(TH("SY", ld, writes=[("xr", b % 2)], sem=f"xr{b % 2}"))
            th.append(TH("V", lambda e, b=b, c=c: e.tensor_sub(
                out=xc1[b % NXC][:, :], in0=xr[b % 2][:, :], in1=m1rep[c % NRP][:, :]),
                reads=[("xr", b % 2), ("m1rep", c % NRP)], writes=[("xc1", b % NXC)]))
            th.append(TH("V", lambda e, b=b, c=c: e.tensor_mul(
                out=xs1[c % NXS][b][:, :], in0=xc1[b % NXC][:, :], in1=i1rep[c % NRP][:, :]),
                reads=[("xc1", b % NXC), ("i1rep", c % NRP)], writes=[("xs1", c % NXS, b)]))
        return th

    def emit_compute1(c, post_q=None):
        for _ in range(4):
            inject(2)
        for db in range(NB):
            inject(2)
            ensure(("wdst", id(wq_h), NB - 1))
            pb = pmain[db % 2]

            def qmm(e, db=db, c=c, pb=pb):
                for kb in range(NB):
                    e.matmul(out=pb[:, :], lhsT=wq_h[kb][:, db * 128:(db + 1) * 128],
                             rhs=xs1[c % NXS][kb][:, :], start=(kb == 0), stop=False)
                return e.matmul(out=pb[:, :], lhsT=bq_rh[0:1, db * 128:(db + 1) * 128],
                                rhs=onesr_h[:, :], start=False, stop=True)
            sc.emit("T", qmm,
                    reads=[("xs1", c % NXS, kb) for kb in range(NB)] +
                          [("wdst", id(wq_h), kb) for kb in range(NB)] +
                          [("bq_rh",), ("onesr_h",)],
                    writes=[("pA", db % 2)])
            sc.emit("S", lambda e, pb=pb, db=db: e.activation(
                out=ebuf[db % 2][:, :], in_=pb[:, :], func=AF.Exp, bias=0.0, scale=1.0),
                reads=[("pA", db % 2)], writes=[("ebuf", db % 2)])
            sc.emit("V", lambda e, db=db: e.tensor_scalar_min(
                out=e1buf[db % 2][:, :], in0=ebuf[db % 2][:, :], scalar1=1.0),
                reads=[("ebuf", db % 2)], writes=[("ebuf", db % 2)])
            sc.emit("V", lambda e, pb=pb, db=db, c=c: e.scalar_tensor_tensor(
                out=qp[db][:, cc(c)], in0=pb[:, :], scalar=0.0, in1=e1buf[db % 2][:, :],
                op0=ALU.max, op1=ALU.add),
                reads=[("pA", db % 2), ("ebuf", db % 2)], writes=[("qp", db, c)])
        if post_q:
            bg.extend(post_q)
        ensure(("wdst", id(wd_h), NB - 1))
        for rb in range(2):
            pb = pmain[rb % 2]

            def kv1mm(e, rb=rb, c=c, pb=pb):
                last = None
                for kb in range(NB):
                    last = e.matmul(out=pb[:, :], lhsT=wd_h[kb][:, rb * 128:(rb + 1) * 128],
                                    rhs=xs1[c % NXS][kb][:, :], start=(kb == 0), stop=(kb == NB - 1))
                return last
            sc.emit("T", kv1mm,
                    reads=[("xs1", c % NXS, kb) for kb in range(NB)] +
                          [("wdst", id(wd_h), kb) for kb in range(NB)],
                    writes=[("pA", rb % 2)])
            sc.emit("V", lambda e, pb=pb, rb=rb, c=c: e.tensor_scalar(
                out=kv1[0][rb][:, :], in0=pb[:, :], scalar1=bd_cf[:, rb:rb + 1],
                scalar2=None, op0=ALU.add),
                reads=[("pA", rb % 2), ("bd",)], writes=[("kv1", 0, rb)])
        ensure(("wdst", id(wu_h), 1))
        for ti in range(4):
            i = c * 4 + ti
            kb_i = i % 2
            kvv = kvsb[kb_i][:].rearrange("p (g n) -> p g n", g=32, n=65)
            for fc in range(4):
                inject(2)
                pk = pkv[fc % 2]
                if fc < 2:
                    def kvmm(e, ti=ti, c=c, fc=fc, pk=pk):
                        for rb in range(2):
                            e.matmul(out=pk[:, :], lhsT=kv1[0][rb][:, ti * 128:(ti + 1) * 128],
                                     rhs=wu_h[rb][:, fc * CH:(fc + 1) * CH], start=(rb == 0), stop=False)
                        return e.matmul(out=pk[:, :], lhsT=onesr_h[0:1, 0:128],
                                        rhs=bu_rh[0:1, fc * CH:(fc + 1) * CH], start=False, stop=True)
                else:
                    def kvmm(e, ti=ti, c=c, fc=fc, pk=pk):
                        last = None
                        for rb in range(2):
                            last = e.matmul(out=pk[:, :], lhsT=kv1[0][rb][:, ti * 128:(ti + 1) * 128],
                                            rhs=wu_h[rb][:, fc * CH:(fc + 1) * CH],
                                            start=(rb == 0), stop=(rb == 1))
                        return last
                sc.emit("T", kvmm,
                        reads=[("kv1", 0, 0), ("kv1", 0, 1),
                               ("wdst", id(wu_h), 0), ("wdst", id(wu_h), 1),
                               ("onesr_h",), ("bu_rh",)],
                        writes=[("pC", fc % 2)])
                if fc < 2:
                    dst = kvv[:, fc * 8:(fc + 1) * 8, 0:64]
                    sc.emit("S", lambda e, pk=pk, fc=fc: e.activation(
                        out=ebuf[fc % 2][:, :], in_=pk[:, :], func=AF.Exp, bias=0.0, scale=1.0),
                        reads=[("pC", fc % 2)], writes=[("ebuf", fc % 2)])
                    sc.emit("V", lambda e, fc=fc: e.tensor_scalar_min(
                        out=e1buf[fc % 2][:, :], in0=ebuf[fc % 2][:, :], scalar1=1.0),
                        reads=[("ebuf", fc % 2)], writes=[("ebuf", fc % 2)])
                    sc.emit("V", lambda e, pk=pk, dst=dst, fc=fc: e.scalar_tensor_tensor(
                        out=dst, in0=pk[:, :], scalar=0.0, in1=e1buf[fc % 2][:, :],
                        op0=ALU.max, op1=ALU.add),
                        reads=[("pC", fc % 2), ("ebuf", fc % 2), ("kvones", kb_i)],
                        writes=[("kvk", kb_i, fc)])
                else:
                    dst = kvv[:, 16 + (fc - 2) * 8:16 + (fc - 1) * 8, 0:64]
                    sc.emit("V", lambda e, pk=pk, dst=dst, fc=fc: e.tensor_add(
                        out=dst, in0=pk[:, :], in1=burep[fc - 2][:, :]),
                        reads=[("pC", fc % 2), ("burep", fc - 2), ("kvones", kb_i)],
                        writes=[("kvv", kb_i, fc - 2)])

            def ctxmm(e, i=i, kb_i=kb_i):
                last = None
                for h in range(H):
                    first_in_bank = h in (0, 7, 14)
                    last = e.matmul(
                        out=ctx_ps(h),
                        lhsT=kvsb[kb_i][:, h * 65:h * 65 + 64],
                        rhs=kvsb[kb_i][:, 1040 + h * 65:1040 + (h + 1) * 65],
                        start=(i == 0 and first_in_bank), stop=(i == NT - 1),
                        skip_group_check=True)
                return last
            sc.emit("T", ctxmm,
                    reads=[("kvk", kb_i, 0), ("kvk", kb_i, 1),
                           ("kvv", kb_i, 0), ("kvv", kb_i, 1)],
                    writes=[("kvsb_free", kb_i)] + ([("ctx_acc",)] if i == NT - 1 else []))

    w_wave2 = load_weight(wo, wo_h, accum=[rwo[:, k:k + 1] for k in range(NB)])
    w_wave2.append(TH("V", lambda e: e.tensor_copy(out=rwo_h[:, :], in_=rwo[:, :]),
                      reads=[("wdst", id(wo_h), k) for k in range(NB)], writes=[("rwo_h",)]))

    s0 = stats1_head(0, q="SY") + stats1_tail(0) + stats1_head(1, q="SY")
    bg.extend(w_wq)
    while s0 or bg:
        if s0:
            s0.pop(0)()
        inject(1)
    for t in normalize_thunks(0):
        t()
    bg.extend(stats1_tail(1))
    bg.extend(normalize_thunks(1))
    bg.extend(w_wave1)
    for c in range(NCH):
        if c + 2 < NCH:
            bg.extend(stats1_head(c + 2))
        emit_compute1(c, post_q=(w_wave2 if c == NCH - 1 else None))
        if c + 2 < NCH:
            bg.extend(stats1_tail(c + 2))
            bg.extend(normalize_thunks(c + 2))
    drain_bg()


    for h in range(H):
        sc.emit("S", lambda e, h=h: e.activation(out=ctxsb_sl(h),
                                                 in_=ctx_ps(h)[0:64, 0:65],
                                                 func=AF.Copy, bias=0.0, scale=1.0),
                reads=[("ctx_acc",)], writes=[("ctxsb", h)])
    sc.emit("V", lambda e: e.memset(zlhs[:, :], 0.0), writes=[("zlhs_z",)])
    for kb in range(NB):
        for j in range(2):
            h = 2 * kb + j
            sc.emit("V", lambda e, kb=kb, j=j, h=h: e.tensor_copy(
                out=zlhs[j * 64:j * 64 + 64, kb * 16 + h:kb * 16 + h + 1],
                in_=ctxsb_sl(h, 65)[:, 64:65]),
                reads=[("ctxsb", h), ("zlhs_z",)], writes=[("zlhs", h)])

    def a1z_thunks(c):
        th = []

        def zmm(e, c=c):
            last = None
            for kb in range(NB):
                last = e.matmul(out=pmisc[0:16, :], lhsT=zlhs[:, kb * 16:(kb + 1) * 16],
                                rhs=qp[kb][:, cc(c)], start=(kb == 0), stop=(kb == NB - 1),
                                skip_group_check=True)
            return last
        th.append(TH("T", zmm, reads=[("zlhs", h) for h in range(H)] +
                     [("qp", kb, c) for kb in range(NB)], writes=[("pzdot",)]))
        th.append(TH("V", lambda e, c=c: e.tensor_scalar_add(
            out=zdsb[:, :], in0=pmisc[0:16, :], scalar1=1e-6),
            reads=[("pzdot",)], writes=[("zdsb",)]))
        th.append(TH("S", lambda e, c=c: e.activation(out=zdsb[:, :], in_=zdsb[:, :],
                                                      func=AF.Ln, bias=0.0, scale=1.0),
                     reads=[("zdsb",)], writes=[("zdsb",)]))
        th.append(TH("V", lambda e, c=c: e.tensor_scalar_mul(out=zdsb[:, :], in0=zdsb[:, :],
                                                             scalar1=-1.0),
                     reads=[("zdsb",)], writes=[("zdsb",)]))
        th.append(TH("S", lambda e, c=c: e.activation(out=zdsb[:, :], in_=zdsb[:, :],
                                                      func=AF.Exp, bias=0.0, scale=1.0),
                     reads=[("zdsb",)], writes=[("zdsb",)]))
        th.append(TH("V", lambda e: e.tensor_scalar_mul(out=z16[:, :], in0=zdsb[:, :],
                                                        scalar1=1024.0),
                     reads=[("zdsb",)], writes=[("z16",)]))
        th.append(TH("SY", lambda e, c=c: e.dma_start(out=zscr[:, cc(c)], in_=z16[:, :]),
                     reads=[("z16",)], writes=[("zscr", c)], sem="zsc"))
        return th

    def zb_thunks(c):
        th = []
        for p in range(PAIRS):
            th.extend(zrep_bcast(c, p))
        return th

    def zrep_bcast(c, p):
        def za(e, p=p, c=c):
            return e.dma_start(out=zreps[c % 2][p][0:64, :],
                               in_=zscr[2 * p:2 * p + 1, cc(c)].partition_broadcast(64))

        def zb(e, p=p, c=c):
            return e.dma_start(out=zreps[c % 2][p][64:128, :],
                               in_=zscr[2 * p + 1:2 * p + 2, cc(c)].partition_broadcast(64))
        return [TH("SY", za, reads=[("zscr", c)],
                   writes=[("zrep", c % 2, p)], sem=f"zr{p}a{c % 2}"),
                TH("SY", zb, reads=[("zscr", c)],
                   writes=[("zrep2", c % 2, p)], sem=f"zr{p}b{c % 2}")]

    def a1o_thunks(c):
        th = []
        for p in range(PAIRS):
            pb = pmain[p % 2]

            def outmm(e, p=p, c=c, pb=pb):
                e.matmul(out=pb[0:64, :], lhsT=ctxsb_sl(2 * p, 64),
                         rhs=qp[p][0:64, cc(c)], start=True, stop=True)
                return e.matmul(out=pb[64:128, :], lhsT=ctxsb_sl(2 * p + 1, 64),
                                rhs=qp[p][64:128, cc(c)], start=True, stop=True)
            th.append(TH("T", outmm, reads=[("ctxsb", 2 * p), ("ctxsb", 2 * p + 1), ("qp", p, c)],
                         writes=[("pA", p % 2)]))
            th.append(TH("S", lambda e, p=p, pb=pb: e.activation(
                out=oev[p % 4][:, :], in_=pb[:, :], func=AF.Identity,
                bias=0.0, scale=1.0 / 1024.0),
                reads=[("pA", p % 2), ("cvals",)], writes=[("oev", p % 4)]))
            th.append(TH("GP", lambda e, p=p, c=c: e.tensor_mul(
                out=outz[c % NOZ][p][:, :], in0=oev[p % 4][:, :], in1=zreps[c % 2][p][:, :]),
                reads=[("oev", p % 4), ("zrep", c % 2, p), ("zrep2", c % 2, p)],
                writes=[("outz", c % NOZ, p)], sem="GP", inc=1))
        return th

    def a2_thunks(c):
        ensure(("wdst", id(wo_h), NB - 1))
        th = []
        for db in range(NB):
            pb = pmain[db % 2]

            def womm(e, db=db, c=c, pb=pb):
                last = None
                for kb in range(NB):
                    last = e.matmul(out=pb[:, :], lhsT=wo_h[kb][:, db * 128:(db + 1) * 128],
                                    rhs=outz[c % NOZ][kb][:, :], start=(kb == 0), stop=(kb == NB - 1))
                return last
            th.append(TH("T", womm,
                         reads=[("outz", c % NOZ, kb) for kb in range(NB)] +
                               [("wdst", id(wo_h), kb) for kb in range(NB)],
                         writes=[("pA", db % 2)]))

            def xld(e, db=db, c=c):
                return e.dma_start(out=xr[db % 2][:, :],
                                   in_=xT[db * 128:(db + 1) * 128, cc(c)])
            th.append(TH("SY", xld, writes=[("xr", db % 2)], sem=f"xr{db % 2}"))
            th.append(TH("V", lambda e, pb=pb, db=db, c=c: e.scalar_tensor_tensor(
                out=x2[c % NX2][db][:, :], in0=pb[:, :], scalar=bo_cf[:, db:db + 1],
                in1=xr[db % 2][:, :], op0=ALU.add, op1=ALU.add),
                reads=[("pA", db % 2), ("xr", db % 2), ("bo",)],
                writes=[("x2", c % NX2, db)]))
        return th

    def c1_thunks(c):
        th = []

        def sdmm(e, c=c):
            for kb in range(NB):
                e.matmul(out=pmisc[32:33, :], lhsT=rwo_h[:, kb:kb + 1],
                         rhs=outz[c % NOZ][kb][:, :], start=(kb == 0), stop=False,
                         skip_group_check=True)
            return e.matmul(out=pmisc[32:33, :], lhsT=sbo_h[:, :], rhs=onesr_h[:, :],
                            start=False, stop=True, skip_group_check=True)
        th.append(TH("T", sdmm, reads=[("outz", c % NOZ, kb) for kb in range(NB)] +
                     [("rwo_h",), ("sbo_h",), ("onesr_h",)], writes=[("psum_sx",)]))
        for db in range(NB):
            th.append(TH("S", lambda e, db=db, c=c: e.activation(
                out=xsq2[db % 2][:, :], in_=x2[c % NX2][db][:, :], func=AF.Square,
                bias=0.0, scale=1.0),
                reads=[("x2", c % NX2, db)], writes=[("xsq2", db % 2)]))
            th.append(TH("T", lambda e, db=db: e.matmul(
                out=pmisc[64:65, :], lhsT=ones_h[:, 0:1], rhs=xsq2[db % 2][:, :],
                start=(db == 0), stop=(db == NB - 1), skip_group_check=True),
                reads=[("xsq2", db % 2), ("ones_h",)], writes=[("psum_sq",)]))
        th.append(TH("SY", lambda e, c=c: e.dma_start(out=m1c[:, :], in_=scr[0:1, cc(c)]),
                     reads=[("scr1", c)], writes=[("m1c",)], sem="m1c"))
        th.append(TH("V", lambda e: e.tensor_scalar_mul(out=trow[0][:, :], in0=pmisc[32:33, :],
                                                        scalar1=1.0 / D),
                     reads=[("psum_sx",)], writes=[("trow0",)]))
        th.append(TH("V", lambda e, c=c: e.tensor_add(out=m2row[:, :], in0=m1c[:, :],
                                                      in1=trow[0][:, :]),
                     reads=[("trow0",), ("m1c",)], writes=[("m2row",)]))
        th.append(TH("V", lambda e: e.tensor_scalar_mul(out=trow[0][:, :], in0=pmisc[64:65, :],
                                                        scalar1=1.0 / D),
                     reads=[("psum_sq",)], writes=[("trow0",)]))
        th.append(TH("S", lambda e, c=c: e.activation(out=trow[1][:, :], in_=m2row[:, :],
                                                      func=AF.Square, bias=0.0, scale=1.0),
                     reads=[("m2row",)], writes=[("trow1",)]))
        th.append(TH("V", lambda e: e.scalar_tensor_tensor(out=trow[0][:, :], in0=trow[0][:, :],
                                                           scalar=1e-5, in1=trow[1][:, :],
                                                           op0=ALU.add, op1=ALU.subtract),
                     reads=[("trow0",), ("trow1",)], writes=[("trow0",)]))
        th.append(TH("S", lambda e: e.activation(out=trow[1][:, :], in_=trow[0][:, :],
                                                 func=AF.Ln, bias=0.0, scale=1.0),
                     reads=[("trow0",)], writes=[("trow1",)]))
        th.append(TH("S", lambda e, c=c: e.activation(out=i2row[:, :], in_=trow[1][:, :],
                                                      func=AF.Exp, bias=0.0, scale=-0.5),
                     reads=[("trow1",), ("cvals",)], writes=[("i2row",)]))

        def bnc(e, c=c):
            return e.dma_start(out=scr[1:2, :].rearrange("a (g t) -> a g t", g=2)[:, :, cc(c)],
                               in_=rows2[0:1, :].rearrange("a (g t) -> a g t", g=2))
        th.append(TH("SY", bnc, reads=[("m2row",), ("i2row",)], writes=[("scr2", c)], sem="sc2"))

        def rp1(e, c=c):
            return e.dma_start(out=m2rep[c % NRP][:, :],
                               in_=scr[1:2, cc(c)].partition_broadcast(128))
        th.append(TH("SY", rp1, reads=[("scr2", c)], writes=[("m1rep", c % NRP)], sem=f"rpa{c % NRP}"))

        def rp2(e, c=c):
            return e.dma_start(out=i2rep[c % NRP][:, :],
                               in_=scr[1:2, T + c * CH:T + (c + 1) * CH].partition_broadcast(128))
        th.append(TH("SY", rp2, reads=[("scr2", c)], writes=[("i1rep", c % NRP)], sem=f"rpb{c % NRP}"))
        return th

    def c2a_thunks(c):
        ensure(("wdst", id(wf8), NB - 1))
        th = []
        for db in range(NB):
            th.append(TH("V", lambda e, db=db, c=c: e.tensor_sub(
                out=xc2[db % NXC][:, :], in0=x2[c % NX2][db][:, :], in1=m2rep[c % NRP][:, :]),
                reads=[("x2", c % NX2, db), ("m1rep", c % NRP)], writes=[("xc1", db % NXC)]))
            th.append(TH("V", lambda e, db=db, c=c: e.tensor_mul(
                out=h2q[c % NH2][db][:, :], in0=xc2[db % NXC][:, :], in1=i2rep[c % NRP][:, :]),
                reads=[("xc1", db % NXC), ("i1rep", c % NRP)], writes=[("h2q", c % NH2, db)]))
            th.append(TH("T", lambda e, kb=db, c=c: e.matmul(
                out=pkv[0][:, :], lhsT=wf8[kb][:, 0:128],
                rhs=h2q[c % NH2][kb][:, :], start=(kb == 0), stop=(kb == NB - 1)),
                reads=[("h2q", c % NH2, db), ("wdst", id(wf8), db)],
                writes=[("pC", 0)]))
        return th

    def c2b_thunks(c):
        th = []
        for fb in range(NB):
            pb = pkv[fb % 2]
            if fb > 0:
                def ffnmm(e, fb=fb, c=c, pb=pb):
                    last = None
                    for kb in range(NB):
                        last = e.matmul(out=pb[:, :], lhsT=wf8[kb][:, fb * 128:(fb + 1) * 128],
                                        rhs=h2q[c % NH2][kb][:, :], start=(kb == 0), stop=(kb == NB - 1))
                    return last
                th.append(TH("T", ffnmm,
                             reads=[("h2q", c % NH2, kb) for kb in range(NB)] +
                                   [("wdst", id(wf8), kb) for kb in range(NB)],
                             writes=[("pC", fb % 2)]))
            th.append(TH("V", lambda e, pb=pb, fb=fb, c=c: e.scalar_tensor_tensor(
                out=x2[c % NX2][fb][:, :], in0=pb[:, :], scalar=bf_cf[:, fb:fb + 1],
                in1=x2[c % NX2][fb][:, :], op0=ALU.add, op1=ALU.add),
                reads=[("pC", fb % 2)], writes=[("x2", c % NX2, fb)]))

            def yst(e, fb=fb, c=c):
                return e.dma_start(out=yT[fb * 128:(fb + 1) * 128, cc(c)],
                                   in_=x2[c % NX2][fb][:, :])
            th.append(TH("GP", yst, reads=[("x2", c % NX2, fb)],
                         writes=[("ydma", c, fb)], sem=f"yd{fb}_{c % 2}"))
        return th

    def run_all(th):
        for t in th:
            t()

    def interleave(l1, l2):
        i = j = 0
        while i < len(l1) or j < len(l2):
            if i < len(l1):
                l1[i]()
                i += 1
            if j < len(l2):
                l2[j]()
                j += 1

    run_all(a1z_thunks(0))
    run_all(zb_thunks(0))
    interleave(a1z_thunks(1) if NCH > 1 else [], a1o_thunks(0))
    run_all(a2_thunks(0))
    for c in range(NCH):
        if c + 1 < NCH:
            run_all(zb_thunks(c + 1))
        ahead = a1z_thunks(c + 2) if c + 2 < NCH else []
        interleave(ahead, c1_thunks(c))
        a1o = a1o_thunks(c + 1) if c + 1 < NCH else []
        interleave(c2a_thunks(c), a1o)
        run_all(c2b_thunks(c))
        if c + 1 < NCH:
            run_all(a2_thunks(c + 1))

    sems = {}
    with ExitStack() as ses:
        for name in sc.sem_names:
            sems[name] = ses.enter_context(nc.semaphore(f"sem{name}"))
        block = ses.enter_context(nc.Block())

        @block.tensor
        def _(eng):
            sc.replay("T", eng, sems)

        @block.vector
        def _(eng):
            sc.replay("V", eng, sems)

        @block.scalar
        def _(eng):
            sc.replay("S", eng, sems)

        @block.sync
        def _(eng):
            sc.replay("SY", eng, sems)

        @block.gpsimd
        def _(eng):
            sc.replay("GP", eng, sems)
            for fb in range(NB):
                for par in range(2):
                    eng.wait_ge(sems[f"yd{fb}_{par}"], sc.count[f"yd{fb}_{par}"])
    es.close()
    return nc


_NC_CACHE = None


def kernel(L0_attn_x, L0_attn_Wq, L0_attn_bq, L0_attn_Wd, L0_attn_bd,
           L0_attn_Wu, L0_attn_bu, L0_attn_Wo, L0_attn_bo,
           L0_ffn_weight, L0_ffn_bias):
    global _NC_CACHE
    x = np.asarray(L0_attn_x, dtype=np.float32)
    B = x.shape[0]
    f32 = lambda a: np.ascontiguousarray(np.asarray(a, dtype=np.float32))
    kidx = np.concatenate([np.arange(h * 128, h * 128 + 64) for h in range(H)])
    vidx = np.concatenate([np.arange(h * 128 + 64, h * 128 + 128) for h in range(H)])
    perm = np.concatenate([kidx, vidx])
    common = {
        "wq": f32(L0_attn_Wq),
        "wd": f32(L0_attn_Wd),
        "wu": f32(np.asarray(L0_attn_Wu, dtype=np.float32)[:, perm]),
        "wo": f32(L0_attn_Wo),
        "wf": f32(np.asarray(L0_ffn_weight, dtype=np.float32).T),
        "bq_r": f32(np.asarray(L0_attn_bq, dtype=np.float32).reshape(1, D)),
        "bd_c": f32(np.asarray(L0_attn_bd, dtype=np.float32).reshape(2, 128).T),
        "bu_r": f32(np.asarray(L0_attn_bu, dtype=np.float32)[perm].reshape(1, 2 * D)),
        "bo_c": f32(np.asarray(L0_attn_bo, dtype=np.float32).reshape(NB, 128).T),
        "bf_c": f32(np.asarray(L0_ffn_bias, dtype=np.float32).reshape(NB, 128).T),
        "ones_in": np.ones((128, 128), dtype=np.float32),
        "onesr_in": np.ones((1, CH), dtype=np.float32),
        "cvals_in": np.array([[-0.5, 1.0 / D, 1e-5, -1.0, 1.0 / 1024.0]], dtype=np.float32),
        "ident_in": np.eye(128, dtype=np.float32),
    }
    in_maps = []
    for b in range(B):
        m = dict(common)
        m["xT"] = np.ascontiguousarray(x[b].T)
        m["xtok_d"] = np.ascontiguousarray(x[b])
        in_maps.append(m)
    if _NC_CACHE is None:
        _NC_CACHE = build_nc()
    res = run_bass_kernel_spmd(_NC_CACHE, in_maps, core_ids=list(range(B)))
    out = np.stack([np.ascontiguousarray(r["yT"].T) for r in res.results], axis=0)
    return out


# version 22
# speedup vs baseline: 1.0757x; 1.0406x over previous
import numpy as np
from contextlib import ExitStack
import concourse.bass as bass
import concourse.mybir as mybir
from concourse.bass_utils import run_bass_kernel_spmd

F32 = mybir.dt.float32
F16 = mybir.dt.float16
FP8 = mybir.dt.float8e4
AF = mybir.ActivationFunctionType
ALU = mybir.AluOpType

T, D, H, DH, RK = 2048, 1024, 16, 64, 256
CH = 512
NCH = T // CH
NB = D // 128
NT = T // 128
PAIRS = H // 2
NXCH, NXS, NX2, NOZ, NH2, NRP, NXC = 1, 2, 2, 1, 1, 2, 1


class Sched:

    ENGS = ("T", "V", "S", "SY", "GP")
    ONE_WAVE = ("cst", "cstg")

    def __init__(self):
        self.steps = {e: [] for e in self.ENGS}
        self.count = {}
        self.writer = {}
        self.readers = {}
        self.watermark = {e: {} for e in self.ENGS}
        self.sem_names = []

    def _bump(self, sem, inc):
        if sem not in self.count:
            self.count[sem] = 0
            self.sem_names.append(sem)
        self.count[sem] += inc
        return self.count[sem]

    def emit(self, eng, fn, reads=(), writes=(), sem=None, inc=None):
        if eng in ("SY", "GP") and inc != 1:
            assert sem is not None, "DMA steps need a sem stream name"
            inc = 16
        else:
            sem = sem or eng
            inc = 1
        waits = {}
        def add(w):
            if w is None:
                return
            ws, wv = w
            if ws == "T" and eng == "T":
                return
            waits[ws] = max(waits.get(ws, 0), wv)
        for t in reads:
            add(self.writer.get(t))
        for t in writes:
            add(self.writer.get(t))
            for r in self.readers.get(t, ()):
                add(r)
        myval = self._bump(sem, inc)
        final_waits = []
        for ws, wv in sorted(waits.items()):
            if ws == sem and eng not in ("SY", "GP"):
                pass
            if self.watermark[eng].get(ws, 0) < wv:
                final_waits.append((ws, wv))
                self.watermark[eng][ws] = wv
        self.steps[eng].append((final_waits, fn, sem, inc))
        for t in writes:
            self.writer[t] = (sem, myval)
            self.readers[t] = []
        for t in reads:
            self.readers.setdefault(t, []).append((sem, myval))

    def replay(self, eng_name, eng_obj, sems):
        for (waits, fn, sem, inc) in self.steps[eng_name]:
            for (ws, wv) in waits:
                if ws in self.ONE_WAVE:
                    wv = self.count[ws]
                eng_obj.wait_ge(sems[ws], wv)
            last = fn(eng_obj)
            assert last is not None, "step fn must return its last instruction"
            last.then_inc(sems[sem], inc)


def build_nc():
    nc = bass.Bass("TRN2", target_bir_lowering=False)
    dt = nc.dram_tensor
    xT = dt("xT", [D, T], F32, kind="ExternalInput").ap()
    xtok_d = dt("xtok_d", [T, D], F32, kind="ExternalInput").ap()
    ident_in = dt("ident_in", [128, 128], F32, kind="ExternalInput").ap()
    wq = dt("wq", [D, D], F32, kind="ExternalInput").ap()
    wd = dt("wd", [D, RK], F32, kind="ExternalInput").ap()
    wu = dt("wu", [RK, 2 * D], F32, kind="ExternalInput").ap()
    wo = dt("wo", [D, D], F32, kind="ExternalInput").ap()
    wf = dt("wf", [D, D], F32, kind="ExternalInput").ap()
    bq_r = dt("bq_r", [1, D], F32, kind="ExternalInput").ap()
    bd_c = dt("bd_c", [128, 2], F32, kind="ExternalInput").ap()
    bu_r = dt("bu_r", [1, 2 * D], F32, kind="ExternalInput").ap()
    bo_c = dt("bo_c", [128, NB], F32, kind="ExternalInput").ap()
    bf_c = dt("bf_c", [128, NB], F32, kind="ExternalInput").ap()
    ones_in = dt("ones_in", [128, 128], F32, kind="ExternalInput").ap()
    onesr_in = dt("onesr_in", [1, CH], F32, kind="ExternalInput").ap()
    cvals_in = dt("cvals_in", [1, 5], F32, kind="ExternalInput").ap()
    scr = dt("scr", [2, 2 * T], F32, kind="Internal").ap()
    zscr = dt("zscr", [H, T], F16, kind="Internal").ap()
    buscr = dt("buscr", [1, 2 * D], F16, kind="Internal").ap()
    yT = dt("yT", [D, T], F32, kind="ExternalOutput").ap()

    es = ExitStack()
    sb = lambda name, shape, d: es.enter_context(nc.sbuf_tensor(name, shape, d))

    wstage = [sb(f"wstage{i}", [128, D], F32) for i in range(2)]
    wq_h = [sb(f"wq_h{k}", [128, D], F16) for k in range(NB)]
    wd_h = [sb(f"wd_h{k}", [128, RK], F16) for k in range(NB)]
    wu_h = [sb(f"wu_h{k}", [128, 2 * D], F16) for k in range(2)]
    wo_h = wq_h
    wf8 = [sb(f"wf8_{k}", [128, D], FP8) for k in range(NB)]
    rwo = sb("rwo", [128, NB], F32)
    rwo_h = sb("rwo_h", [128, NB], F16)
    bq_rf = sb("bq_rf", [1, D], F32)
    bq_rh = sb("bq_rh", [1, D], F16)
    bu_rf = sb("bu_rf", [1, 2 * D], F32)
    bu_rh = sb("bu_rh", [1, 2 * D], F16)
    bd_cf = sb("bd_cf", [128, 2], F32)
    bo_cf = sb("bo_cf", [128, NB], F32)
    bf_cf = sb("bf_cf", [128, NB], F32)
    cvals = sb("cvals", [1, 5], F32)
    ones_f = sb("ones_f", [128, 128], F32)
    ones_h = sb("ones_h", [128, 128], F16)
    onesr_f = sb("onesr_f", [1, CH], F32)
    onesr_h = sb("onesr_h", [1, CH], F16)
    burep = [sb(f"burep{i}", [128, CH], F16) for i in range(2)]

    xtok = [sb(f"xtok{i}", [128, D], F32) for i in range(2)]
    scol = [sb(f"scol{i}", [128, 8], F32) for i in range(2)]
    pst8 = sb("pst8", [8, 128], F32)
    msq4t = sb("msq4t", [128, 4], F32)
    ident = sb("ident", [128, 128], F32)
    m1rep = [sb(f"m1rep{i}", [128, CH], F32) for i in range(NRP)]
    i1rep = [sb(f"i1rep{i}", [128, CH], F32) for i in range(NRP)]
    m2rep = m1rep
    i2rep = i1rep
    xc1 = [sb(f"xc1_{i}", [128, CH], F32) for i in range(NXC)]
    xs1 = [[sb(f"xs1_{i}_{b}", [128, CH], F16) for b in range(NB)] for i in range(NXS)]
    qp = [sb(f"qp{b}", [128, T], F16) for b in range(NB)]
    ebuf = [sb(f"ebuf{i}", [128, CH], F16) for i in range(2)]
    e1buf = ebuf
    kv1 = [[sb(f"kv1_{i}_{r}", [128, CH], F16) for r in range(2)] for i in range(1)]
    kvsb = [sb(f"kvsb{i}", [128, 2 * 1040], F16) for i in range(2)]
    ctxsb = sb("ctxsb", [128, PAIRS * 65], F16)
    bo16 = sb("bo16", [128, NB], F16)
    sbo = sb("sbo", [1, 1], F32)
    sbo_h = sb("sbo_h", [1, 1], F16)
    zlhs = sb("zlhs", [128, NB * 16], F16)
    zdsb = sb("zdsb", [16, CH], F32)
    z16 = sb("z16", [16, CH], F16)
    oev = [sb(f"oev{i}", [128, CH], F16) for i in range(4)]
    rows2 = sb("rows2", [1, 2 * CH], F32)
    m2row = rows2[0:1, 0:CH]
    i2row = rows2[0:1, CH:2 * CH]
    m1c = sb("m1c", [1, CH], F32)
    trow = [sb(f"trow{i}", [1, CH], F32) for i in range(2)]
    trow = trow + trow
    zrep = [sb(f"zrep{i}", [128, CH], F16) for i in range(8)]
    zrv = []
    for nm, hostt in (("zrv0", wu_h[0]), ("zrv1", wu_h[1])):
        vt = nc.virtual_sbuf_tensor(nm, [128, 4 * CH], F16)
        nc.back_virtual_tensor_with(vt, hostt)
        zrv.append(vt)
    zreps = [zrep, [zrv[p // 4][:, (p % 4) * CH:(p % 4 + 1) * CH] for p in range(8)]]
    outz = [[sb(f"outz{i}_{b}", [128, CH], F16) for b in range(NB)] for i in range(NOZ)]
    xr = [sb(f"xr{i}", [128, CH], F32) for i in range(2)]
    x2 = [[sb(f"x2_0_{b}", [128, CH], F32) for b in range(NB)]]
    x2v = []
    for nm, hostt in (("x2v0", wstage[0]), ("x2v1", wstage[1]),
                      ("x2v2", xtok[0]), ("x2v3", xtok[1])):
        vt = nc.virtual_sbuf_tensor(nm, [128, 2 * CH], F32)
        nc.back_virtual_tensor_with(vt, hostt)
        x2v.append(vt)
    x2.append([x2v[b // 2][:, (b % 2) * CH:(b % 2 + 1) * CH] for b in range(NB)])
    xsq2 = [sb(f"xsq2_{i}", [128, CH], F16) for i in range(2)]
    xc2 = xc1
    h2q = [[sb(f"h2q{i}_{b}", [128, CH], FP8) for b in range(NB)] for i in range(NH2)]

    psum = lambda name, shape: es.enter_context(nc.psum_tensor(name, shape, F32))
    pmain = [psum("pm0", [128, CH]), psum("pm1", [128, CH])]
    pctx = [psum("pc0", [64, 7 * 65]), psum("pc1", [64, 7 * 65]),
            psum("pc2", [64, 2 * 65])]
    pmisc = psum("pmisc", [128, CH])
    pkv = [psum("pk0", [128, CH]), psum("pk1", [128, CH])]
    pst = pmisc[0:8, 0:128]

    def ctx_ps(h):
        if h < 7:
            return pctx[0][0:64, h * 65:(h + 1) * 65]
        if h < 14:
            return pctx[1][0:64, (h - 7) * 65:(h - 6) * 65]
        return pctx[2][0:64, (h - 14) * 65:(h - 13) * 65]

    def ctxsb_sl(h, ncol=65):
        r0 = (h % 2) * 64
        c0 = (h // 2) * 65
        return ctxsb[r0:r0 + 64, c0:c0 + ncol]

    sc = Sched()
    cc = lambda c: slice(c * CH, (c + 1) * CH)
    cc2 = lambda c: slice(c * 128, (c + 1) * 128)
    import collections as _co
    bg = _co.deque()

    def TH(eng, fn, reads=(), writes=(), sem=None, inc=None):
        return lambda: sc.emit(eng, fn, reads=reads, writes=writes, sem=sem, inc=inc)

    def inject(n=1):
        for _ in range(n):
            if not bg:
                return
            bg.popleft()()

    def drain_bg():
        while bg:
            bg.popleft()()

    def ensure(tok):
        while tok not in sc.writer and bg:
            bg.popleft()()

    def dma(q, key, out, in_, reads=(), writes=()):
        sc.emit(q, lambda e, o=out, i=in_: e.dma_start(out=o, in_=i),
                reads=reads, writes=writes, sem=key)

    dma("SY", "cst", cvals[:, :], cvals_in[:, :], writes=[("cvals",)])
    nc.const_aps.register(F32, -0.5, cvals[0:1, 0:1])
    nc.const_aps.register(F32, 1.0 / 1024.0, cvals[0:1, 4:5])
    dma("SY", "cst", ones_f[:, :], ones_in[:, :], writes=[("ones_f",)])
    dma("SY", "cst", ident[:, :], ident_in[:, :], writes=[("ident",)])
    dma("SY", "cst", onesr_f[:, :], onesr_in[:, :], writes=[("onesr_f",)])
    dma("SY", "cst", bq_rf[:, :], bq_r[:, :], writes=[("bq_rf",)])
    dma("SY", "cst", bd_cf[:, :], bd_c[:, :], writes=[("bd",)])
    dma("SY", "cst", bu_rf[:, :], bu_r[:, :], writes=[("bu_rf",)])
    dma("SY", "cst", bo_cf[:, :], bo_c[:, :], writes=[("bo",)])
    dma("SY", "cst", bf_cf[:, :], bf_c[:, :], writes=[("bf",)])

    sc.emit("V", lambda e: e.tensor_copy(out=ones_h[:, :], in_=ones_f[:, :]),
            reads=[("ones_f",)], writes=[("ones_h",)])
    sc.emit("V", lambda e: e.tensor_copy(out=onesr_h[:, :], in_=onesr_f[:, :]),
            reads=[("onesr_f",)], writes=[("onesr_h",)])
    sc.emit("V", lambda e: e.tensor_copy(out=bq_rh[:, :], in_=bq_rf[:, :]),
            reads=[("bq_rf",)], writes=[("bq_rh",)])
    sc.emit("V", lambda e: e.tensor_copy(out=bu_rh[:, :], in_=bu_rf[:, :]),
            reads=[("bu_rf",)], writes=[("bu_rh",)])
    dma("GP", "busc", buscr[0:1, :], bu_rh[0:1, :], reads=[("bu_rh",)], writes=[("buscr",)])
    dma("GP", "cstg", burep[0][:, :], buscr[0:1, D:D + CH].partition_broadcast(128),
        reads=[("buscr",)], writes=[("burep", 0)])
    dma("GP", "cstg", burep[1][:, :], buscr[0:1, D + CH:2 * D].partition_broadcast(128),
        reads=[("buscr",)], writes=[("burep", 1)])

    wi = [0]

    def load_weight(src_ap, dst_list, accum=None):
        th = []
        nblk = src_ap.shape[0] // 128
        cols = src_ap.shape[1]
        nh = (cols + D - 1) // D
        for k in range(nblk):
            for hh in range(nh):
                c0, c1 = hh * D, min((hh + 1) * D, cols)
                sidx = wi[0] % 2
                wi[0] += 1
                st = wstage[sidx]

                def dth(e, s=st, c0=c0, c1=c1, k=k, sa=src_ap):
                    return e.dma_start(out=s[:, 0:c1 - c0],
                                       in_=sa[k * 128:(k + 1) * 128, c0:c1])
                th.append(TH("SY", dth, reads=(), writes=[("wstage", sidx)],
                             sem=f"wst{sidx}"))
                acc = accum[k] if (accum is not None and hh == 0) else None

                def cast(e, s=st, d=dst_list[k], a=acc, c0=c0, c1=c1):
                    return e.activation(out=d[:, c0:c1], in_=s[:, 0:c1 - c0],
                                        func=AF.Copy, bias=0.0, scale=1.0, accum_out=a)
                th.append(TH("S", cast, reads=[("wstage", sidx)],
                             writes=[("wdst", id(dst_list), k)] if hh == nh - 1 else [("wdsth", id(dst_list), k)]))
        return th

    w_wq = load_weight(wq, wq_h)
    w_wave1 = (load_weight(wd, wd_h) + load_weight(wu, wu_h) + load_weight(wf, wf8))
    sc.emit("V", lambda e: e.tensor_copy(out=bo16[:, :], in_=bo_cf[:, :]),
            reads=[("bo",)], writes=[("bo16",)])
    sc.emit("T", lambda e: e.matmul(out=pmisc[32:33, 0:NB], lhsT=ones_h[:, 0:1],
                                    rhs=bo16[:, :], start=True, stop=True,
                                    skip_group_check=True),
            reads=[("bo16",), ("ones_h",)], writes=[("psum_sx",)])
    sc.emit("V", lambda e: e.reduce_sum(out=sbo[:, :], in_=pmisc[32:33, 0:NB],
                                        axis=mybir.AxisListType.X),
            reads=[("psum_sx",)], writes=[("sbo",)])
    sc.emit("V", lambda e: e.tensor_copy(out=sbo_h[:, :], in_=sbo[:, :]),
            reads=[("sbo",)], writes=[("sbo_h",)])
    for i in range(2):
        sc.emit("V", lambda e, i=i: e.memset(
            kvsb[i][:].rearrange("p (g n) -> p g n", g=32, n=65)[:, :, 64:65], 1.0),
            writes=[("kvones", i)])

    def stats1_head(c, q="GP"):
        th = []
        sc_ = scol[c % 2]
        for ti in range(4):
            def ld(e, ti=ti, c=c):
                return e.dma_start(out=xtok[ti % 2][:, :],
                                   in_=xtok_d[c * CH + ti * 128:c * CH + (ti + 1) * 128, :])
            pfx = "xtk" if q == "GP" else "xtsy"
            th.append(TH(q, ld, writes=[("xtok", ti % 2)], sem=f"{pfx}{ti % 2}"))
            th.append(TH("S", lambda e, ti=ti, sc_=sc_: e.activation(
                out=xtok[ti % 2][:, :], in_=xtok[ti % 2][:, :], func=AF.Copy,
                bias=0.0, scale=1.0, accum_out=sc_[:, ti:ti + 1]),
                reads=[("xtok", ti % 2)], writes=[("scolm", c % 2, ti), ("xtok", ti % 2)]))
            th.append(TH("S", lambda e, ti=ti, sc_=sc_: e.activation(
                out=xtok[ti % 2][:, :], in_=xtok[ti % 2][:, :], func=AF.Square,
                bias=0.0, scale=1.0, accum_out=sc_[:, 4 + ti:5 + ti]),
                reads=[("xtok", ti % 2)], writes=[("scolq", c % 2, ti), ("xtok", ti % 2)]))
        return th

    def stats1_tail(c):
        th = []
        sc_ = scol[c % 2]
        th.append(TH("V", lambda e, sc_=sc_: e.tensor_scalar_mul(out=sc_[:, 0:8], in0=sc_[:, 0:8],
                                                                scalar1=1.0 / D),
                     reads=[("scolm", c % 2, 0), ("scolm", c % 2, 1), ("scolm", c % 2, 2),
                            ("scolm", c % 2, 3), ("scolq", c % 2, 0), ("scolq", c % 2, 1),
                            ("scolq", c % 2, 2), ("scolq", c % 2, 3)],
                     writes=[("scol", c % 2)]))
        th.append(TH("S", lambda e, sc_=sc_: e.activation(out=msq4t[:, :], in_=sc_[:, 0:4],
                                                          func=AF.Square, bias=0.0, scale=1.0),
                     reads=[("scol", c % 2)], writes=[("msq4",)]))
        th.append(TH("V", lambda e, sc_=sc_: e.scalar_tensor_tensor(out=sc_[:, 4:8], in0=sc_[:, 4:8],
                                                                    scalar=1e-5, in1=msq4t[:, :],
                                                                    op0=ALU.add, op1=ALU.subtract),
                     reads=[("scol", c % 2), ("msq4",)], writes=[("scol", c % 2)]))
        th.append(TH("S", lambda e, sc_=sc_: e.activation(out=sc_[:, 4:8], in_=sc_[:, 4:8],
                                                          func=AF.Ln, bias=0.0, scale=1.0),
                     reads=[("scol", c % 2)], writes=[("scol", c % 2)]))
        th.append(TH("S", lambda e, sc_=sc_: e.activation(out=sc_[:, 4:8], in_=sc_[:, 4:8],
                                                          func=AF.Exp, bias=0.0, scale=-0.5),
                     reads=[("scol", c % 2), ("cvals",)], writes=[("scol", c % 2)]))
        th.append(TH("T", lambda e, sc_=sc_: e.transpose(out=pst, in_=sc_[:, :],
                                                         identity=ident[:, :]),
                     reads=[("scol", c % 2), ("ident",)], writes=[("pzdot",)]))
        th.append(TH("V", lambda e: e.tensor_copy(out=pst8[:, :], in_=pst),
                     reads=[("pzdot",)], writes=[("pst8",)]))

        def bnc(e, c=c):
            dst = scr[0:1, :].rearrange("a (g ch u t) -> a g ch u t",
                                        g=2, ch=NCH, u=4)[:, :, c, :, :]
            return e.dma_start(out=dst, in_=pst8[:, :])
        th.append(TH("SY", bnc, reads=[("pst8",)], writes=[("scr1", c)], sem="sc1"))

        def rp1(e, c=c):
            return e.dma_start(out=m1rep[c % NRP][:, :],
                               in_=scr[0:1, cc(c)].partition_broadcast(128))
        th.append(TH("SY", rp1, reads=[("scr1", c)], writes=[("m1rep", c % NRP)], sem=f"rpa{c % NRP}"))

        def rp2(e, c=c):
            return e.dma_start(out=i1rep[c % NRP][:, :],
                               in_=scr[0:1, T + c * CH:T + (c + 1) * CH].partition_broadcast(128))
        th.append(TH("SY", rp2, reads=[("scr1", c)], writes=[("i1rep", c % NRP)], sem=f"rpb{c % NRP}"))
        return th

    def normalize_thunks(c):
        th = []
        for b in range(NB):
            def ld(e, b=b, c=c):
                return e.dma_start(out=xr[b % 2][:, :],
                                   in_=xT[b * 128:(b + 1) * 128, cc(c)])
            th.append(TH("SY", ld, writes=[("xr", b % 2)], sem=f"xr{b % 2}"))
            th.append(TH("V", lambda e, b=b, c=c: e.tensor_sub(
                out=xc1[b % NXC][:, :], in0=xr[b % 2][:, :], in1=m1rep[c % NRP][:, :]),
                reads=[("xr", b % 2), ("m1rep", c % NRP)], writes=[("xc1", b % NXC)]))
            th.append(TH("V", lambda e, b=b, c=c: e.tensor_mul(
                out=xs1[c % NXS][b][:, :], in0=xc1[b % NXC][:, :], in1=i1rep[c % NRP][:, :]),
                reads=[("xc1", b % NXC), ("i1rep", c % NRP)], writes=[("xs1", c % NXS, b)]))
        return th

    def emit_compute1(c, post_q=None):
        for _ in range(4):
            inject(2)
        for db in range(NB):
            inject(2)
            ensure(("wdst", id(wq_h), NB - 1))
            pb = pmain[db % 2]

            def qmm(e, db=db, c=c, pb=pb):
                for kb in range(NB):
                    e.matmul(out=pb[:, :], lhsT=wq_h[kb][:, db * 128:(db + 1) * 128],
                             rhs=xs1[c % NXS][kb][:, :], start=(kb == 0), stop=False)
                return e.matmul(out=pb[:, :], lhsT=bq_rh[0:1, db * 128:(db + 1) * 128],
                                rhs=onesr_h[:, :], start=False, stop=True)
            sc.emit("T", qmm,
                    reads=[("xs1", c % NXS, kb) for kb in range(NB)] +
                          [("wdst", id(wq_h), kb) for kb in range(NB)] +
                          [("bq_rh",), ("onesr_h",)],
                    writes=[("pA", db % 2)])
            sc.emit("S", lambda e, pb=pb, db=db: e.activation(
                out=ebuf[db % 2][:, :], in_=pb[:, :], func=AF.Exp, bias=0.0, scale=1.0),
                reads=[("pA", db % 2)], writes=[("ebuf", db % 2)])
            sc.emit("V", lambda e, db=db: e.tensor_scalar_min(
                out=e1buf[db % 2][:, :], in0=ebuf[db % 2][:, :], scalar1=1.0),
                reads=[("ebuf", db % 2)], writes=[("ebuf", db % 2)])
            sc.emit("V", lambda e, pb=pb, db=db, c=c: e.scalar_tensor_tensor(
                out=qp[db][:, cc(c)], in0=pb[:, :], scalar=0.0, in1=e1buf[db % 2][:, :],
                op0=ALU.max, op1=ALU.add),
                reads=[("pA", db % 2), ("ebuf", db % 2)], writes=[("qp", db, c)])
        if post_q:
            bg.extend(post_q)
        ensure(("wdst", id(wd_h), NB - 1))
        for rb in range(2):
            pb = pmain[rb % 2]

            def kv1mm(e, rb=rb, c=c, pb=pb):
                last = None
                for kb in range(NB):
                    last = e.matmul(out=pb[:, :], lhsT=wd_h[kb][:, rb * 128:(rb + 1) * 128],
                                    rhs=xs1[c % NXS][kb][:, :], start=(kb == 0), stop=(kb == NB - 1))
                return last
            sc.emit("T", kv1mm,
                    reads=[("xs1", c % NXS, kb) for kb in range(NB)] +
                          [("wdst", id(wd_h), kb) for kb in range(NB)],
                    writes=[("pA", rb % 2)])
            sc.emit("V", lambda e, pb=pb, rb=rb, c=c: e.tensor_scalar(
                out=kv1[0][rb][:, :], in0=pb[:, :], scalar1=bd_cf[:, rb:rb + 1],
                scalar2=None, op0=ALU.add),
                reads=[("pA", rb % 2), ("bd",)], writes=[("kv1", 0, rb)])
        ensure(("wdst", id(wu_h), 1))
        for ti in range(4):
            i = c * 4 + ti
            kb_i = i % 2
            kvv = kvsb[kb_i][:].rearrange("p (g n) -> p g n", g=32, n=65)
            for fc in range(4):
                inject(2)
                pk = pkv[fc % 2]
                if fc < 2:
                    def kvmm(e, ti=ti, c=c, fc=fc, pk=pk):
                        for rb in range(2):
                            e.matmul(out=pk[:, :], lhsT=kv1[0][rb][:, ti * 128:(ti + 1) * 128],
                                     rhs=wu_h[rb][:, fc * CH:(fc + 1) * CH], start=(rb == 0), stop=False)
                        return e.matmul(out=pk[:, :], lhsT=onesr_h[0:1, 0:128],
                                        rhs=bu_rh[0:1, fc * CH:(fc + 1) * CH], start=False, stop=True)
                else:
                    def kvmm(e, ti=ti, c=c, fc=fc, pk=pk):
                        last = None
                        for rb in range(2):
                            last = e.matmul(out=pk[:, :], lhsT=kv1[0][rb][:, ti * 128:(ti + 1) * 128],
                                            rhs=wu_h[rb][:, fc * CH:(fc + 1) * CH],
                                            start=(rb == 0), stop=(rb == 1))
                        return last
                sc.emit("T", kvmm,
                        reads=[("kv1", 0, 0), ("kv1", 0, 1),
                               ("wdst", id(wu_h), 0), ("wdst", id(wu_h), 1),
                               ("onesr_h",), ("bu_rh",)],
                        writes=[("pC", fc % 2)])
                if fc < 2:
                    dst = kvv[:, fc * 8:(fc + 1) * 8, 0:64]
                    sc.emit("S", lambda e, pk=pk, fc=fc: e.activation(
                        out=ebuf[fc % 2][:, :], in_=pk[:, :], func=AF.Exp, bias=0.0, scale=1.0),
                        reads=[("pC", fc % 2)], writes=[("ebuf", fc % 2)])
                    sc.emit("V", lambda e, fc=fc: e.tensor_scalar_min(
                        out=e1buf[fc % 2][:, :], in0=ebuf[fc % 2][:, :], scalar1=1.0),
                        reads=[("ebuf", fc % 2)], writes=[("ebuf", fc % 2)])
                    sc.emit("V", lambda e, pk=pk, dst=dst, fc=fc: e.scalar_tensor_tensor(
                        out=dst, in0=pk[:, :], scalar=0.0, in1=e1buf[fc % 2][:, :],
                        op0=ALU.max, op1=ALU.add),
                        reads=[("pC", fc % 2), ("ebuf", fc % 2), ("kvones", kb_i)],
                        writes=[("kvk", kb_i, fc)])
                else:
                    dst = kvv[:, 16 + (fc - 2) * 8:16 + (fc - 1) * 8, 0:64]
                    sc.emit("V", lambda e, pk=pk, dst=dst, fc=fc: e.tensor_add(
                        out=dst, in0=pk[:, :], in1=burep[fc - 2][:, :]),
                        reads=[("pC", fc % 2), ("burep", fc - 2), ("kvones", kb_i)],
                        writes=[("kvv", kb_i, fc - 2)])

            def ctxmm(e, i=i, kb_i=kb_i):
                last = None
                for h in range(H):
                    first_in_bank = h in (0, 7, 14)
                    last = e.matmul(
                        out=ctx_ps(h),
                        lhsT=kvsb[kb_i][:, h * 65:h * 65 + 64],
                        rhs=kvsb[kb_i][:, 1040 + h * 65:1040 + (h + 1) * 65],
                        start=(i == 0 and first_in_bank), stop=(i == NT - 1),
                        skip_group_check=True)
                return last
            sc.emit("T", ctxmm,
                    reads=[("kvk", kb_i, 0), ("kvk", kb_i, 1),
                           ("kvv", kb_i, 0), ("kvv", kb_i, 1)],
                    writes=[("kvsb_free", kb_i)] + ([("ctx_acc",)] if i == NT - 1 else []))

    w_wave2 = load_weight(wo, wo_h, accum=[rwo[:, k:k + 1] for k in range(NB)])
    w_wave2.append(TH("V", lambda e: e.tensor_copy(out=rwo_h[:, :], in_=rwo[:, :]),
                      reads=[("wdst", id(wo_h), k) for k in range(NB)], writes=[("rwo_h",)]))

    s0 = stats1_head(0, q="SY") + stats1_tail(0) + stats1_head(1, q="SY")
    bg.extend(w_wq)
    while s0 or bg:
        if s0:
            s0.pop(0)()
        inject(1)
    for t in normalize_thunks(0):
        t()
    bg.extend(stats1_tail(1))
    bg.extend(normalize_thunks(1))
    bg.extend(w_wave1)
    for c in range(NCH):
        if c + 2 < NCH:
            bg.extend(stats1_head(c + 2))
        emit_compute1(c, post_q=(w_wave2 if c == NCH - 1 else None))
        if c + 2 < NCH:
            bg.extend(stats1_tail(c + 2))
            bg.extend(normalize_thunks(c + 2))
    drain_bg()


    for h in range(H):
        sc.emit("S", lambda e, h=h: e.activation(out=ctxsb_sl(h),
                                                 in_=ctx_ps(h)[0:64, 0:65],
                                                 func=AF.Copy, bias=0.0, scale=1.0),
                reads=[("ctx_acc",)], writes=[("ctxsb", h)])
    sc.emit("V", lambda e: e.memset(zlhs[:, :], 0.0), writes=[("zlhs_z",)])
    for kb in range(NB):
        for j in range(2):
            h = 2 * kb + j
            sc.emit("V", lambda e, kb=kb, j=j, h=h: e.tensor_copy(
                out=zlhs[j * 64:j * 64 + 64, kb * 16 + h:kb * 16 + h + 1],
                in_=ctxsb_sl(h, 65)[:, 64:65]),
                reads=[("ctxsb", h), ("zlhs_z",)], writes=[("zlhs", h)])

    def a1z_thunks(c):
        th = []

        def zmm(e, c=c):
            last = None
            for kb in range(NB):
                last = e.matmul(out=pmisc[0:16, :], lhsT=zlhs[:, kb * 16:(kb + 1) * 16],
                                rhs=qp[kb][:, cc(c)], start=(kb == 0), stop=(kb == NB - 1),
                                skip_group_check=True)
            return last
        th.append(TH("T", zmm, reads=[("zlhs", h) for h in range(H)] +
                     [("qp", kb, c) for kb in range(NB)], writes=[("pzdot",)]))
        th.append(TH("V", lambda e, c=c: e.tensor_scalar_add(
            out=zdsb[:, :], in0=pmisc[0:16, :], scalar1=1e-6),
            reads=[("pzdot",)], writes=[("zdsb",)]))
        th.append(TH("S", lambda e, c=c: e.activation(out=zdsb[:, :], in_=zdsb[:, :],
                                                      func=AF.Ln, bias=0.0, scale=1.0),
                     reads=[("zdsb",)], writes=[("zdsb",)]))
        th.append(TH("V", lambda e, c=c: e.tensor_scalar_mul(out=zdsb[:, :], in0=zdsb[:, :],
                                                             scalar1=-1.0),
                     reads=[("zdsb",)], writes=[("zdsb",)]))
        th.append(TH("S", lambda e, c=c: e.activation(out=zdsb[:, :], in_=zdsb[:, :],
                                                      func=AF.Exp, bias=0.0, scale=1.0),
                     reads=[("zdsb",)], writes=[("zdsb",)]))
        th.append(TH("V", lambda e: e.tensor_scalar_mul(out=z16[:, :], in0=zdsb[:, :],
                                                        scalar1=1024.0),
                     reads=[("zdsb",)], writes=[("z16",)]))
        th.append(TH("SY", lambda e, c=c: e.dma_start(out=zscr[:, cc(c)], in_=z16[:, :]),
                     reads=[("z16",)], writes=[("zscr", c)], sem="zsc"))
        return th

    def zb_thunks(c):
        th = []
        for p in range(PAIRS):
            th.extend(zrep_bcast(c, p))
        return th

    def zrep_bcast(c, p):
        def za(e, p=p, c=c):
            return e.dma_start(out=zreps[c % 2][p][0:64, :],
                               in_=zscr[2 * p:2 * p + 1, cc(c)].partition_broadcast(64))

        def zb(e, p=p, c=c):
            return e.dma_start(out=zreps[c % 2][p][64:128, :],
                               in_=zscr[2 * p + 1:2 * p + 2, cc(c)].partition_broadcast(64))
        return [TH("SY", za, reads=[("zscr", c)],
                   writes=[("zrep", c % 2, p)], sem=f"zr{p}a{c % 2}"),
                TH("SY", zb, reads=[("zscr", c)],
                   writes=[("zrep2", c % 2, p)], sem=f"zr{p}b{c % 2}")]

    def a1o_thunks(c):
        th = []
        for p in range(PAIRS):
            pb = pmain[p % 2]

            def outmm(e, p=p, c=c, pb=pb):
                e.matmul(out=pb[0:64, :], lhsT=ctxsb_sl(2 * p, 64),
                         rhs=qp[p][0:64, cc(c)], start=True, stop=True)
                return e.matmul(out=pb[64:128, :], lhsT=ctxsb_sl(2 * p + 1, 64),
                                rhs=qp[p][64:128, cc(c)], start=True, stop=True)
            th.append(TH("T", outmm, reads=[("ctxsb", 2 * p), ("ctxsb", 2 * p + 1), ("qp", p, c)],
                         writes=[("pA", p % 2)]))
            th.append(TH("S", lambda e, p=p, pb=pb: e.activation(
                out=oev[p % 4][:, :], in_=pb[:, :], func=AF.Identity,
                bias=0.0, scale=1.0 / 1024.0),
                reads=[("pA", p % 2), ("cvals",)], writes=[("oev", p % 4)]))
            th.append(TH("GP", lambda e, p=p, c=c: e.tensor_mul(
                out=outz[c % NOZ][p][:, :], in0=oev[p % 4][:, :], in1=zreps[c % 2][p][:, :]),
                reads=[("oev", p % 4), ("zrep", c % 2, p), ("zrep2", c % 2, p)],
                writes=[("outz", c % NOZ, p)], sem="GP", inc=1))
        return th

    def a2_thunks(c):
        ensure(("wdst", id(wo_h), NB - 1))
        th = []
        for db in range(NB):
            pb = pmain[db % 2]

            def womm(e, db=db, c=c, pb=pb):
                last = None
                for kb in range(NB):
                    last = e.matmul(out=pb[:, :], lhsT=wo_h[kb][:, db * 128:(db + 1) * 128],
                                    rhs=outz[c % NOZ][kb][:, :], start=(kb == 0), stop=(kb == NB - 1))
                return last
            th.append(TH("T", womm,
                         reads=[("outz", c % NOZ, kb) for kb in range(NB)] +
                               [("wdst", id(wo_h), kb) for kb in range(NB)],
                         writes=[("pA", db % 2)]))

            def xld(e, db=db, c=c):
                return e.dma_start(out=xr[db % 2][:, :],
                                   in_=xT[db * 128:(db + 1) * 128, cc(c)])
            th.append(TH("SY", xld, writes=[("xr", db % 2)], sem=f"xr{db % 2}"))
            th.append(TH("V", lambda e, pb=pb, db=db, c=c: e.scalar_tensor_tensor(
                out=x2[c % NX2][db][:, :], in0=pb[:, :], scalar=bo_cf[:, db:db + 1],
                in1=xr[db % 2][:, :], op0=ALU.add, op1=ALU.add),
                reads=[("pA", db % 2), ("xr", db % 2), ("bo",)],
                writes=[("x2", c % NX2, db)]))
        return th

    def c1_thunks(c):
        th = []

        def sdmm(e, c=c):
            for kb in range(NB):
                e.matmul(out=pmisc[32:33, :], lhsT=rwo_h[:, kb:kb + 1],
                         rhs=outz[c % NOZ][kb][:, :], start=(kb == 0), stop=False,
                         skip_group_check=True)
            return e.matmul(out=pmisc[32:33, :], lhsT=sbo_h[:, :], rhs=onesr_h[:, :],
                            start=False, stop=True, skip_group_check=True)
        th.append(TH("T", sdmm, reads=[("outz", c % NOZ, kb) for kb in range(NB)] +
                     [("rwo_h",), ("sbo_h",), ("onesr_h",)], writes=[("psum_sx",)]))
        for db in range(NB):
            th.append(TH("S", lambda e, db=db, c=c: e.activation(
                out=xsq2[db % 2][:, :], in_=x2[c % NX2][db][:, :], func=AF.Square,
                bias=0.0, scale=1.0),
                reads=[("x2", c % NX2, db)], writes=[("xsq2", db % 2)]))
            th.append(TH("T", lambda e, db=db: e.matmul(
                out=pmisc[64:65, :], lhsT=ones_h[:, 0:1], rhs=xsq2[db % 2][:, :],
                start=(db == 0), stop=(db == NB - 1), skip_group_check=True),
                reads=[("xsq2", db % 2), ("ones_h",)], writes=[("psum_sq",)]))
        th.append(TH("SY", lambda e, c=c: e.dma_start(out=m1c[:, :], in_=scr[0:1, cc(c)]),
                     reads=[("scr1", c)], writes=[("m1c",)], sem="m1c"))
        th.append(TH("V", lambda e: e.tensor_scalar_mul(out=trow[0][:, :], in0=pmisc[32:33, :],
                                                        scalar1=1.0 / D),
                     reads=[("psum_sx",)], writes=[("trow0",)]))
        th.append(TH("V", lambda e, c=c: e.tensor_add(out=m2row[:, :], in0=m1c[:, :],
                                                      in1=trow[0][:, :]),
                     reads=[("trow0",), ("m1c",)], writes=[("m2row",)]))
        th.append(TH("V", lambda e: e.tensor_scalar_mul(out=trow[0][:, :], in0=pmisc[64:65, :],
                                                        scalar1=1.0 / D),
                     reads=[("psum_sq",)], writes=[("trow0",)]))
        th.append(TH("S", lambda e, c=c: e.activation(out=trow[1][:, :], in_=m2row[:, :],
                                                      func=AF.Square, bias=0.0, scale=1.0),
                     reads=[("m2row",)], writes=[("trow1",)]))
        th.append(TH("V", lambda e: e.scalar_tensor_tensor(out=trow[0][:, :], in0=trow[0][:, :],
                                                           scalar=1e-5, in1=trow[1][:, :],
                                                           op0=ALU.add, op1=ALU.subtract),
                     reads=[("trow0",), ("trow1",)], writes=[("trow0",)]))
        th.append(TH("S", lambda e: e.activation(out=trow[1][:, :], in_=trow[0][:, :],
                                                 func=AF.Ln, bias=0.0, scale=1.0),
                     reads=[("trow0",)], writes=[("trow1",)]))
        th.append(TH("S", lambda e, c=c: e.activation(out=i2row[:, :], in_=trow[1][:, :],
                                                      func=AF.Exp, bias=0.0, scale=-0.5),
                     reads=[("trow1",), ("cvals",)], writes=[("i2row",)]))

        def bnc(e, c=c):
            return e.dma_start(out=scr[1:2, :].rearrange("a (g t) -> a g t", g=2)[:, :, cc(c)],
                               in_=rows2[0:1, :].rearrange("a (g t) -> a g t", g=2))
        th.append(TH("SY", bnc, reads=[("m2row",), ("i2row",)], writes=[("scr2", c)], sem="sc2"))

        def rp1(e, c=c):
            return e.dma_start(out=m2rep[c % NRP][:, :],
                               in_=scr[1:2, cc(c)].partition_broadcast(128))
        th.append(TH("SY", rp1, reads=[("scr2", c)], writes=[("m1rep", c % NRP)], sem=f"rpa{c % NRP}"))

        def rp2(e, c=c):
            return e.dma_start(out=i2rep[c % NRP][:, :],
                               in_=scr[1:2, T + c * CH:T + (c + 1) * CH].partition_broadcast(128))
        th.append(TH("SY", rp2, reads=[("scr2", c)], writes=[("i1rep", c % NRP)], sem=f"rpb{c % NRP}"))
        return th

    def c2a_thunks(c):
        ensure(("wdst", id(wf8), NB - 1))
        th = []
        for db in range(NB):
            th.append(TH("V", lambda e, db=db, c=c: e.tensor_sub(
                out=xc2[db % NXC][:, :], in0=x2[c % NX2][db][:, :], in1=m2rep[c % NRP][:, :]),
                reads=[("x2", c % NX2, db), ("m1rep", c % NRP)], writes=[("xc1", db % NXC)]))
            th.append(TH("V", lambda e, db=db, c=c: e.tensor_mul(
                out=h2q[c % NH2][db][:, :], in0=xc2[db % NXC][:, :], in1=i2rep[c % NRP][:, :]),
                reads=[("xc1", db % NXC), ("i1rep", c % NRP)], writes=[("h2q", c % NH2, db)]))
            th.append(TH("T", lambda e, kb=db, c=c: e.matmul(
                out=pkv[0][:, :], lhsT=wf8[kb][:, 0:128],
                rhs=h2q[c % NH2][kb][:, :], start=(kb == 0), stop=(kb == NB - 1)),
                reads=[("h2q", c % NH2, db), ("wdst", id(wf8), db)],
                writes=[("pC", 0)]))
        return th

    def c2b_thunks(c):
        th = []
        for fb in range(NB):
            pb = pkv[fb % 2]
            if fb > 0:
                def ffnmm(e, fb=fb, c=c, pb=pb):
                    last = None
                    for kb in range(NB):
                        last = e.matmul(out=pb[:, :], lhsT=wf8[kb][:, fb * 128:(fb + 1) * 128],
                                        rhs=h2q[c % NH2][kb][:, :], start=(kb == 0), stop=(kb == NB - 1))
                    return last
                th.append(TH("T", ffnmm,
                             reads=[("h2q", c % NH2, kb) for kb in range(NB)] +
                                   [("wdst", id(wf8), kb) for kb in range(NB)],
                             writes=[("pC", fb % 2)]))
            th.append(TH("V", lambda e, pb=pb, fb=fb, c=c: e.scalar_tensor_tensor(
                out=x2[c % NX2][fb][:, :], in0=pb[:, :], scalar=bf_cf[:, fb:fb + 1],
                in1=x2[c % NX2][fb][:, :], op0=ALU.add, op1=ALU.add),
                reads=[("pC", fb % 2)], writes=[("x2", c % NX2, fb)]))

            def yst(e, fb=fb, c=c):
                return e.dma_start(out=yT[fb * 128:(fb + 1) * 128, cc(c)],
                                   in_=x2[c % NX2][fb][:, :])
            th.append(TH("GP", yst, reads=[("x2", c % NX2, fb)],
                         writes=[("ydma", c, fb)], sem=f"yd{fb}_{c % 2}"))
        return th

    def run_all(th):
        for t in th:
            t()

    def interleave(l1, l2):
        i = j = 0
        while i < len(l1) or j < len(l2):
            if i < len(l1):
                l1[i]()
                i += 1
            if j < len(l2):
                l2[j]()
                j += 1

    run_all(a1z_thunks(0))
    run_all(zb_thunks(0))
    interleave(a1z_thunks(1) if NCH > 1 else [], a1o_thunks(0))
    run_all(a2_thunks(0))
    run_all(c1_thunks(0))
    for c in range(NCH):
        if c + 1 < NCH:
            run_all(zb_thunks(c + 1))
        l2 = (a1z_thunks(c + 2) if c + 2 < NCH else []) + \
             (a1o_thunks(c + 1) if c + 1 < NCH else [])
        interleave(c2a_thunks(c), l2)
        if c + 1 < NCH:
            run_all(a2_thunks(c + 1))
            run_all(c1_thunks(c + 1))
        run_all(c2b_thunks(c))

    sems = {}
    with ExitStack() as ses:
        for name in sc.sem_names:
            sems[name] = ses.enter_context(nc.semaphore(f"sem{name}"))
        block = ses.enter_context(nc.Block())

        @block.tensor
        def _(eng):
            sc.replay("T", eng, sems)

        @block.vector
        def _(eng):
            sc.replay("V", eng, sems)

        @block.scalar
        def _(eng):
            sc.replay("S", eng, sems)

        @block.sync
        def _(eng):
            sc.replay("SY", eng, sems)

        @block.gpsimd
        def _(eng):
            sc.replay("GP", eng, sems)
            for fb in range(NB):
                for par in range(2):
                    eng.wait_ge(sems[f"yd{fb}_{par}"], sc.count[f"yd{fb}_{par}"])
    es.close()
    return nc


_NC_CACHE = None


def kernel(L0_attn_x, L0_attn_Wq, L0_attn_bq, L0_attn_Wd, L0_attn_bd,
           L0_attn_Wu, L0_attn_bu, L0_attn_Wo, L0_attn_bo,
           L0_ffn_weight, L0_ffn_bias):
    global _NC_CACHE
    x = np.asarray(L0_attn_x, dtype=np.float32)
    B = x.shape[0]
    f32 = lambda a: np.ascontiguousarray(np.asarray(a, dtype=np.float32))
    kidx = np.concatenate([np.arange(h * 128, h * 128 + 64) for h in range(H)])
    vidx = np.concatenate([np.arange(h * 128 + 64, h * 128 + 128) for h in range(H)])
    perm = np.concatenate([kidx, vidx])
    common = {
        "wq": f32(L0_attn_Wq),
        "wd": f32(L0_attn_Wd),
        "wu": f32(np.asarray(L0_attn_Wu, dtype=np.float32)[:, perm]),
        "wo": f32(L0_attn_Wo),
        "wf": f32(np.asarray(L0_ffn_weight, dtype=np.float32).T),
        "bq_r": f32(np.asarray(L0_attn_bq, dtype=np.float32).reshape(1, D)),
        "bd_c": f32(np.asarray(L0_attn_bd, dtype=np.float32).reshape(2, 128).T),
        "bu_r": f32(np.asarray(L0_attn_bu, dtype=np.float32)[perm].reshape(1, 2 * D)),
        "bo_c": f32(np.asarray(L0_attn_bo, dtype=np.float32).reshape(NB, 128).T),
        "bf_c": f32(np.asarray(L0_ffn_bias, dtype=np.float32).reshape(NB, 128).T),
        "ones_in": np.ones((128, 128), dtype=np.float32),
        "onesr_in": np.ones((1, CH), dtype=np.float32),
        "cvals_in": np.array([[-0.5, 1.0 / D, 1e-5, -1.0, 1.0 / 1024.0]], dtype=np.float32),
        "ident_in": np.eye(128, dtype=np.float32),
    }
    in_maps = []
    for b in range(B):
        m = dict(common)
        m["xT"] = np.ascontiguousarray(x[b].T)
        m["xtok_d"] = np.ascontiguousarray(x[b])
        in_maps.append(m)
    if _NC_CACHE is None:
        _NC_CACHE = build_nc()
    res = run_bass_kernel_spmd(_NC_CACHE, in_maps, core_ids=list(range(B)))
    out = np.stack([np.ascontiguousarray(r["yT"].T) for r in res.results], axis=0)
    return out


# version 24
# speedup vs baseline: 1.0858x; 1.0094x over previous
import numpy as np
from contextlib import ExitStack
import concourse.bass as bass
import concourse.mybir as mybir
from concourse.bass_utils import run_bass_kernel_spmd

F32 = mybir.dt.float32
F16 = mybir.dt.float16
FP8 = mybir.dt.float8e4
AF = mybir.ActivationFunctionType
ALU = mybir.AluOpType
PM = mybir.MatmulPerfMode.DoubleRow

T, D, H, DH, RK = 2048, 1024, 16, 64, 256
CH = 512
NCH = T // CH
NB = D // 128
NT = T // 128
PAIRS = H // 2
NXCH, NXS, NX2, NOZ, NH2, NRP, NXC = 1, 2, 2, 1, 1, 2, 1


class Sched:

    ENGS = ("T", "V", "S", "SY", "GP")
    ONE_WAVE = ("cst", "cstg")

    def __init__(self):
        self.steps = {e: [] for e in self.ENGS}
        self.count = {}
        self.writer = {}
        self.readers = {}
        self.watermark = {e: {} for e in self.ENGS}
        self.sem_names = []

    def _bump(self, sem, inc):
        if sem not in self.count:
            self.count[sem] = 0
            self.sem_names.append(sem)
        self.count[sem] += inc
        return self.count[sem]

    def emit(self, eng, fn, reads=(), writes=(), sem=None, inc=None):
        if eng in ("SY", "GP") and inc != 1:
            assert sem is not None, "DMA steps need a sem stream name"
            inc = 16
        else:
            sem = sem or eng
            inc = 1
        waits = {}
        def add(w):
            if w is None:
                return
            ws, wv = w
            if ws == "T" and eng == "T":
                return
            waits[ws] = max(waits.get(ws, 0), wv)
        for t in reads:
            add(self.writer.get(t))
        for t in writes:
            add(self.writer.get(t))
            for r in self.readers.get(t, ()):
                add(r)
        myval = self._bump(sem, inc)
        final_waits = []
        for ws, wv in sorted(waits.items()):
            if ws == sem and eng not in ("SY", "GP"):
                pass
            if self.watermark[eng].get(ws, 0) < wv:
                final_waits.append((ws, wv))
                self.watermark[eng][ws] = wv
        self.steps[eng].append((final_waits, fn, sem, inc))
        for t in writes:
            self.writer[t] = (sem, myval)
            self.readers[t] = []
        for t in reads:
            self.readers.setdefault(t, []).append((sem, myval))

    def replay(self, eng_name, eng_obj, sems):
        for (waits, fn, sem, inc) in self.steps[eng_name]:
            for (ws, wv) in waits:
                if ws in self.ONE_WAVE:
                    wv = self.count[ws]
                eng_obj.wait_ge(sems[ws], wv)
            last = fn(eng_obj)
            assert last is not None, "step fn must return its last instruction"
            last.then_inc(sems[sem], inc)


def build_nc():
    nc = bass.Bass("TRN2", target_bir_lowering=False)
    dt = nc.dram_tensor
    xT = dt("xT", [D, T], F32, kind="ExternalInput").ap()
    xtok_d = dt("xtok_d", [T, D], F32, kind="ExternalInput").ap()
    ident_in = dt("ident_in", [128, 128], F32, kind="ExternalInput").ap()
    wq = dt("wq", [D, D], F32, kind="ExternalInput").ap()
    wd = dt("wd", [D, RK], F32, kind="ExternalInput").ap()
    wu = dt("wu", [RK, 2 * D], F32, kind="ExternalInput").ap()
    wo = dt("wo", [D, D], F32, kind="ExternalInput").ap()
    wf = dt("wf", [D, D], F32, kind="ExternalInput").ap()
    bq_r = dt("bq_r", [1, D], F32, kind="ExternalInput").ap()
    bd_c = dt("bd_c", [128, 2], F32, kind="ExternalInput").ap()
    bu_r = dt("bu_r", [1, 2 * D], F32, kind="ExternalInput").ap()
    bo_c = dt("bo_c", [128, NB], F32, kind="ExternalInput").ap()
    bf_c = dt("bf_c", [128, NB], F32, kind="ExternalInput").ap()
    ones_in = dt("ones_in", [128, 128], F32, kind="ExternalInput").ap()
    onesr_in = dt("onesr_in", [1, CH], F32, kind="ExternalInput").ap()
    cvals_in = dt("cvals_in", [1, 5], F32, kind="ExternalInput").ap()
    scr = dt("scr", [2, 2 * T], F32, kind="Internal").ap()
    zscr = dt("zscr", [H, T], F16, kind="Internal").ap()
    buscr = dt("buscr", [1, 2 * D], F16, kind="Internal").ap()
    yT = dt("yT", [D, T], F32, kind="ExternalOutput").ap()

    es = ExitStack()
    sb = lambda name, shape, d: es.enter_context(nc.sbuf_tensor(name, shape, d))

    wstage = [sb(f"wstage{i}", [128, D], F32) for i in range(2)]
    wq_h = [sb(f"wq_h{k}", [128, D], F16) for k in range(NB)]
    wd_h = [sb(f"wd_h{k}", [128, RK], F16) for k in range(NB)]
    wu_h = [sb(f"wu_h{k}", [128, 2 * D], F16) for k in range(2)]
    wo_h = wq_h
    wf8d = [sb(f"wf8d{s}", [128, 2 * D], FP8) for s in range(4)]
    wf8 = [wf8d[k // 2][:, (k % 2) * D:(k % 2 + 1) * D] for k in range(NB)]
    rwo = sb("rwo", [128, NB], F32)
    rwo_h = sb("rwo_h", [128, NB], F16)
    bq_rf = sb("bq_rf", [1, D], F32)
    bq_rh = sb("bq_rh", [1, D], F16)
    bu_rf = sb("bu_rf", [1, 2 * D], F32)
    bu_rh = sb("bu_rh", [1, 2 * D], F16)
    bd_cf = sb("bd_cf", [128, 2], F32)
    bo_cf = sb("bo_cf", [128, NB], F32)
    bf_cf = sb("bf_cf", [128, NB], F32)
    cvals = sb("cvals", [1, 5], F32)
    ones_f = sb("ones_f", [128, 128], F32)
    ones_h = sb("ones_h", [128, 128], F16)
    onesr_f = sb("onesr_f", [1, CH], F32)
    onesr_h = sb("onesr_h", [1, CH], F16)
    burep = [sb(f"burep{i}", [128, CH], F16) for i in range(2)]

    xtok = [sb(f"xtok{i}", [128, D], F32) for i in range(2)]
    scol = [sb(f"scol{i}", [128, 8], F32) for i in range(2)]
    pst8 = sb("pst8", [8, 128], F32)
    msq4t = sb("msq4t", [128, 4], F32)
    ident = sb("ident", [128, 128], F32)
    m1rep = [sb(f"m1rep{i}", [128, CH], F32) for i in range(NRP)]
    i1rep = [sb(f"i1rep{i}", [128, CH], F32) for i in range(NRP)]
    m2rep = m1rep
    i2rep = i1rep
    xc1 = [sb(f"xc1_{i}", [128, CH], F32) for i in range(NXC)]
    xs1 = [[sb(f"xs1_{i}_{b}", [128, CH], F16) for b in range(NB)] for i in range(NXS)]
    qp = [sb(f"qp{b}", [128, T], F16) for b in range(NB)]
    ebuf = [sb(f"ebuf{i}", [128, CH], F16) for i in range(2)]
    e1buf = ebuf
    kv1 = [[sb(f"kv1_{i}_{r}", [128, CH], F16) for r in range(2)] for i in range(1)]
    kvsb = [sb(f"kvsb{i}", [128, 2 * 1040], F16) for i in range(2)]
    ctxsb = sb("ctxsb", [128, PAIRS * 65], F16)
    bo16 = sb("bo16", [128, NB], F16)
    sbo = sb("sbo", [1, 1], F32)
    sbo_h = sb("sbo_h", [1, 1], F16)
    zlhs = sb("zlhs", [128, NB * 16], F16)
    zdsb = sb("zdsb", [16, CH], F32)
    z16 = sb("z16", [16, CH], F16)
    oev = [sb(f"oev{i}", [128, CH], F16) for i in range(4)]
    rows2 = sb("rows2", [1, 2 * CH], F32)
    m2row = rows2[0:1, 0:CH]
    i2row = rows2[0:1, CH:2 * CH]
    m1c = sb("m1c", [1, CH], F32)
    trow = [sb(f"trow{i}", [1, CH], F32) for i in range(2)]
    trow = trow + trow
    zrep = [sb(f"zrep{i}", [128, CH], F16) for i in range(8)]
    zrv = []
    for nm, hostt in (("zrv0", wu_h[0]), ("zrv1", wu_h[1])):
        vt = nc.virtual_sbuf_tensor(nm, [128, 4 * CH], F16)
        nc.back_virtual_tensor_with(vt, hostt)
        zrv.append(vt)
    zreps = [zrep, [zrv[p // 4][:, (p % 4) * CH:(p % 4 + 1) * CH] for p in range(8)]]
    outz = [[sb(f"outz{i}_{b}", [128, CH], F16) for b in range(NB)] for i in range(NOZ)]
    xr = [sb(f"xr{i}", [128, CH], F32) for i in range(2)]
    x2 = [[sb(f"x2_0_{b}", [128, CH], F32) for b in range(NB)]]
    x2v = []
    for nm, hostt in (("x2v0", wstage[0]), ("x2v1", wstage[1]),
                      ("x2v2", xtok[0]), ("x2v3", xtok[1])):
        vt = nc.virtual_sbuf_tensor(nm, [128, 2 * CH], F32)
        nc.back_virtual_tensor_with(vt, hostt)
        x2v.append(vt)
    x2.append([x2v[b // 2][:, (b % 2) * CH:(b % 2 + 1) * CH] for b in range(NB)])
    xsq2 = [sb(f"xsq2_{i}", [128, CH], F16) for i in range(2)]
    xc2 = xc1
    h2qd = [sb(f"h2qd{s}", [128, 2 * CH], FP8) for s in range(4)]
    h2q = [[h2qd[b // 2][:, (b % 2) * CH:(b % 2 + 1) * CH] for b in range(NB)]]

    psum = lambda name, shape: es.enter_context(nc.psum_tensor(name, shape, F32))
    pmain = [psum("pm0", [128, CH]), psum("pm1", [128, CH])]
    pctx = [psum("pc0", [64, 7 * 65]), psum("pc1", [64, 7 * 65]),
            psum("pc2", [64, 2 * 65])]
    pmisc = psum("pmisc", [128, CH])
    pkv = [psum("pk0", [128, CH]), psum("pk1", [128, CH])]
    pst = pmisc[0:8, 0:128]

    def ctx_ps(h):
        if h < 7:
            return pctx[0][0:64, h * 65:(h + 1) * 65]
        if h < 14:
            return pctx[1][0:64, (h - 7) * 65:(h - 6) * 65]
        return pctx[2][0:64, (h - 14) * 65:(h - 13) * 65]

    def ctxsb_sl(h, ncol=65):
        r0 = (h % 2) * 64
        c0 = (h // 2) * 65
        return ctxsb[r0:r0 + 64, c0:c0 + ncol]

    sc = Sched()
    cc = lambda c: slice(c * CH, (c + 1) * CH)
    cc2 = lambda c: slice(c * 128, (c + 1) * 128)
    import collections as _co
    bg = _co.deque()

    def TH(eng, fn, reads=(), writes=(), sem=None, inc=None):
        return lambda: sc.emit(eng, fn, reads=reads, writes=writes, sem=sem, inc=inc)

    def inject(n=1):
        for _ in range(n):
            if not bg:
                return
            bg.popleft()()

    def drain_bg():
        while bg:
            bg.popleft()()

    def ensure(tok):
        while tok not in sc.writer and bg:
            bg.popleft()()

    def dma(q, key, out, in_, reads=(), writes=()):
        sc.emit(q, lambda e, o=out, i=in_: e.dma_start(out=o, in_=i),
                reads=reads, writes=writes, sem=key)

    dma("SY", "cst", cvals[:, :], cvals_in[:, :], writes=[("cvals",)])
    nc.const_aps.register(F32, -0.5, cvals[0:1, 0:1])
    nc.const_aps.register(F32, 1.0 / 1024.0, cvals[0:1, 4:5])
    dma("SY", "cst", ones_f[:, :], ones_in[:, :], writes=[("ones_f",)])
    dma("SY", "cst", ident[:, :], ident_in[:, :], writes=[("ident",)])
    dma("SY", "cst", onesr_f[:, :], onesr_in[:, :], writes=[("onesr_f",)])
    dma("SY", "cst", bq_rf[:, :], bq_r[:, :], writes=[("bq_rf",)])
    dma("SY", "cst", bd_cf[:, :], bd_c[:, :], writes=[("bd",)])
    dma("SY", "cst", bu_rf[:, :], bu_r[:, :], writes=[("bu_rf",)])
    dma("SY", "cst", bo_cf[:, :], bo_c[:, :], writes=[("bo",)])
    dma("SY", "cst", bf_cf[:, :], bf_c[:, :], writes=[("bf",)])

    sc.emit("V", lambda e: e.tensor_copy(out=ones_h[:, :], in_=ones_f[:, :]),
            reads=[("ones_f",)], writes=[("ones_h",)])
    sc.emit("V", lambda e: e.tensor_copy(out=onesr_h[:, :], in_=onesr_f[:, :]),
            reads=[("onesr_f",)], writes=[("onesr_h",)])
    sc.emit("V", lambda e: e.tensor_copy(out=bq_rh[:, :], in_=bq_rf[:, :]),
            reads=[("bq_rf",)], writes=[("bq_rh",)])
    sc.emit("V", lambda e: e.tensor_copy(out=bu_rh[:, :], in_=bu_rf[:, :]),
            reads=[("bu_rf",)], writes=[("bu_rh",)])
    dma("GP", "busc", buscr[0:1, :], bu_rh[0:1, :], reads=[("bu_rh",)], writes=[("buscr",)])
    dma("GP", "cstg", burep[0][:, :], buscr[0:1, D:D + CH].partition_broadcast(128),
        reads=[("buscr",)], writes=[("burep", 0)])
    dma("GP", "cstg", burep[1][:, :], buscr[0:1, D + CH:2 * D].partition_broadcast(128),
        reads=[("buscr",)], writes=[("burep", 1)])

    wi = [0]

    def load_weight(src_ap, dst_list, accum=None):
        th = []
        nblk = src_ap.shape[0] // 128
        cols = src_ap.shape[1]
        nh = (cols + D - 1) // D
        for k in range(nblk):
            for hh in range(nh):
                c0, c1 = hh * D, min((hh + 1) * D, cols)
                sidx = wi[0] % 2
                wi[0] += 1
                st = wstage[sidx]

                def dth(e, s=st, c0=c0, c1=c1, k=k, sa=src_ap):
                    return e.dma_start(out=s[:, 0:c1 - c0],
                                       in_=sa[k * 128:(k + 1) * 128, c0:c1])
                th.append(TH("SY", dth, reads=(), writes=[("wstage", sidx)],
                             sem=f"wst{sidx}"))
                acc = accum[k] if (accum is not None and hh == 0) else None

                def cast(e, s=st, d=dst_list[k], a=acc, c0=c0, c1=c1):
                    return e.activation(out=d[:, c0:c1], in_=s[:, 0:c1 - c0],
                                        func=AF.Copy, bias=0.0, scale=1.0, accum_out=a)
                th.append(TH("S", cast, reads=[("wstage", sidx)],
                             writes=[("wdst", id(dst_list), k)] if hh == nh - 1 else [("wdsth", id(dst_list), k)]))
        return th

    w_wq = load_weight(wq, wq_h)
    w_wave1 = (load_weight(wd, wd_h) + load_weight(wu, wu_h) + load_weight(wf, wf8))
    sc.emit("V", lambda e: e.tensor_copy(out=bo16[:, :], in_=bo_cf[:, :]),
            reads=[("bo",)], writes=[("bo16",)])
    sc.emit("T", lambda e: e.matmul(out=pmisc[32:33, 0:NB], lhsT=ones_h[:, 0:1],
                                    rhs=bo16[:, :], start=True, stop=True,
                                    skip_group_check=True),
            reads=[("bo16",), ("ones_h",)], writes=[("psum_sx",)])
    sc.emit("V", lambda e: e.reduce_sum(out=sbo[:, :], in_=pmisc[32:33, 0:NB],
                                        axis=mybir.AxisListType.X),
            reads=[("psum_sx",)], writes=[("sbo",)])
    sc.emit("V", lambda e: e.tensor_copy(out=sbo_h[:, :], in_=sbo[:, :]),
            reads=[("sbo",)], writes=[("sbo_h",)])
    for i in range(2):
        sc.emit("V", lambda e, i=i: e.memset(
            kvsb[i][:].rearrange("p (g n) -> p g n", g=32, n=65)[:, :, 64:65], 1.0),
            writes=[("kvones", i)])

    def stats1_head(c, q="GP"):
        th = []
        sc_ = scol[c % 2]
        for ti in range(4):
            def ld(e, ti=ti, c=c):
                return e.dma_start(out=xtok[ti % 2][:, :],
                                   in_=xtok_d[c * CH + ti * 128:c * CH + (ti + 1) * 128, :])
            pfx = "xtk" if q == "GP" else "xtsy"
            th.append(TH(q, ld, writes=[("xtok", ti % 2)], sem=f"{pfx}{ti % 2}"))
            th.append(TH("S", lambda e, ti=ti, sc_=sc_: e.activation(
                out=xtok[ti % 2][:, :], in_=xtok[ti % 2][:, :], func=AF.Copy,
                bias=0.0, scale=1.0, accum_out=sc_[:, ti:ti + 1]),
                reads=[("xtok", ti % 2)], writes=[("scolm", c % 2, ti), ("xtok", ti % 2)]))
            th.append(TH("S", lambda e, ti=ti, sc_=sc_: e.activation(
                out=xtok[ti % 2][:, :], in_=xtok[ti % 2][:, :], func=AF.Square,
                bias=0.0, scale=1.0, accum_out=sc_[:, 4 + ti:5 + ti]),
                reads=[("xtok", ti % 2)], writes=[("scolq", c % 2, ti), ("xtok", ti % 2)]))
        return th

    def stats1_tail(c):
        th = []
        sc_ = scol[c % 2]
        th.append(TH("V", lambda e, sc_=sc_: e.tensor_scalar_mul(out=sc_[:, 0:8], in0=sc_[:, 0:8],
                                                                scalar1=1.0 / D),
                     reads=[("scolm", c % 2, 0), ("scolm", c % 2, 1), ("scolm", c % 2, 2),
                            ("scolm", c % 2, 3), ("scolq", c % 2, 0), ("scolq", c % 2, 1),
                            ("scolq", c % 2, 2), ("scolq", c % 2, 3)],
                     writes=[("scol", c % 2)]))
        th.append(TH("S", lambda e, sc_=sc_: e.activation(out=msq4t[:, :], in_=sc_[:, 0:4],
                                                          func=AF.Square, bias=0.0, scale=1.0),
                     reads=[("scol", c % 2)], writes=[("msq4",)]))
        th.append(TH("V", lambda e, sc_=sc_: e.scalar_tensor_tensor(out=sc_[:, 4:8], in0=sc_[:, 4:8],
                                                                    scalar=1e-5, in1=msq4t[:, :],
                                                                    op0=ALU.add, op1=ALU.subtract),
                     reads=[("scol", c % 2), ("msq4",)], writes=[("scol", c % 2)]))
        th.append(TH("S", lambda e, sc_=sc_: e.activation(out=sc_[:, 4:8], in_=sc_[:, 4:8],
                                                          func=AF.Ln, bias=0.0, scale=1.0),
                     reads=[("scol", c % 2)], writes=[("scol", c % 2)]))
        th.append(TH("S", lambda e, sc_=sc_: e.activation(out=sc_[:, 4:8], in_=sc_[:, 4:8],
                                                          func=AF.Exp, bias=0.0, scale=-0.5),
                     reads=[("scol", c % 2), ("cvals",)], writes=[("scol", c % 2)]))
        th.append(TH("T", lambda e, sc_=sc_: e.transpose(out=pst, in_=sc_[:, :],
                                                         identity=ident[:, :]),
                     reads=[("scol", c % 2), ("ident",)], writes=[("pzdot",)]))
        th.append(TH("V", lambda e: e.tensor_copy(out=pst8[:, :], in_=pst),
                     reads=[("pzdot",)], writes=[("pst8",)]))

        def bnc(e, c=c):
            dst = scr[0:1, :].rearrange("a (g ch u t) -> a g ch u t",
                                        g=2, ch=NCH, u=4)[:, :, c, :, :]
            return e.dma_start(out=dst, in_=pst8[:, :])
        th.append(TH("SY", bnc, reads=[("pst8",)], writes=[("scr1", c)], sem="sc1"))

        def rp1(e, c=c):
            return e.dma_start(out=m1rep[c % NRP][:, :],
                               in_=scr[0:1, cc(c)].partition_broadcast(128))
        th.append(TH("SY", rp1, reads=[("scr1", c)], writes=[("m1rep", c % NRP)], sem=f"rpa{c % NRP}"))

        def rp2(e, c=c):
            return e.dma_start(out=i1rep[c % NRP][:, :],
                               in_=scr[0:1, T + c * CH:T + (c + 1) * CH].partition_broadcast(128))
        th.append(TH("SY", rp2, reads=[("scr1", c)], writes=[("i1rep", c % NRP)], sem=f"rpb{c % NRP}"))
        return th

    def normalize_thunks(c):
        th = []
        for b in range(NB):
            def ld(e, b=b, c=c):
                return e.dma_start(out=xr[b % 2][:, :],
                                   in_=xT[b * 128:(b + 1) * 128, cc(c)])
            th.append(TH("SY", ld, writes=[("xr", b % 2)], sem=f"xr{b % 2}"))
            th.append(TH("V", lambda e, b=b, c=c: e.tensor_sub(
                out=xc1[b % NXC][:, :], in0=xr[b % 2][:, :], in1=m1rep[c % NRP][:, :]),
                reads=[("xr", b % 2), ("m1rep", c % NRP)], writes=[("xc1", b % NXC)]))
            th.append(TH("V", lambda e, b=b, c=c: e.tensor_mul(
                out=xs1[c % NXS][b][:, :], in0=xc1[b % NXC][:, :], in1=i1rep[c % NRP][:, :]),
                reads=[("xc1", b % NXC), ("i1rep", c % NRP)], writes=[("xs1", c % NXS, b)]))
        return th

    def emit_compute1(c, post_q=None):
        for _ in range(4):
            inject(2)
        for db in range(NB):
            inject(2)
            ensure(("wdst", id(wq_h), NB - 1))
            pb = pmain[db % 2]

            def qmm(e, db=db, c=c, pb=pb):
                for kb in range(NB):
                    e.matmul(out=pb[:, :], lhsT=wq_h[kb][:, db * 128:(db + 1) * 128],
                             rhs=xs1[c % NXS][kb][:, :], start=(kb == 0), stop=False)
                return e.matmul(out=pb[:, :], lhsT=bq_rh[0:1, db * 128:(db + 1) * 128],
                                rhs=onesr_h[:, :], start=False, stop=True)
            sc.emit("T", qmm,
                    reads=[("xs1", c % NXS, kb) for kb in range(NB)] +
                          [("wdst", id(wq_h), kb) for kb in range(NB)] +
                          [("bq_rh",), ("onesr_h",)],
                    writes=[("pA", db % 2)])
            sc.emit("S", lambda e, pb=pb, db=db: e.activation(
                out=ebuf[db % 2][:, :], in_=pb[:, :], func=AF.Exp, bias=0.0, scale=1.0),
                reads=[("pA", db % 2)], writes=[("ebuf", db % 2)])
            sc.emit("V", lambda e, db=db: e.tensor_scalar_min(
                out=e1buf[db % 2][:, :], in0=ebuf[db % 2][:, :], scalar1=1.0),
                reads=[("ebuf", db % 2)], writes=[("ebuf", db % 2)])
            sc.emit("V", lambda e, pb=pb, db=db, c=c: e.scalar_tensor_tensor(
                out=qp[db][:, cc(c)], in0=pb[:, :], scalar=0.0, in1=e1buf[db % 2][:, :],
                op0=ALU.max, op1=ALU.add),
                reads=[("pA", db % 2), ("ebuf", db % 2)], writes=[("qp", db, c)])
        if post_q:
            bg.extend(post_q)
        ensure(("wdst", id(wd_h), NB - 1))
        for rb in range(2):
            pb = pmain[rb % 2]

            def kv1mm(e, rb=rb, c=c, pb=pb):
                last = None
                for kb in range(NB):
                    last = e.matmul(out=pb[:, :], lhsT=wd_h[kb][:, rb * 128:(rb + 1) * 128],
                                    rhs=xs1[c % NXS][kb][:, :], start=(kb == 0), stop=(kb == NB - 1))
                return last
            sc.emit("T", kv1mm,
                    reads=[("xs1", c % NXS, kb) for kb in range(NB)] +
                          [("wdst", id(wd_h), kb) for kb in range(NB)],
                    writes=[("pA", rb % 2)])
            sc.emit("V", lambda e, pb=pb, rb=rb, c=c: e.tensor_scalar(
                out=kv1[0][rb][:, :], in0=pb[:, :], scalar1=bd_cf[:, rb:rb + 1],
                scalar2=None, op0=ALU.add),
                reads=[("pA", rb % 2), ("bd",)], writes=[("kv1", 0, rb)])
        ensure(("wdst", id(wu_h), 1))
        for ti in range(4):
            i = c * 4 + ti
            kb_i = i % 2
            kvv = kvsb[kb_i][:].rearrange("p (g n) -> p g n", g=32, n=65)
            for fc in range(4):
                inject(2)
                pk = pkv[fc % 2]
                if fc < 2:
                    def kvmm(e, ti=ti, c=c, fc=fc, pk=pk):
                        for rb in range(2):
                            e.matmul(out=pk[:, :], lhsT=kv1[0][rb][:, ti * 128:(ti + 1) * 128],
                                     rhs=wu_h[rb][:, fc * CH:(fc + 1) * CH], start=(rb == 0), stop=False)
                        return e.matmul(out=pk[:, :], lhsT=onesr_h[0:1, 0:128],
                                        rhs=bu_rh[0:1, fc * CH:(fc + 1) * CH], start=False, stop=True)
                else:
                    def kvmm(e, ti=ti, c=c, fc=fc, pk=pk):
                        last = None
                        for rb in range(2):
                            last = e.matmul(out=pk[:, :], lhsT=kv1[0][rb][:, ti * 128:(ti + 1) * 128],
                                            rhs=wu_h[rb][:, fc * CH:(fc + 1) * CH],
                                            start=(rb == 0), stop=(rb == 1))
                        return last
                sc.emit("T", kvmm,
                        reads=[("kv1", 0, 0), ("kv1", 0, 1),
                               ("wdst", id(wu_h), 0), ("wdst", id(wu_h), 1),
                               ("onesr_h",), ("bu_rh",)],
                        writes=[("pC", fc % 2)])
                if fc < 2:
                    dst = kvv[:, fc * 8:(fc + 1) * 8, 0:64]
                    sc.emit("S", lambda e, pk=pk, fc=fc: e.activation(
                        out=ebuf[fc % 2][:, :], in_=pk[:, :], func=AF.Exp, bias=0.0, scale=1.0),
                        reads=[("pC", fc % 2)], writes=[("ebuf", fc % 2)])
                    sc.emit("V", lambda e, fc=fc: e.tensor_scalar_min(
                        out=e1buf[fc % 2][:, :], in0=ebuf[fc % 2][:, :], scalar1=1.0),
                        reads=[("ebuf", fc % 2)], writes=[("ebuf", fc % 2)])
                    sc.emit("V", lambda e, pk=pk, dst=dst, fc=fc: e.scalar_tensor_tensor(
                        out=dst, in0=pk[:, :], scalar=0.0, in1=e1buf[fc % 2][:, :],
                        op0=ALU.max, op1=ALU.add),
                        reads=[("pC", fc % 2), ("ebuf", fc % 2), ("kvones", kb_i)],
                        writes=[("kvk", kb_i, fc)])
                else:
                    dst = kvv[:, 16 + (fc - 2) * 8:16 + (fc - 1) * 8, 0:64]
                    sc.emit("V", lambda e, pk=pk, dst=dst, fc=fc: e.tensor_add(
                        out=dst, in0=pk[:, :], in1=burep[fc - 2][:, :]),
                        reads=[("pC", fc % 2), ("burep", fc - 2), ("kvones", kb_i)],
                        writes=[("kvv", kb_i, fc - 2)])

            def ctxmm(e, i=i, kb_i=kb_i):
                last = None
                for h in range(H):
                    first_in_bank = h in (0, 7, 14)
                    last = e.matmul(
                        out=ctx_ps(h),
                        lhsT=kvsb[kb_i][:, h * 65:h * 65 + 64],
                        rhs=kvsb[kb_i][:, 1040 + h * 65:1040 + (h + 1) * 65],
                        start=(i == 0 and first_in_bank), stop=(i == NT - 1),
                        skip_group_check=True)
                return last
            sc.emit("T", ctxmm,
                    reads=[("kvk", kb_i, 0), ("kvk", kb_i, 1),
                           ("kvv", kb_i, 0), ("kvv", kb_i, 1)],
                    writes=[("kvsb_free", kb_i)] + ([("ctx_acc",)] if i == NT - 1 else []))

    w_wave2 = load_weight(wo, wo_h, accum=[rwo[:, k:k + 1] for k in range(NB)])
    w_wave2.append(TH("V", lambda e: e.tensor_copy(out=rwo_h[:, :], in_=rwo[:, :]),
                      reads=[("wdst", id(wo_h), k) for k in range(NB)], writes=[("rwo_h",)]))

    s0 = stats1_head(0, q="SY") + stats1_tail(0) + stats1_head(1, q="SY")
    bg.extend(w_wq)
    while s0 or bg:
        if s0:
            s0.pop(0)()
        inject(1)
    for t in normalize_thunks(0):
        t()
    bg.extend(stats1_tail(1))
    bg.extend(normalize_thunks(1))
    bg.extend(w_wave1)
    for c in range(NCH):
        if c + 2 < NCH:
            bg.extend(stats1_head(c + 2))
        emit_compute1(c, post_q=(w_wave2 if c == NCH - 1 else None))
        if c + 2 < NCH:
            bg.extend(stats1_tail(c + 2))
            bg.extend(normalize_thunks(c + 2))
    drain_bg()


    for h in range(H):
        sc.emit("S", lambda e, h=h: e.activation(out=ctxsb_sl(h),
                                                 in_=ctx_ps(h)[0:64, 0:65],
                                                 func=AF.Copy, bias=0.0, scale=1.0),
                reads=[("ctx_acc",)], writes=[("ctxsb", h)])
    sc.emit("V", lambda e: e.memset(zlhs[:, :], 0.0), writes=[("zlhs_z",)])
    for kb in range(NB):
        for j in range(2):
            h = 2 * kb + j
            sc.emit("V", lambda e, kb=kb, j=j, h=h: e.tensor_copy(
                out=zlhs[j * 64:j * 64 + 64, kb * 16 + h:kb * 16 + h + 1],
                in_=ctxsb_sl(h, 65)[:, 64:65]),
                reads=[("ctxsb", h), ("zlhs_z",)], writes=[("zlhs", h)])

    def a1z_thunks(c):
        th = []

        def zmm(e, c=c):
            last = None
            for kb in range(NB):
                last = e.matmul(out=pmisc[0:16, :], lhsT=zlhs[:, kb * 16:(kb + 1) * 16],
                                rhs=qp[kb][:, cc(c)], start=(kb == 0), stop=(kb == NB - 1),
                                skip_group_check=True)
            return last
        th.append(TH("T", zmm, reads=[("zlhs", h) for h in range(H)] +
                     [("qp", kb, c) for kb in range(NB)], writes=[("pzdot",)]))
        th.append(TH("V", lambda e, c=c: e.tensor_scalar_add(
            out=zdsb[:, :], in0=pmisc[0:16, :], scalar1=1e-6),
            reads=[("pzdot",)], writes=[("zdsb",)]))
        th.append(TH("S", lambda e, c=c: e.activation(out=zdsb[:, :], in_=zdsb[:, :],
                                                      func=AF.Ln, bias=0.0, scale=1.0),
                     reads=[("zdsb",)], writes=[("zdsb",)]))
        th.append(TH("V", lambda e, c=c: e.tensor_scalar_mul(out=zdsb[:, :], in0=zdsb[:, :],
                                                             scalar1=-1.0),
                     reads=[("zdsb",)], writes=[("zdsb",)]))
        th.append(TH("S", lambda e, c=c: e.activation(out=zdsb[:, :], in_=zdsb[:, :],
                                                      func=AF.Exp, bias=0.0, scale=1.0),
                     reads=[("zdsb",)], writes=[("zdsb",)]))
        th.append(TH("V", lambda e: e.tensor_scalar_mul(out=z16[:, :], in0=zdsb[:, :],
                                                        scalar1=1024.0),
                     reads=[("zdsb",)], writes=[("z16",)]))
        th.append(TH("SY", lambda e, c=c: e.dma_start(out=zscr[:, cc(c)], in_=z16[:, :]),
                     reads=[("z16",)], writes=[("zscr", c)], sem="zsc"))
        return th

    def zb_thunks(c):
        th = []
        for p in range(PAIRS):
            th.extend(zrep_bcast(c, p))
        return th

    def zrep_bcast(c, p):
        def za(e, p=p, c=c):
            return e.dma_start(out=zreps[c % 2][p][0:64, :],
                               in_=zscr[2 * p:2 * p + 1, cc(c)].partition_broadcast(64))

        def zb(e, p=p, c=c):
            return e.dma_start(out=zreps[c % 2][p][64:128, :],
                               in_=zscr[2 * p + 1:2 * p + 2, cc(c)].partition_broadcast(64))
        return [TH("SY", za, reads=[("zscr", c)],
                   writes=[("zrep", c % 2, p)], sem=f"zr{p}a{c % 2}"),
                TH("SY", zb, reads=[("zscr", c)],
                   writes=[("zrep2", c % 2, p)], sem=f"zr{p}b{c % 2}")]

    def a1o_thunks(c):
        th = []
        for p in range(PAIRS):
            pb = pmain[p % 2]

            def outmm(e, p=p, c=c, pb=pb):
                e.matmul(out=pb[0:64, :], lhsT=ctxsb_sl(2 * p, 64),
                         rhs=qp[p][0:64, cc(c)], start=True, stop=True)
                return e.matmul(out=pb[64:128, :], lhsT=ctxsb_sl(2 * p + 1, 64),
                                rhs=qp[p][64:128, cc(c)], start=True, stop=True)
            th.append(TH("T", outmm, reads=[("ctxsb", 2 * p), ("ctxsb", 2 * p + 1), ("qp", p, c)],
                         writes=[("pA", p % 2)]))
            th.append(TH("S", lambda e, p=p, pb=pb: e.activation(
                out=oev[p % 4][:, :], in_=pb[:, :], func=AF.Identity,
                bias=0.0, scale=1.0 / 1024.0),
                reads=[("pA", p % 2), ("cvals",)], writes=[("oev", p % 4)]))
            th.append(TH("GP", lambda e, p=p, c=c: e.tensor_mul(
                out=outz[c % NOZ][p][:, :], in0=oev[p % 4][:, :], in1=zreps[c % 2][p][:, :]),
                reads=[("oev", p % 4), ("zrep", c % 2, p), ("zrep2", c % 2, p)],
                writes=[("outz", c % NOZ, p)], sem="GP", inc=1))
        return th

    def a2_thunks(c):
        ensure(("wdst", id(wo_h), NB - 1))
        th = []
        for db in range(NB):
            pb = pmain[db % 2]

            def womm(e, db=db, c=c, pb=pb):
                last = None
                for kb in range(NB):
                    last = e.matmul(out=pb[:, :], lhsT=wo_h[kb][:, db * 128:(db + 1) * 128],
                                    rhs=outz[c % NOZ][kb][:, :], start=(kb == 0), stop=(kb == NB - 1))
                return last
            th.append(TH("T", womm,
                         reads=[("outz", c % NOZ, kb) for kb in range(NB)] +
                               [("wdst", id(wo_h), kb) for kb in range(NB)],
                         writes=[("pA", db % 2)]))

            def xld(e, db=db, c=c):
                return e.dma_start(out=xr[db % 2][:, :],
                                   in_=xT[db * 128:(db + 1) * 128, cc(c)])
            th.append(TH("SY", xld, writes=[("xr", db % 2)], sem=f"xr{db % 2}"))
            th.append(TH("V", lambda e, pb=pb, db=db, c=c: e.scalar_tensor_tensor(
                out=x2[c % NX2][db][:, :], in0=pb[:, :], scalar=bo_cf[:, db:db + 1],
                in1=xr[db % 2][:, :], op0=ALU.add, op1=ALU.add),
                reads=[("pA", db % 2), ("xr", db % 2), ("bo",)],
                writes=[("x2", c % NX2, db)]))
        return th

    def c1_thunks(c):
        th = []

        def sdmm(e, c=c):
            for kb in range(NB):
                e.matmul(out=pmisc[32:33, :], lhsT=rwo_h[:, kb:kb + 1],
                         rhs=outz[c % NOZ][kb][:, :], start=(kb == 0), stop=False,
                         skip_group_check=True)
            return e.matmul(out=pmisc[32:33, :], lhsT=sbo_h[:, :], rhs=onesr_h[:, :],
                            start=False, stop=True, skip_group_check=True)
        th.append(TH("T", sdmm, reads=[("outz", c % NOZ, kb) for kb in range(NB)] +
                     [("rwo_h",), ("sbo_h",), ("onesr_h",)], writes=[("psum_sx",)]))
        for db in range(NB):
            th.append(TH("S", lambda e, db=db, c=c: e.activation(
                out=xsq2[db % 2][:, :], in_=x2[c % NX2][db][:, :], func=AF.Square,
                bias=0.0, scale=1.0),
                reads=[("x2", c % NX2, db)], writes=[("xsq2", db % 2)]))
            th.append(TH("T", lambda e, db=db: e.matmul(
                out=pmisc[64:65, :], lhsT=ones_h[:, 0:1], rhs=xsq2[db % 2][:, :],
                start=(db == 0), stop=(db == NB - 1), skip_group_check=True),
                reads=[("xsq2", db % 2), ("ones_h",)], writes=[("psum_sq",)]))
        th.append(TH("SY", lambda e, c=c: e.dma_start(out=m1c[:, :], in_=scr[0:1, cc(c)]),
                     reads=[("scr1", c)], writes=[("m1c",)], sem="m1c"))
        th.append(TH("V", lambda e: e.tensor_scalar_mul(out=trow[0][:, :], in0=pmisc[32:33, :],
                                                        scalar1=1.0 / D),
                     reads=[("psum_sx",)], writes=[("trow0",)]))
        th.append(TH("V", lambda e, c=c: e.tensor_add(out=m2row[:, :], in0=m1c[:, :],
                                                      in1=trow[0][:, :]),
                     reads=[("trow0",), ("m1c",)], writes=[("m2row",)]))
        th.append(TH("V", lambda e: e.tensor_scalar_mul(out=trow[0][:, :], in0=pmisc[64:65, :],
                                                        scalar1=1.0 / D),
                     reads=[("psum_sq",)], writes=[("trow0",)]))
        th.append(TH("S", lambda e, c=c: e.activation(out=trow[1][:, :], in_=m2row[:, :],
                                                      func=AF.Square, bias=0.0, scale=1.0),
                     reads=[("m2row",)], writes=[("trow1",)]))
        th.append(TH("V", lambda e: e.scalar_tensor_tensor(out=trow[0][:, :], in0=trow[0][:, :],
                                                           scalar=1e-5, in1=trow[1][:, :],
                                                           op0=ALU.add, op1=ALU.subtract),
                     reads=[("trow0",), ("trow1",)], writes=[("trow0",)]))
        th.append(TH("S", lambda e: e.activation(out=trow[1][:, :], in_=trow[0][:, :],
                                                 func=AF.Ln, bias=0.0, scale=1.0),
                     reads=[("trow0",)], writes=[("trow1",)]))
        th.append(TH("S", lambda e, c=c: e.activation(out=i2row[:, :], in_=trow[1][:, :],
                                                      func=AF.Exp, bias=0.0, scale=-0.5),
                     reads=[("trow1",), ("cvals",)], writes=[("i2row",)]))

        def bnc(e, c=c):
            return e.dma_start(out=scr[1:2, :].rearrange("a (g t) -> a g t", g=2)[:, :, cc(c)],
                               in_=rows2[0:1, :].rearrange("a (g t) -> a g t", g=2))
        th.append(TH("SY", bnc, reads=[("m2row",), ("i2row",)], writes=[("scr2", c)], sem="sc2"))

        def rp1(e, c=c):
            return e.dma_start(out=m2rep[c % NRP][:, :],
                               in_=scr[1:2, cc(c)].partition_broadcast(128))
        th.append(TH("SY", rp1, reads=[("scr2", c)], writes=[("m1rep", c % NRP)], sem=f"rpa{c % NRP}"))

        def rp2(e, c=c):
            return e.dma_start(out=i2rep[c % NRP][:, :],
                               in_=scr[1:2, T + c * CH:T + (c + 1) * CH].partition_broadcast(128))
        th.append(TH("SY", rp2, reads=[("scr2", c)], writes=[("i1rep", c % NRP)], sem=f"rpb{c % NRP}"))
        return th

    def c2a_thunks(c):
        ensure(("wdst", id(wf8), NB - 1))
        th = []
        for db in range(NB):
            th.append(TH("V", lambda e, db=db, c=c: e.tensor_sub(
                out=xc2[db % NXC][:, :], in0=x2[c % NX2][db][:, :], in1=m2rep[c % NRP][:, :]),
                reads=[("x2", c % NX2, db), ("m1rep", c % NRP)], writes=[("xc1", db % NXC)]))
            th.append(TH("V", lambda e, db=db, c=c: e.tensor_mul(
                out=h2q[c % NH2][db][:, :], in0=xc2[db % NXC][:, :], in1=i2rep[c % NRP][:, :]),
                reads=[("xc1", db % NXC), ("i1rep", c % NRP)], writes=[("h2q", c % NH2, db)]))
            if db % 2 == 1:
                s = db // 2

                def ffn0(e, s=s, c=c):
                    last = None
                    lh = wf8d[s][:].rearrange("k (i f) -> k i f", i=2)
                    rh = h2qd[s][:].rearrange("k (i n) -> k i n", i=2)
                    for n in range(2):
                        last = e.matmul(
                            out=pkv[0][:, n * 256:(n + 1) * 256],
                            lhsT=lh[:, :, 0:128],
                            rhs=rh[:, :, n * 256:(n + 1) * 256],
                            start=(s == 0 and n == 0), stop=(s == 3),
                            perf_mode=PM, skip_group_check=True)
                    return last
                th.append(TH("T", ffn0,
                             reads=[("h2q", c % NH2, 2 * s), ("h2q", c % NH2, 2 * s + 1),
                                    ("wdst", id(wf8), 2 * s), ("wdst", id(wf8), 2 * s + 1)],
                             writes=[("pC", 0)]))
        return th

    def c2b_thunks(c):
        th = []
        for fb in range(NB):
            pb = pkv[fb % 2]
            if fb > 0:
                def ffnmm(e, fb=fb, c=c, pb=pb):
                    last = None
                    for n in range(2):
                        for s in range(4):
                            lh = wf8d[s][:].rearrange("k (i f) -> k i f", i=2)
                            rh = h2qd[s][:].rearrange("k (i n) -> k i n", i=2)
                            last = e.matmul(
                                out=pb[:, n * 256:(n + 1) * 256],
                                lhsT=lh[:, :, fb * 128:(fb + 1) * 128],
                                rhs=rh[:, :, n * 256:(n + 1) * 256],
                                start=(s == 0 and n == 0), stop=(s == 3),
                                perf_mode=PM, skip_group_check=True)
                    return last
                th.append(TH("T", ffnmm,
                             reads=[("h2q", c % NH2, kb) for kb in range(NB)] +
                                   [("wdst", id(wf8), kb) for kb in range(NB)],
                             writes=[("pC", fb % 2)]))
            th.append(TH("V", lambda e, pb=pb, fb=fb, c=c: e.scalar_tensor_tensor(
                out=x2[c % NX2][fb][:, :], in0=pb[:, :], scalar=bf_cf[:, fb:fb + 1],
                in1=x2[c % NX2][fb][:, :], op0=ALU.add, op1=ALU.add),
                reads=[("pC", fb % 2)], writes=[("x2", c % NX2, fb)]))

            def yst(e, fb=fb, c=c):
                return e.dma_start(out=yT[fb * 128:(fb + 1) * 128, cc(c)],
                                   in_=x2[c % NX2][fb][:, :])
            th.append(TH("GP", yst, reads=[("x2", c % NX2, fb)],
                         writes=[("ydma", c, fb)], sem=f"yd{fb}_{c % 2}"))
        return th

    def run_all(th):
        for t in th:
            t()

    def interleave(l1, l2):
        i = j = 0
        while i < len(l1) or j < len(l2):
            if i < len(l1):
                l1[i]()
                i += 1
            if j < len(l2):
                l2[j]()
                j += 1

    run_all(a1z_thunks(0))
    run_all(zb_thunks(0))
    interleave(a1z_thunks(1) if NCH > 1 else [], a1o_thunks(0))
    run_all(a2_thunks(0))
    run_all(c1_thunks(0))
    for c in range(NCH):
        if c + 1 < NCH:
            run_all(zb_thunks(c + 1))
        l2 = (a1z_thunks(c + 2) if c + 2 < NCH else []) + \
             (a1o_thunks(c + 1) if c + 1 < NCH else [])
        interleave(c2a_thunks(c), l2)
        if c + 1 < NCH:
            run_all(a2_thunks(c + 1))
            run_all(c1_thunks(c + 1))
        run_all(c2b_thunks(c))

    sems = {}
    with ExitStack() as ses:
        for name in sc.sem_names:
            sems[name] = ses.enter_context(nc.semaphore(f"sem{name}"))
        block = ses.enter_context(nc.Block())

        @block.tensor
        def _(eng):
            sc.replay("T", eng, sems)

        @block.vector
        def _(eng):
            sc.replay("V", eng, sems)

        @block.scalar
        def _(eng):
            sc.replay("S", eng, sems)

        @block.sync
        def _(eng):
            sc.replay("SY", eng, sems)

        @block.gpsimd
        def _(eng):
            sc.replay("GP", eng, sems)
            for fb in range(NB):
                for par in range(2):
                    eng.wait_ge(sems[f"yd{fb}_{par}"], sc.count[f"yd{fb}_{par}"])
    es.close()
    return nc


_NC_CACHE = None


def kernel(L0_attn_x, L0_attn_Wq, L0_attn_bq, L0_attn_Wd, L0_attn_bd,
           L0_attn_Wu, L0_attn_bu, L0_attn_Wo, L0_attn_bo,
           L0_ffn_weight, L0_ffn_bias):
    global _NC_CACHE
    x = np.asarray(L0_attn_x, dtype=np.float32)
    B = x.shape[0]
    f32 = lambda a: np.ascontiguousarray(np.asarray(a, dtype=np.float32))
    kidx = np.concatenate([np.arange(h * 128, h * 128 + 64) for h in range(H)])
    vidx = np.concatenate([np.arange(h * 128 + 64, h * 128 + 128) for h in range(H)])
    perm = np.concatenate([kidx, vidx])
    common = {
        "wq": f32(L0_attn_Wq),
        "wd": f32(L0_attn_Wd),
        "wu": f32(np.asarray(L0_attn_Wu, dtype=np.float32)[:, perm]),
        "wo": f32(L0_attn_Wo),
        "wf": f32(np.asarray(L0_ffn_weight, dtype=np.float32).T),
        "bq_r": f32(np.asarray(L0_attn_bq, dtype=np.float32).reshape(1, D)),
        "bd_c": f32(np.asarray(L0_attn_bd, dtype=np.float32).reshape(2, 128).T),
        "bu_r": f32(np.asarray(L0_attn_bu, dtype=np.float32)[perm].reshape(1, 2 * D)),
        "bo_c": f32(np.asarray(L0_attn_bo, dtype=np.float32).reshape(NB, 128).T),
        "bf_c": f32(np.asarray(L0_ffn_bias, dtype=np.float32).reshape(NB, 128).T),
        "ones_in": np.ones((128, 128), dtype=np.float32),
        "onesr_in": np.ones((1, CH), dtype=np.float32),
        "cvals_in": np.array([[-0.5, 1.0 / D, 1e-5, -1.0, 1.0 / 1024.0]], dtype=np.float32),
        "ident_in": np.eye(128, dtype=np.float32),
    }
    in_maps = []
    for b in range(B):
        m = dict(common)
        m["xT"] = np.ascontiguousarray(x[b].T)
        m["xtok_d"] = np.ascontiguousarray(x[b])
        in_maps.append(m)
    if _NC_CACHE is None:
        _NC_CACHE = build_nc()
    res = run_bass_kernel_spmd(_NC_CACHE, in_maps, core_ids=list(range(B)))
    out = np.stack([np.ascontiguousarray(r["yT"].T) for r in res.results], axis=0)
    return out
